# version 04
# baseline (speedup 1.0000x reference)
import numpy as np
import ml_dtypes
from contextlib import ExitStack
import concourse.bass as bass
import concourse.mybir as mybir
from concourse.bass_utils import run_bass_kernel_spmd

F32 = mybir.dt.float32
BF16 = mybir.dt.bfloat16
AF = mybir.ActivationFunctionType
ALU = mybir.AluOpType
AX = mybir.AxisListType

L, D, H, E = 2048, 1024, 16, 64
DEBUG_ATTNB = False
S2_PHASES = 99
S2_V7 = 0
S0_PHASES = 99
NT = L // 128
NCH = D // 128
MAGIC = 12582912.0

ENGINES = ('tensor', 'vector', 'scalar', 'gpsimd', 'sync')



_PROXY_SKIP = {'wait_ge', 'drain', 'engine_nop', 'nop'}


class _EngProxy:

    def __init__(self, p, eng, semname):
        self._p, self._eng, self._sem = p, eng, semname

    def __getattr__(self, name):
        val = getattr(self._eng, name)
        if not callable(val) or name.startswith('_') or name in _PROXY_SKIP:
            return val

        def wrapper(*a, **k):
            p, eng, semname = self._p, self._eng, self._sem
            if p.cnt.get(semname, 0) > 0:
                eng.wait_ge(p._sem(semname), p.cnt[semname])
            ins = val(*a, **k)
            if ins is not None and hasattr(ins, 'then_inc'):
                p.inc(ins, semname)
            return ins
        return wrapper


class P:

    def __init__(self, nc, block, es, mode='record', wlog=None):
        self.nc, self.block, self.es = nc, block, es
        self.sems, self.cnt = {}, {}
        self.marks = {}
        self.ones2 = None
        self.mode = mode
        self.wlog = [] if wlog is None else wlog
        self.nsec = -1
        self.cur = -1
        self.queue = {e: [] for e in ENGINES}
        self.wptr = {}
        for n in ('sd', 'gd', 'st', 'pe', 'gp', 've', 'vps', 'se',
                  'vself', 'sself'):
            self.cnt[n] = 0

    def _sem(self, name):
        if name not in self.sems:
            self.sems[name] = self.es.enter_context(self.nc.semaphore(name))
            self.cnt.setdefault(name, 0)
        return self.sems[name]

    def _group(self, eng, pairs_fn):
        if self.mode == 'record':
            pairs = [(n, t) for (n, t) in pairs_fn() if t > 0]
            self.wlog[self.cur].append(pairs)
        else:
            i = self.wptr.get(self.cur, 0)
            self.wptr[self.cur] = i + 1
            pairs = self.wlog[self.cur][i]
        for n, t in pairs:
            eng.wait_ge(self._sem(n), t)

    def _section(self, engname, fn):
        self.nsec += 1
        if self.mode == 'record':
            self.cur = self.nsec
            self.wlog.append([])
            getattr(self.block, engname)(fn)
        else:
            self.queue[engname].append((self.nsec, fn))

    def flush(self):
        assert self.mode == 'replay'
        for engname in ENGINES:
            items = self.queue[engname]
            if not items:
                continue

            def runner(eng, items=items):
                for sec, fn in items:
                    self.cur = sec
                    fn(eng)
                    assert self.wptr.get(sec, 0) == len(self.wlog[sec]), \
                        f"section {sec}: wait groups not fully consumed"
            getattr(self.block, engname)(runner)

    ALIAS = {'ve': 'vself', 'vps': 'vself', 'se': 'sself'}

    def inc(self, instr, name, n=1):
        if name in self.ALIAS:
            base = self.ALIAS[name]
            self.cnt[name] = self.cnt.get(name, 0) + n
            self.marks.setdefault(name, {})[self.cnt[name]] = \
                self.cnt.get(base, 0)
            return self.cnt[name]
        instr.then_inc(self._sem(name), n)
        self.cnt[name] = self.cnt.get(name, 0) + n
        return self.cnt[name]

    def wait(self, eng, name, val=None):
        def pairs():
            if name in self.ALIAS:
                base = self.ALIAS[name]
                if val is None:
                    tgt = self.cnt.get(base, 0)
                else:
                    if val <= 0:
                        return []
                    tgt = self.marks[name][val]
                return [(base, tgt)]
            tgt = self.cnt.get(name, 0) if val is None else val
            return [(name, tgt)]
        self._group(eng, pairs)

    def t(self, fn): self._section('tensor', fn)

    def v(self, fn):
        self._section('vector',
                      lambda eng: fn(_EngProxy(self, eng, 'vself')))

    def s(self, fn):
        self._section('scalar',
                      lambda eng: fn(_EngProxy(self, eng, 'sself')))

    def _dma_section(self, engname, fn, sems):
        before = {n: self.cnt.get(n, 0) for n in sems}

        def wrapped(eng):
            fn(eng)
            self._group(eng, lambda: [(n, self.cnt[n]) for n in sems
                                      if self.cnt.get(n, 0) > before[n]])
        self._section(engname, wrapped)

    def g(self, fn): self._dma_section('gpsimd', fn, ('gd',))
    def y(self, fn): self._dma_section('sync', fn, ('sd', 'st'))

    def barrier(self):
        def mk(engname):
            def f(eng):
                self._group(eng, lambda: [(n, self.cnt[n])
                                          for n in list(self.sems)
                                          if self.cnt[n] > 0])
            return f
        for e in ENGINES:
            self._section(e, mk(e))



def hilo_bias(p, ar, vecs, slot):
    ntot = sum(n for _, n in vecs)
    f32row = ar['brow'][:, 0:ntot]
    f32row2 = ar['brow'][:, 1024:1024 + ntot]
    dst = ar['bias2'][slot][:, 0:ntot]

    def ld(sync):
        p.wait(sync, 'gd')
        p.wait(sync, 've')
        p.wait(sync, 'pe')
        off = 0
        for v, n in vecs:
            src = v if v.ndim == 2 else v.unsqueeze(0)
            ins = sync.dma_start(out=f32row[:, off:off + n], in_=src)
            p.inc(ins, 'sd', 16)
            off += n
    p.y(ld)

    def mk(vector):
        p.wait(vector, 'sd')
        vector.tensor_copy(dst[0:1, :], f32row)
        vector.drain()
        vector.tensor_copy(f32row2, dst[0:1, :])
        vector.drain()
        ins = vector.tensor_tensor(f32row2, f32row, f32row2, ALU.subtract)
        vector.drain()
        p.inc(ins, 've')
    p.v(mk)

    def place_lo(gpsimd):
        p.wait(gpsimd, 've')
        ins = gpsimd.dma_start(out=dst[1:2, :], in_=f32row2)
        p.inc(ins, 'gd', 16)
    p.g(place_lo)
    return dst


def bias_mm(p, tensor, psum, bias2, c0, c1, start=False, stop=False):
    tensor.matmul(psum, p.ones2[:, 0:psum.shape[0]], bias2[:, c0:c1],
                  start=start, stop=stop)


def emit_load_x(p, x_in, xa):
    def load(sync):
        p.wait(sync, 'st')
        ins = sync.dma_start(out=xa.rearrange("p (t f) -> p t f", t=NT),
                             in_=x_in.rearrange("(t p) f -> p t f", p=128))
        p.inc(ins, 'sd', 16)
    p.y(load)


def emit_ln_stats(p, xa, st_dump, stats, eps, double_eps=None, ntiles=NT):
    sums, sq = stats[:, 0:16], stats[:, 16:32]
    sc, nb = stats[:, 32:48], stats[:, 48:64]

    def pass1(scalar):
        p.wait(scalar, 'sd')
        for t in range(ntiles):
            ins = scalar.activation(out=st_dump, in_=xa[:, 1024 * t:1024 * (t + 1)],
                                    func=AF.Identity, accum_out=sums[:, t:t + 1])
        p.inc(ins, 'se')
    p.s(pass1)

    def negm(vector):
        p.wait(vector, 'se')
        ins = vector.tensor_scalar(sums[:, 0:ntiles], sums[:, 0:ntiles],
                                   -1.0 / 1024.0, None, ALU.mult)
        vector.drain()
        p.inc(ins, 've')
    p.v(negm)

    def pass2(scalar):
        p.wait(scalar, 've')
        for t in range(ntiles):
            ins = scalar.activation(out=st_dump, in_=xa[:, 1024 * t:1024 * (t + 1)],
                                    func=AF.Square, bias=sums[:, t:t + 1], scale=1.0,
                                    accum_out=sq[:, t:t + 1])
        p.inc(ins, 'se')
    p.s(pass2)

    def var_recip(vector):
        p.wait(vector, 'se')
        vector.tensor_scalar(sc[:, 0:ntiles], sq[:, 0:ntiles], 1.0 / 1024.0, eps,
                             ALU.mult, ALU.add)
        vector.drain()
        ins = vector.reciprocal(sc[:, 0:ntiles], sc[:, 0:ntiles])
        vector.drain()
        p.inc(ins, 've')
    p.v(var_recip)

    def sqrt1(scalar):
        p.wait(scalar, 've')
        ins = scalar.activation(out=sc[:, 0:ntiles], in_=sc[:, 0:ntiles],
                                func=AF.Sqrt)
        p.inc(ins, 'se')
    p.s(sqrt1)

    if double_eps is not None:
        def comb(vector):
            p.wait(vector, 'se')
            vector.tensor_scalar(nb[:, 0:ntiles], sq[:, 0:ntiles], 1.0 / 1024.0,
                                 None, ALU.mult)
            vector.drain()
            vector.tensor_tensor(nb[:, 0:ntiles], nb[:, 0:ntiles],
                                 sc[:, 0:ntiles], ALU.mult)
            vector.drain()
            vector.tensor_tensor(nb[:, 0:ntiles], nb[:, 0:ntiles],
                                 sc[:, 0:ntiles], ALU.mult)
            vector.drain()
            vector.tensor_scalar(nb[:, 0:ntiles], nb[:, 0:ntiles], double_eps,
                                 None, ALU.add)
            vector.drain()
            ins = vector.reciprocal(nb[:, 0:ntiles], nb[:, 0:ntiles])
            vector.drain()
            p.inc(ins, 've')
        p.v(comb)

        def sqrt2(scalar):
            p.wait(scalar, 've')
            ins = scalar.activation(out=nb[:, 0:ntiles], in_=nb[:, 0:ntiles],
                                    func=AF.Sqrt)
            p.inc(ins, 'se')
        p.s(sqrt2)

        def comb2(vector):
            p.wait(vector, 'se')
            vector.tensor_tensor(stats[:, 80:80 + ntiles], sums[:, 0:ntiles],
                                 sc[:, 0:ntiles], ALU.mult)
            vector.tensor_copy(stats[:, 64:64 + ntiles], sc[:, 0:ntiles])
            vector.tensor_tensor(sc[:, 0:ntiles], sc[:, 0:ntiles],
                                 nb[:, 0:ntiles], ALU.mult)
            vector.drain()
            ins = vector.tensor_tensor(nb[:, 0:ntiles], sums[:, 0:ntiles],
                                       sc[:, 0:ntiles], ALU.mult)
            vector.drain()
            p.inc(ins, 've')
        p.v(comb2)
    else:
        def finish(vector):
            p.wait(vector, 'se')
            ins = vector.tensor_tensor(nb[:, 0:ntiles], sums[:, 0:ntiles],
                                       sc[:, 0:ntiles], ALU.mult)
            vector.drain()
            p.inc(ins, 've')
        p.v(finish)


def emit_normalize(p, xa, stats, dst_bf, dst_f32=None, ntiles=NT):
    sc, nb = stats[:, 32:48], stats[:, 48:64]

    def norm(scalar):
        p.wait(scalar, 've')
        for t in range(ntiles):
            if dst_bf is not None:
                ins = scalar.activation(out=dst_bf[:, 1024 * t:1024 * (t + 1)],
                                        in_=xa[:, 1024 * t:1024 * (t + 1)],
                                        func=AF.Identity, bias=nb[:, t:t + 1],
                                        scale=sc[:, t:t + 1])
            if dst_f32 is not None:
                ins = scalar.activation(out=dst_f32[:, 1024 * t:1024 * (t + 1)],
                                        in_=xa[:, 1024 * t:1024 * (t + 1)],
                                        func=AF.Identity, bias=nb[:, t:t + 1],
                                        scale=sc[:, t:t + 1])
        p.inc(ins, 'se')
    p.s(norm)


def emit_tm_to_fm(p, src_bf_tm, fm_dst, scr, wait_sem, ntiles=NT, chunks=NCH):
    def to_dram(sync):
        p.wait(sync, wait_sem)
        ins = sync.dma_start(out=scr.rearrange("(t p) f -> p t f", p=128),
                             in_=src_bf_tm.rearrange("p (t f) -> p t f", t=ntiles))
        p.inc(ins, 'sd', 16)
    p.y(to_dram)

    def from_dram(sync):
        p.wait(sync, 'sd')
        for c in range(chunks):
            ins = sync.dma_start_transpose(
                out=fm_dst[:, ntiles * 128 * c:ntiles * 128 * (c + 1)],
                in_=scr[:, 128 * c:128 * (c + 1)])
            p.inc(ins, 'sd', 16)
    p.y(from_dram)


def emit_quant_tile(p, vector, src_f32, dst_bf, q16, q32):
    bm, scl, rs = q16[:, 0:32], q16[:, 32:64], q16[:, 64:96]
    aq = q32[:, 0:1024]
    t1 = q32[:, 1024:2048]
    t2 = q32[:, 2048:3072]
    t3 = q32[:, 3072:4096]
    seg = lambda ap: ap.rearrange("p (n c) -> p n c", c=32)
    vector.tensor_reduce(bm, seg(src_f32), AX.X, ALU.max, apply_absolute_value=True)
    vector.drain()
    vector.tensor_scalar(scl, bm, 1.0 / 7.5, 1e-30, ALU.mult, ALU.max)
    vector.drain()
    vector.reciprocal(rs, scl)
    vector.drain()
    vector.tensor_tensor(seg(aq), seg(src_f32),
                         rs.unsqueeze(2).broadcast_to([128, 32, 32]), ALU.mult)
    vector.tensor_tensor(t3, aq, aq, ALU.mult)
    vector.tensor_scalar(t1, t3, 4.0, None, ALU.is_ge)
    vector.tensor_scalar(t2, t3, 16.0, None, ALU.is_ge)
    vector.tensor_scalar(t3, t1, 0.125, 0.125, ALU.mult, ALU.add)
    vector.tensor_scalar(t2, t2, 1.0, None, ALU.add)
    vector.tensor_tensor(t2, t2, t3, ALU.mult)
    vector.tensor_tensor(t3, aq, aq, ALU.mult)
    vector.tensor_scalar(t3, t3, 16.0, None, ALU.is_ge)
    vector.tensor_scalar(t1, t1, -4.0, 8.0, ALU.mult, ALU.add)
    vector.tensor_scalar(t3, t3, -0.5, 1.0, ALU.mult, ALU.add)
    vector.tensor_tensor(t1, t1, t3, ALU.mult)
    vector.tensor_tensor(t1, aq, t1, ALU.mult)
    vector.tensor_scalar(t3, t1, MAGIC, MAGIC, ALU.add, ALU.subtract)
    vector.tensor_tensor(t1, t1, t3, ALU.subtract)
    vector.tensor_scalar(t1, t1, -0.5, None, ALU.is_equal)
    vector.tensor_tensor(t1, t3, t1, ALU.subtract)
    vector.tensor_tensor(t1, t1, t2, ALU.mult)
    ins = vector.tensor_tensor(seg(dst_bf), seg(t1),
                               scl.unsqueeze(2).broadcast_to([128, 32, 32]),
                               ALU.mult)
    return ins


def emit_store_x(p, xsrc, x_out, wait='vps'):
    def store(sync):
        p.wait(sync, wait)
        ins = sync.dma_start(out=x_out.rearrange("(t p) f -> p t f", p=128),
                             in_=xsrc.rearrange("p (t f) -> p t f", t=NT))
        p.inc(ins, 'st', 16)
    p.y(store)



def emit_l1_ffn(p, ar, x_in, x_out, w, first):
    xa, xb = ar['XA'], ar['XB']
    xc = ar['XCb']
    wdT = ar['WSb'][:, 0:8192]
    stats = ar['stats']
    q16, q32 = ar['q16'], ar['q32']

    emit_load_x(p, x_in, xa)
    b2 = hilo_bias(p, ar, [(w['L1_ffn_bias'], 1024)], 0)

    if first:
        def loadw(sync):
            ins = sync.dma_start(out=xb[:, 0:8192].rearrange("p (o f) -> p o f", o=8),
                                 in_=w['L1_ffn_weight'].rearrange(
                                     "(o p) f -> p o f", p=128))
            p.inc(ins, 'sd', 16)
        p.y(loadw)

        def quantw(vector):
            p.wait(vector, 'sd')
            p.wait(vector, 'gd')
            for o in range(8):
                ins = emit_quant_tile(p, vector, xb[:, 1024 * o:1024 * (o + 1)],
                                      xc[:, 1024 * o:1024 * (o + 1)], q16, q32)
            vector.drain()
            p.inc(ins, 've')
        p.v(quantw)

        def wtod(sync):
            p.wait(sync, 've')
            ins = sync.dma_start(
                out=ar['scr_w'].rearrange("(t p) f -> p t f", p=128),
                in_=xc[:, 0:8192].rearrange("p (t f) -> p t f", t=8))
            p.inc(ins, 'sd', 16)
        p.y(wtod)

        def wfrd(sync):
            p.wait(sync, 'sd')
            for c in range(8):
                ins = sync.dma_start_transpose(
                    out=wdT[:, 1024 * c:1024 * (c + 1)],
                    in_=ar['scr_w'][:, 128 * c:128 * (c + 1)])
                p.inc(ins, 'sd', 16)
        p.y(wfrd)

    emit_ln_stats(p, xa, ar['dump'], stats, 1e-5)
    emit_normalize(p, xa, stats, None, dst_f32=xb)

    def quantx(vector):
        p.wait(vector, 'se')
        p.wait(vector, 'gd')
        for t in range(NT):
            ins = emit_quant_tile(p, vector, xb[:, 1024 * t:1024 * (t + 1)],
                                  xc[:, 1024 * t:1024 * (t + 1)], q16, q32)
        vector.drain()
        p.inc(ins, 've')
    p.v(quantx)

    emit_tm_to_fm(p, xc[:, 0:16384], xc, ar['scr_tm'], wait_sem='ve')

    sd_ready = p.cnt['sd']
    ve_ready = p.cnt['ve']
    vps_base = p.cnt.get('vps', 0)

    gd_ready = p.cnt.get('gd', 0)

    def gate(tensor):
        p.wait(tensor, 'sd', sd_ready)
        p.wait(tensor, 've', ve_ready)
        p.wait(tensor, 'gd', gd_ready)
        p.wait(tensor, 'gp')
    p.t(gate)

    for t in range(NT):
        pb0 = ar['psf'][(t % 2) * 2 + 0]
        pb1 = ar['psf'][(t % 2) * 2 + 1]

        def mm_t(tensor, t=t, pb0=pb0, pb1=pb1):
            if t >= 2:
                p.wait(tensor, 'vps', vps_base + t - 1)
            for half, pb in ((0, pb0), (1, pb1)):
                bias_mm(p, tensor, pb, b2, 512 * half, 512 * (half + 1),
                        start=True, stop=False)
                for k in range(8):
                    ins = tensor.matmul(
                        pb,
                        xc[:, 2048 * k + 128 * t:2048 * k + 128 * (t + 1)],
                        wdT[:, 1024 * k + 512 * half:1024 * k + 512 * (half + 1)],
                        start=False, stop=(k == 7))
            p.inc(ins, 'pe')
        p.t(mm_t)

        def evac_t(vector, t=t, pb0=pb0, pb1=pb1):
            p.wait(vector, 'pe')
            vector.scalar_tensor_tensor(
                out=xb[:, 1024 * t:1024 * t + 512], in0=pb0, scalar=1.0,
                in1=xa[:, 1024 * t:1024 * t + 512], op0=ALU.mult, op1=ALU.add)
            ins = vector.scalar_tensor_tensor(
                out=xb[:, 1024 * t + 512:1024 * (t + 1)], in0=pb1, scalar=1.0,
                in1=xa[:, 1024 * t + 512:1024 * (t + 1)], op0=ALU.mult,
                op1=ALU.add)
            p.inc(ins, 'vps')
        p.v(evac_t)

    emit_store_x(p, xb, x_out)



def emit_l1_attn_b(p, ar, x_in, x_out, w):
    xa, xb = ar['XA'], ar['XB']
    xc = ar['XCb']
    stats = ar['stats']
    q32 = ar['q32']
    w1b = ar['WSb'][:, 0:8192]
    w23b = ar['w23']

    emit_load_x(p, x_in, xa)
    b1 = hilo_bias(p, ar, [(w['L1_attn_b_b1'], 1024)], 0)
    b23 = hilo_bias(p, ar, [(w['L1_attn_b_b2'], 16), (w['L1_attn_b_b3'], 16)], 1)

    def loadw1(sync):
        p.wait(sync, 'pe')
        ins = sync.dma_start(out=w1b.rearrange("p (k f) -> p k f", k=8),
                             in_=ar['cache']['L1_attn_b_W1'].rearrange(
                                 "(k p) f -> p k f", p=128))
        p.inc(ins, 'sd', 16)
    p.y(loadw1)

    def loadw23(gpsimd):
        ins = gpsimd.dma_start(
            out=w23b[:].rearrange("p (k a f) -> p k a f", k=8, a=2)[:, :, 0, :],
            in_=w['L1_attn_b_W2'].rearrange("(k p) f -> p k f", p=128))
        p.inc(ins, 'gd', 16)
        ins = gpsimd.dma_start(
            out=w23b[:].rearrange("p (k a f) -> p k a f", k=8, a=2)[:, :, 1, :],
            in_=w['L1_attn_b_W3'].rearrange("(k p) f -> p k f", p=128))
        p.inc(ins, 'gd', 16)
    p.g(loadw23)

    emit_ln_stats(p, xa, ar['dump'], stats, 1e-5)
    emit_normalize(p, xa, stats, xc[:, 0:16384], dst_f32=xb)
    emit_tm_to_fm(p, xc[:, 0:16384], xc, ar['scr_tm'], wait_sem='se')

    sd_ready = p.cnt['sd']
    gd_ready = p.cnt['gd']
    ve_ready = p.cnt['ve']
    vps_base = p.cnt.get('vps', 0)

    def gate(tensor):
        p.wait(tensor, 'sd', sd_ready)
        p.wait(tensor, 'gd', gd_ready)
        p.wait(tensor, 've', ve_ready)
        p.wait(tensor, 'gp')
    p.t(gate)

    for t in range(NT):
        pb0 = ar['psf'][(t % 2) * 2 + 0]
        pb1 = ar['psf'][(t % 2) * 2 + 1]
        pbc = ar['psb'][t % 2][:, 0:32]

        def mm_t(tensor, t=t, pb0=pb0, pb1=pb1, pbc=pbc):
            if t >= 2:
                p.wait(tensor, 'vps', vps_base + t - 1)
            for half, pb in ((0, pb0), (1, pb1)):
                bias_mm(p, tensor, pb, b1, 512 * half, 512 * (half + 1),
                        start=True, stop=False)
                for k in range(8):
                    tensor.matmul(
                        pb,
                        xc[:, 2048 * k + 128 * t:2048 * k + 128 * (t + 1)],
                        w1b[:, 1024 * k + 512 * half:1024 * k + 512 * (half + 1)],
                        start=False, stop=(k == 7))
            bias_mm(p, tensor, pbc, b23, 0, 32, start=True, stop=False)
            for k in range(8):
                ins = tensor.matmul(
                    pbc,
                    xc[:, 2048 * k + 128 * t:2048 * k + 128 * (t + 1)],
                    w23b[:, 32 * k:32 * (k + 1)],
                    start=False, stop=(k == 7))
            p.inc(ins, 'pe')
        p.t(mm_t)

        d0 = q32[:, 0:1024]
        d1 = q32[:, 1024:2048]
        d2 = q32[:, 2048:3072]
        rd = ar['q16'][:, 0:16]

        def evac_v(vector, t=t, pb0=pb0, pb1=pb1, pbc=pbc, d0=d0, d1=d1, rd=rd):
            p.wait(vector, 'pe')
            vector.tensor_copy(d0[:, 0:512], pb0)
            vector.tensor_copy(d0[:, 512:1024], pb1)
            vector.tensor_copy(d1[:, 0:32], pbc)
            vector.drain()
            ins = vector.scalar_tensor_tensor(
                out=d1[:, 32:48], in0=d1[:, 0:16], scalar=1.0,
                in1=d1[:, 16:32], op0=ALU.mult, op1=ALU.mult,
                accum_out=rd[:, 0:1])
            vector.drain()
            p.inc(ins, 've')
        p.v(evac_v)

        def absz(vector, d0=d0, d1=d1):
            vector.tensor_scalar(d1, d0, -1.0, None, ALU.mult)
            ins = vector.tensor_tensor(d1, d0, d1, ALU.max)
            p.inc(ins, 've')
        p.v(absz)

        def sp_a(scalar, d1=d1, d2=d2):
            p.wait(scalar, 've')
            scalar.activation(out=d2, in_=d1, func=AF.Exp, scale=-1.0)
            ins = scalar.activation(out=d2, in_=d2, func=AF.Ln, bias=1.0)
            p.inc(ins, 'se')
        p.s(sp_a)

        def fin_v(vector, t=t, d0=d0, d1=d1, d2=d2, rd=rd):
            p.wait(vector, 'se')
            if DEBUG_ATTNB and t == 0:
                vector.tensor_copy(ar['dump'], d0)
                vector.tensor_copy(stats[:, 0:1], rd[:, 0:1])
            vector.tensor_scalar(d0, d0, 0.0, None, ALU.max)
            vector.tensor_tensor(d0, d0, d2, ALU.add)
            vector.scalar_tensor_tensor(
                out=d0, in0=d0, scalar=rd[:, 0:1],
                in1=xb[:, 1024 * t:1024 * (t + 1)], op0=ALU.mult, op1=ALU.mult)
            ins = vector.tensor_tensor(
                xb[:, 1024 * t:1024 * (t + 1)], d0,
                xa[:, 1024 * t:1024 * (t + 1)], ALU.add)
            p.inc(ins, 'vps')
        p.v(fin_v)

    emit_store_x(p, xb, x_out)
    if DEBUG_ATTNB:
        def dbgst(sync):
            p.wait(sync, 'vps')
            ins = sync.dma_start(out=ar['dbg_out'][:, 0:1024], in_=ar['dump'])
            p.inc(ins, 'st', 16)
            ins = sync.dma_start(out=ar['dbg_out'][:, 1024:1056],
                                 in_=stats[:, 0:32])
            p.inc(ins, 'st', 16)
            ins = sync.dma_start(out=ar['dbg2_out'][:, 0:1024],
                                 in_=xb[:, 0:1024])
            p.inc(ins, 'st', 16)
            ins = sync.dma_start(out=ar['dbg3_out'], in_=xc[:, 0:2048])
            p.inc(ins, 'st', 16)
        p.y(dbgst)





FFN_SPECS = [
    ('L0_ffn_w1_w', 'L0_ffn_w1_s', 4096, 1024, 'cT_w1'),
    ('L0_ffn_w2_w', 'L0_ffn_w2_s', 4096, 1024, 'cT_w2'),
    ('L0_ffn_h1_w', 'L0_ffn_h1_s', 4096, 4096, 'cT_h1'),
    ('L0_ffn_h2_w', 'L0_ffn_h2_s', 4096, 4096, 'cT_h2'),
    ('L0_ffn_w3_w', 'L0_ffn_w3_s', 1024, 4096, 'cT_w3'),
]


def emit_scale_load(p, ar, dram_vec, n, dst_cols):
    nch = n // 128
    tmp = ar['s32']
    dst = ar['ssb'][:, dst_cols:dst_cols + nch]

    def ld(gpsimd):
        p.wait(gpsimd, 'pe')
        ins = gpsimd.dma_start(out=tmp[0:nch, :],
                               in_=dram_vec.rearrange("(c q) -> c q", q=128))
        p.inc(ins, 'gd', 16)
    p.g(ld)



    def tr(tensor):
        p.wait(tensor, 'gd')
        p.wait(tensor, 've')
        ins = tensor.transpose(ar['pst'][:, 0:nch], tmp[0:nch, :],
                               ar['ident32'][0:nch, 0:nch])
        p.inc(ins, 'pe')
    p.t(tr)

    def ev(vector):
        p.wait(vector, 'pe')
        ins = vector.tensor_copy(dst, ar['pst'][:, 0:nch])
        p.inc(ins, 've')
    p.v(ev)


def emit_ffn_cache_build(p, ar, w):
    wsb = ar['WSb']
    for wi, (wname, sname, O, K, cname) in enumerate(FFN_SPECS):
        cT = ar[cname]
        craw = ar['c_raw']
        is_w3 = (wname == 'L0_ffn_w3_w')
        for oc in range(O // 128):
            buf = wsb[:, (oc % 2) * 4096:(oc % 2) * 4096 + K]

            def cast(gpsimd, oc=oc, buf=buf, wname=wname, K=K):
                p.wait(gpsimd, 'sd')
                ins = gpsimd.dma_start(
                    out=buf, in_=w[wname][128 * oc:128 * (oc + 1), 0:K])
                p.inc(ins, 'gd', 16)
            p.g(cast)

            if is_w3:
                def scale(vector, oc=oc, buf=buf):
                    p.wait(vector, 'gd')
                    ins = vector.tensor_scalar(
                        buf, buf, ar['ssb'][:, 128 + oc:129 + oc], None, ALU.mult)
                    p.inc(ins, 've')
                p.v(scale)

            def wr(sync, oc=oc, buf=buf, craw=craw, K=K, is_w3=is_w3):
                p.wait(sync, 'gd')
                if is_w3:
                    p.wait(sync, 've')
                ins = sync.dma_start(out=craw[128 * oc:128 * (oc + 1), 0:K],
                                     in_=buf)
                p.inc(ins, 'sd', 16)
            p.y(wr)
        for kc in range(K // 128):
            buf = wsb[:, (kc % 2) * 4096:(kc % 2) * 4096 + O]

            def tr(sync, kc=kc, buf=buf, craw=craw, O=O):
                p.wait(sync, 'sd')
                ins = sync.dma_start_transpose(
                    out=buf, in_=craw[0:O, 128 * kc:128 * (kc + 1)])
                p.inc(ins, 'sd', 16)
                return ins
            p.y(tr)

            def wr2(sync, kc=kc, buf=buf, cT=cT, O=O):
                p.wait(sync, 'sd')
                ins = sync.dma_start(out=cT[128 * kc:128 * (kc + 1), 0:O],
                                     in_=buf)
                p.inc(ins, 'sd', 16)
            p.y(wr2)


def emit_l0_ffn(p, ar, x_in, x_out, w, first):
    xa, xb = ar['XA'], ar['XB']
    xab, xbb = ar['XAb'], ar['XBb']
    xc = ar['XCb']
    wsb = ar['WSb']
    stats = ar['stats']
    q32 = ar['q32']
    ssb = ar['ssb']

    emit_scale_load(p, ar, w['L0_ffn_w1_s'], 4096, 0)
    emit_scale_load(p, ar, w['L0_ffn_w2_s'], 4096, 32)
    emit_scale_load(p, ar, w['L0_ffn_h1_s'], 4096, 64)
    emit_scale_load(p, ar, w['L0_ffn_h2_s'], 4096, 96)
    emit_scale_load(p, ar, w['L0_ffn_w3_s'], 1024, 128)

    if first:
        emit_ffn_cache_build(p, ar, w)

    emit_load_x(p, x_in, xa)
    emit_ln_stats(p, xa, ar['dump'], stats, 1e-5)
    emit_normalize(p, xa, stats, xc[:, 0:16384])
    emit_tm_to_fm(p, xc[:, 0:16384], xc, ar['scr_tm'], wait_sem='se')

    ve_ready = p.cnt['ve']
    vps_base = p.cnt.get('vps', 0)

    for tb in range(2):
        t0 = 1024 * tb
        for og in range(8):
            o0 = 512 * og

            def lda(sync, o0=o0):
                p.wait(sync, 'pe')
                p.wait(sync, 'sd')
                p.wait(sync, 'vps')
                p.wait(sync, 'st')
                ins = sync.dma_start(
                    out=wsb[:, 0:4096].rearrange("p (k o) -> p k o", k=8),
                    in_=ar['cT_w1'].rearrange("(k p) o -> p k o", p=128)[
                        :, :, o0:o0 + 512])
                p.inc(ins, 'sd', 16)
                ins = sync.dma_start(
                    out=wsb[:, 4096:8192].rearrange("p (k o) -> p k o", k=8),
                    in_=ar['cT_w2'].rearrange("(k p) o -> p k o", p=128)[
                        :, :, o0:o0 + 512])
                p.inc(ins, 'sd', 16)
            p.y(lda)

            for sub in range(8):
                oc, g = divmod(sub, 2)
                oabs = 4 * og + oc
                par = sub % 2
                ps1, ps2 = ar['psf'][par * 2], ar['psf'][par * 2 + 1]

                def mma(tensor, oc=oc, g=g, ps1=ps1, ps2=ps2, sub=sub, t0=t0):
                    p.wait(tensor, 'sd')
                    v0 = p.cnt.get('vps', 0)
                    p.wait(tensor, 'vps',
                           max(vps_base, v0 - 1) if v0 > vps_base else 0)
                    for wsl, ps in ((0, ps1), (4096, ps2)):
                        for k in range(8):
                            ins = tensor.matmul(
                                ps,
                                wsb[:, wsl + 512 * k + 128 * oc:
                                    wsl + 512 * k + 128 * (oc + 1)],
                                xc[:, 2048 * k + t0 + 512 * g:
                                    2048 * k + t0 + 512 * (g + 1)],
                                start=(k == 0), stop=(k == 7))
                    p.inc(ins, 'pe')
                p.t(mma)

                def eva_s(scalar, ps1=ps1, oabs=oabs):
                    p.wait(scalar, 'pe')
                    p.wait(scalar, 'vps')
                    ins = scalar.activation(out=ar['u512'], in_=ps1,
                                            func=AF.Silu,
                                            scale=ssb[:, oabs:oabs + 1])
                    p.inc(ins, 'se')
                p.s(eva_s)

                def eva_v(vector, ps2=ps2, oabs=oabs, g=g):
                    p.wait(vector, 'se')
                    ins = vector.scalar_tensor_tensor(
                        out=xbb[:, 1024 * oabs + 512 * g:
                                1024 * oabs + 512 * (g + 1)],
                        in0=ps2, scalar=ssb[:, 32 + oabs:33 + oabs],
                        in1=ar['u512'], op0=ALU.mult, op1=ALU.mult)
                    p.inc(ins, 'vps')
                p.v(eva_v)

        for stage, (hname, scol, src, dst) in enumerate(
                [('cT_h1', 64, xbb, xab), ('cT_h2', 96, xab, xbb)]):
            for og in range(16):
                o0 = 256 * og

                def ldb(sync, o0=o0, hname=hname):
                    p.wait(sync, 'pe')
                    p.wait(sync, 'vps')
                    p.wait(sync, 'sd')
                    ins = sync.dma_start(
                        out=wsb[:, 0:8192].rearrange("p (k o) -> p k o", k=32),
                        in_=ar[hname].rearrange("(k p) o -> p k o", p=128)[
                            :, :, o0:o0 + 256])
                    p.inc(ins, 'sd', 16)
                p.y(ldb)

                for sub in range(4):
                    oc, g = divmod(sub, 2)
                    oabs = 2 * og + oc
                    ps = ar['psf'][sub]

                    def mmb(tensor, oc=oc, g=g, ps=ps, src=src):
                        p.wait(tensor, 'sd')
                        p.wait(tensor, 'se')
                        for k in range(32):
                            ins = tensor.matmul(
                                ps,
                                wsb[:, 256 * k + 128 * oc:256 * k + 128 * (oc + 1)],
                                src[:, 1024 * k + 512 * g:
                                    1024 * k + 512 * (g + 1)],
                                start=(k == 0), stop=(k == 31))
                        p.inc(ins, 'pe')
                    p.t(mmb)

                    def evb(scalar, ps=ps, oabs=oabs, g=g, dst=dst, scol=scol):
                        p.wait(scalar, 'pe')
                        ins = scalar.activation(
                            out=dst[:, 1024 * oabs + 512 * g:
                                    1024 * oabs + 512 * (g + 1)],
                            in_=ps, func=AF.Gelu,
                            scale=ssb[:, scol + oabs:scol + oabs + 1])
                        p.inc(ins, 'se')
                    p.s(evb)

        for th in range(2):
            tt0 = t0 + 512 * th

            def ldx(sync, tt0=tt0):
                p.wait(sync, 'vps')
                p.wait(sync, 've')
                p.wait(sync, 'st')
                p.wait(sync, 'gd')
                ins = sync.dma_start(
                    out=q32.rearrange("p (t f) -> p t f", t=4),
                    in_=x_in[tt0:tt0 + 512, :].rearrange(
                        "(t p) f -> p t f", p=128))
                p.inc(ins, 'sd', 16)
            p.y(ldx)

            for kg in range(4):
                def ldw3(sync, kg=kg):
                    p.wait(sync, 'pe')
                    p.wait(sync, 'sd')
                    if kg == 0:
                        p.wait(sync, 'se')
                    ins = sync.dma_start(
                        out=wsb[:, 0:8192].rearrange("p (k o) -> p k o", k=8),
                        in_=ar['cT_w3'].rearrange("(k p) o -> p k o", p=128)[
                            :, 8 * kg:8 * kg + 8, :])
                    p.inc(ins, 'sd', 16)
                p.y(ldw3)

                def mmd(tensor, kg=kg, tt0=tt0, t0=t0):
                    p.wait(tensor, 'sd')
                    if kg == 0:
                        p.wait(tensor, 'vps')
                    for tt in range(4):
                        for half in range(2):
                            ps = ar['psf'][tt] if half == 0 else ar['psb'][tt]
                            for k8 in range(8):
                                k = 8 * kg + k8
                                ins = tensor.matmul(
                                    ps,
                                    xbb[:, 1024 * k + (tt0 - t0) + 128 * tt:
                                        1024 * k + (tt0 - t0) + 128 * (tt + 1)],
                                    wsb[:, 1024 * k8 + 512 * half:
                                        1024 * k8 + 512 * (half + 1)],
                                    start=(kg == 0 and k8 == 0),
                                    stop=(kg == 3 and k8 == 7))
                    p.inc(ins, 'pe')
                p.t(mmd)

            def evd(vector, tt0=tt0):
                p.wait(vector, 'pe')
                for tt in range(4):
                    for half in range(2):
                        ps = ar['psf'][tt] if half == 0 else ar['psb'][tt]
                        ins = vector.scalar_tensor_tensor(
                            out=q32[:, 1024 * tt + 512 * half:
                                    1024 * tt + 512 * (half + 1)],
                            in0=ps, scalar=1.0,
                            in1=q32[:, 1024 * tt + 512 * half:
                                    1024 * tt + 512 * (half + 1)],
                            op0=ALU.mult, op1=ALU.add)
                p.inc(ins, 'vps')
            p.v(evd)

            def std(sync, tt0=tt0):
                p.wait(sync, 'vps')
                ins = sync.dma_start(
                    out=x_out[tt0:tt0 + 512, :].rearrange(
                        "(t p) f -> p t f", p=128),
                    in_=q32.rearrange("p (t f) -> p t f", t=4))
                p.inc(ins, 'st', 16)
            p.y(std)





def emit_l0_attn_b(p, ar, x_in, x_out, w):
    xa, xb = ar['XA'], ar['XB']
    xab, xbb = ar['XAb'], ar['XBb']
    xc = ar['XCb']
    wsb = ar['WSb']
    stats = ar['stats']
    q32 = ar['q32']
    q32b = ar['q32'].bitcast(BF16)
    ssb = ar['ssb']
    u512 = ar['u512']

    emit_scale_load(p, ar, w['L0_attn_b_gamma1'], 1024, 0)
    emit_scale_load(p, ar, w['L0_attn_b_beta1'], 1024, 8)
    emit_scale_load(p, ar, w['L0_attn_b_b1'], 4096, 16)
    b2 = hilo_bias(p, ar, [(w['L0_attn_b_b2'], 1024)], 0)

    emit_load_x(p, x_in, xa)
    emit_ln_stats(p, xa, ar['dump'], stats, 1e-5, double_eps=1e-6)
    emit_normalize(p, xa, stats, xc[:, 0:16384])
    emit_tm_to_fm(p, xc[:, 0:16384], xc, ar['scr_tm'], wait_sem='se')

    def affine(vector):
        p.wait(vector, 'sd')
        for c in range(NCH):
            ins = vector.tensor_scalar(
                xc[:, 2048 * c:2048 * (c + 1)], xc[:, 2048 * c:2048 * (c + 1)],
                ssb[:, c:c + 1], ssb[:, 8 + c:9 + c], ALU.mult, ALU.add)
        p.inc(ins, 've')
    p.v(affine)


    for wi, wname in enumerate(['L0_attn_b_wk', 'L0_attn_b_wv']):
        def ldw(sync, wname=wname):
            p.wait(sync, 'pe')
            ins = sync.dma_start(
                out=wsb[:, 0:8192].rearrange("p (k o) -> p k o", k=8),
                in_=ar['cache'][wname].rearrange("(k p) o -> p k o", p=128))
            p.inc(ins, 'sd', 16)
        p.y(ldw)

        for t in range(NT):
            ps0, ps1 = ar['psf'][(t % 2) * 2], ar['psf'][(t % 2) * 2 + 1]

            def mmp(tensor, t=t, ps0=ps0, ps1=ps1, wi=wi):
                p.wait(tensor, 'gd')
                p.wait(tensor, 'sd')
                p.wait(tensor, 've')
                if not (wi == 0 and t < 2):
                    p.wait(tensor, 'vps', p.cnt['vps'] - 1)
                for half, ps in ((0, ps0), (1, ps1)):
                    for k in range(8):
                        ins = tensor.matmul(
                            ps, xc[:, 2048 * k + 128 * t:2048 * k + 128 * (t + 1)],
                            wsb[:, 1024 * k + 512 * half:1024 * k + 512 * (half + 1)],
                            start=(k == 0), stop=(k == 7))
                p.inc(ins, 'pe')
            p.t(mmp)

            if wi == 0:
                def evk(vector, t=t, ps0=ps0, ps1=ps1):
                    p.wait(vector, 'pe')
                    vector.tensor_scalar(xbb[:, 1024 * t:1024 * t + 512],
                                         ps0, 0.0, 0.001, ALU.max, ALU.add)
                    ins = vector.tensor_scalar(
                        xbb[:, 1024 * t + 512:1024 * (t + 1)],
                        ps1, 0.0, 0.001, ALU.max, ALU.add)
                    p.inc(ins, 'vps')
                p.v(evk)
            else:
                def evv(vector, t=t, ps0=ps0, ps1=ps1):
                    p.wait(vector, 'pe')
                    vector.tensor_copy(xbb[:, 16384 + 1024 * t:
                                           16384 + 1024 * t + 512], ps0)
                    ins = vector.tensor_copy(
                        xbb[:, 16384 + 1024 * t + 512:16384 + 1024 * (t + 1)],
                        ps1)
                    p.inc(ins, 'vps')
                p.v(evv)

    if S2_PHASES == 2:
        emit_store_x(p, xb, x_out, wait='vps')
        return
    kvs = q32b[0:64, 0:1040]
    klist = []
    for h in range(H):
        psk = ar['psb'][1 + (h % 2)][0:64, 0:65]

        def mmk(tensor, h=h, psk=psk):
            if h < 2:
                p.wait(tensor, 'vps')
                p.wait(tensor, 'gp')
            if h >= 2:
                p.wait(tensor, 've', klist[h - 2])
            for t in range(NT):
                tensor.matmul(
                    psk[:, 0:64],
                    xbb[:, 1024 * t + 64 * h:1024 * t + 64 * (h + 1)],
                    xbb[:, 16384 + 1024 * t + 64 * h:16384 + 1024 * t + 64 * (h + 1)],
                    start=(t == 0), stop=False, skip_group_check=True)
                ins = tensor.matmul(
                    psk[:, 64:65],
                    xbb[:, 1024 * t + 64 * h:1024 * t + 64 * (h + 1)],
                    p.ones128,
                    start=False, stop=(t == NT - 1), skip_group_check=True)
            p.inc(ins, 'pe')
        p.t(mmk)

        def evkv(vector, h=h, psk=psk):
            p.wait(vector, 'pe')
            ins = vector.tensor_copy(kvs[:, 65 * h:65 * (h + 1)], psk)
            p.inc(ins, 've')
        p.v(evkv)
        klist.append(p.cnt['ve'])

    if S2_PHASES == 25:
        def gate25(vector):
            p.wait(vector, 've')
            ins = vector.tensor_copy(xb[0:64, 0:520], kvs[0:64, 0:520])
            p.inc(ins, 'vps')
        p.v(gate25)
        emit_store_x(p, xb, x_out, wait='vps')
        return
    def ldwq(sync):
        p.wait(sync, 'pe')
        p.wait(sync, 'vps')
        ins = sync.dma_start(
            out=wsb[:, 0:8192].rearrange("p (k o) -> p k o", k=8),
            in_=ar['cache']['L0_attn_b_wq'].rearrange("(k p) o -> p k o", p=128))
        p.inc(ins, 'sd', 16)
    p.y(ldwq)

    qpf = q32b[0:64, 1664:3712]
    for t in range(NT):
        ps0, ps1 = ar['psf'][0], ar['psf'][1]
        pstA = ar['psf'][2][:, 0:512].bitcast(BF16)
        pstB = ar['psb'][0][:, 0:512].bitcast(BF16)

        def mmq(tensor, t=t, ps0=ps0, ps1=ps1):
            p.wait(tensor, 'gd')
            p.wait(tensor, 'sd')
            p.wait(tensor, 've')
            for half, ps in ((0, ps0), (1, ps1)):
                for k in range(8):
                    ins = tensor.matmul(
                        ps, xc[:, 2048 * k + 128 * t:2048 * k + 128 * (t + 1)],
                        wsb[:, 1024 * k + 512 * half:1024 * k + 512 * (half + 1)],
                        start=(k == 0), stop=(k == 7))
            p.inc(ins, 'pe')
        p.t(mmq)

        def evq(vector, t=t, ps0=ps0, ps1=ps1):
            p.wait(vector, 'pe')
            vector.tensor_scalar(xbb[:, 1024 * t:1024 * t + 512],
                                 ps0, 0.0, 0.001, ALU.max, ALU.add)
            ins = vector.tensor_scalar(
                xbb[:, 1024 * t + 512:1024 * (t + 1)],
                ps1, 0.0, 0.001, ALU.max, ALU.add)
            p.inc(ins, 've')
        p.v(evq)

        if S2_PHASES == 3.1:
            continue

        def trq(tensor, t=t, pstA=pstA, pstB=pstB):
            p.wait(tensor, 've')
            for h in range(H):
                pst = pstA if h < 8 else pstB
                ins = tensor.transpose(
                    pst[0:64, 128 * (h % 8):128 * (h % 8 + 1)],
                    xbb[:, 1024 * t + 64 * h:1024 * t + 64 * (h + 1)],
                    ar['ident'])
            p.inc(ins, 'pe')
        p.t(trq)

        def evt(vector, t=t, pstA=pstA, pstB=pstB):
            p.wait(vector, 'pe')
            vector.tensor_copy(qpf[:, 0:1024], pstA[0:64, :])
            ins = vector.tensor_copy(qpf[:, 1024:2048], pstB[0:64, :])
            p.inc(ins, 've')
        p.v(evt)

        if S2_PHASES == 3.2:
            continue
        for hg in range(4):
            psn = ar['psb'][3][:, 0:264] if hg % 2 == 0 else ar['psf'][3][:, 0:264]

            def mmn(tensor, t=t, hg=hg, psn=psn):
                p.wait(tensor, 've')
                for hh in range(4):
                    h = 4 * hg + hh
                    ins = tensor.matmul(
                        psn[:, 66 * hh:66 * hh + 65],
                        qpf[:, 128 * h:128 * (h + 1)],
                        kvs[:, 65 * h:65 * (h + 1)],
                        start=(hh == 0), stop=(hh == 3), skip_group_check=True)
                p.inc(ins, 'pe')
            p.t(mmn)

            def evn(vector, t=t, hg=hg, psn=psn):
                p.wait(vector, 'pe')
                if S2_PHASES == 3.3:
                    ins = vector.tensor_copy(u512[:, 0:64], u512[:, 64:128])
                    p.inc(ins, 've')
                    return
                if S2_PHASES == 3.4:
                    for hh in range(4):
                        h = 4 * hg + hh
                        ins = vector.tensor_scalar(
                            xbb[:, 1024 * t + 64 * h:1024 * t + 64 * (h + 1)],
                            psn[:, 66 * hh:66 * hh + 64],
                            1.0, None, ALU.mult)
                    p.inc(ins, 've')
                    return
                for hh in range(4):
                    h = 4 * hg + hh
                    vector.reciprocal(u512[:, h:h + 1],
                                      psn[:, 66 * hh + 64:66 * hh + 65])
                vector.drain()
                for hh in range(4):
                    h = 4 * hg + hh
                    ins = vector.tensor_scalar(
                        xbb[:, 1024 * t + 64 * h:1024 * t + 64 * (h + 1)],
                        psn[:, 66 * hh:66 * hh + 64],
                        u512[:, h:h + 1], None, ALU.mult)
                p.inc(ins, 've')
            p.v(evn)

    if S2_PHASES in (3, 3.1, 3.2, 3.3, 3.4):
        def gate3(vector):
            p.wait(vector, 've')
            ins = vector.tensor_copy(xb[:, 0:1024], xab[:, 0:1024])
            p.inc(ins, 'vps')
        p.v(gate3)
        emit_store_x(p, xb, x_out, wait='vps')
        return

    def ldwo(sync):
        p.wait(sync, 'pe')
        p.wait(sync, 've')
        ins = sync.dma_start(
            out=wsb[:, 0:8192].rearrange("p (k o) -> p k o", k=8),
            in_=ar['cache']['L0_attn_b_wo'].rearrange("(k p) o -> p k o", p=128))
        p.inc(ins, 'sd', 16)
    p.y(ldwo)

    attf = q32b[:, 1664:2688]
    for t in range(NT):
        pst = ar['psf'][2][:, 0:512].bitcast(BF16)

        def tra(tensor, t=t, pst=pst):
            p.wait(tensor, 've')
            p.wait(tensor, 'gd')
            for pr in range(8):
                ins = tensor.transpose(
                    pst[:, 128 * pr:128 * (pr + 1)],
                    xbb[:, 1024 * t + 128 * pr:1024 * t + 128 * (pr + 1)],
                    ar['ident'])
            p.inc(ins, 'pe')
        p.t(tra)

        def eva(vector, t=t, pst=pst):
            p.wait(vector, 'pe')
            ins = vector.tensor_copy(attf, pst)
            p.inc(ins, 've')
        p.v(eva)

        ps0, ps1 = ar['psf'][0], ar['psf'][1]

        def mmo(tensor, t=t, ps0=ps0, ps1=ps1):
            p.wait(tensor, 've')
            p.wait(tensor, 'sd')
            if t >= 1:
                p.wait(tensor, 'vps', p.cnt['vps'] - 0)
            for half, ps in ((0, ps0), (1, ps1)):
                for k in range(8):
                    ins = tensor.matmul(
                        ps, attf[:, 128 * k:128 * (k + 1)],
                        wsb[:, 1024 * k + 512 * half:1024 * k + 512 * (half + 1)],
                        start=(k == 0), stop=(k == 7))
            p.inc(ins, 'pe')
        p.t(mmo)

        def evo(vector, t=t, ps0=ps0, ps1=ps1):
            p.wait(vector, 'pe')
            vector.tensor_scalar(xa[:, 1024 * t:1024 * t + 512],
                                 xa[:, 1024 * t:1024 * t + 512],
                                 stats[:, 64 + t:65 + t], stats[:, 80 + t:81 + t],
                                 ALU.mult, ALU.add)
            vector.tensor_scalar(xa[:, 1024 * t + 512:1024 * (t + 1)],
                                 xa[:, 1024 * t + 512:1024 * (t + 1)],
                                 stats[:, 64 + t:65 + t], stats[:, 80 + t:81 + t],
                                 ALU.mult, ALU.add)
            vector.tensor_tensor(xa[:, 1024 * t:1024 * t + 512],
                                 xa[:, 1024 * t:1024 * t + 512], ps0, ALU.add)
            ins = vector.tensor_tensor(
                xa[:, 1024 * t + 512:1024 * (t + 1)],
                xa[:, 1024 * t + 512:1024 * (t + 1)], ps1, ALU.add)
            p.inc(ins, 'vps')
        p.v(evo)

    if S2_PHASES <= 4.5:
        emit_store_x(p, xb, x_out, wait='vps')
        return
    def gate5(scalar):
        p.wait(scalar, 'vps')
    p.s(gate5)
    emit_ln_stats(p, xa, ar['dump'], stats, 1e-6)
    emit_normalize(p, xa, stats, xc[:, 0:16384])
    emit_tm_to_fm(p, xc[:, 0:16384], xc, ar['scr_tm'], wait_sem='se')

    def affine2(vector):
        p.wait(vector, 'sd')
        for c in range(NCH):
            ins = vector.tensor_scalar(
                xc[:, 2048 * c:2048 * (c + 1)], xc[:, 2048 * c:2048 * (c + 1)],
                ssb[:, c:c + 1], ssb[:, 8 + c:9 + c], ALU.mult, ALU.add)
        p.inc(ins, 've')
    p.v(affine2)

    if S2_PHASES <= 5.5:
        emit_store_x(p, xb, x_out, wait='vps')
        return
    for tb in range(2):
        t0 = 1024 * tb
        for og in range(8):
            o0 = 512 * og

            def ldw1(sync, o0=o0):
                p.wait(sync, 'pe')
                p.wait(sync, 've')
                p.wait(sync, 'se')
                p.wait(sync, 'vps')
                ins = sync.dma_start(
                    out=wsb[:, 0:4096].rearrange("p (k o) -> p k o", k=8),
                    in_=ar['cache']['L0_attn_b_w1'].rearrange(
                        "(k p) o -> p k o", p=128)[:, :, o0:o0 + 512])
                p.inc(ins, 'sd', 16)
            p.y(ldw1)

            for sub in range(8):
                oc, g = divmod(sub, 2)
                oabs = 4 * og + oc
                ps = ar['psf'][sub % 4]

                def mm1(tensor, oc=oc, g=g, ps=ps, t0=t0):
                    p.wait(tensor, 'gd')
                    p.wait(tensor, 'sd')
                    p.wait(tensor, 'se')
                    for k in range(8):
                        ins = tensor.matmul(
                            ps,
                            wsb[:, 512 * k + 128 * oc:512 * k + 128 * (oc + 1)],
                            xc[:, 2048 * k + t0 + 512 * g:
                                2048 * k + t0 + 512 * (g + 1)],
                            start=(k == 0), stop=(k == 7))
                    p.inc(ins, 'pe')
                p.t(mm1)

                def ev1(scalar, ps=ps, oabs=oabs, g=g):
                    p.wait(scalar, 'pe')
                    ins = scalar.activation(
                        out=xbb[:, 1024 * oabs + 512 * g:
                                1024 * oabs + 512 * (g + 1)],
                        in_=ps, func=AF.Gelu,
                        scale=1.0, bias=ssb[:, 16 + oabs:17 + oabs])
                    p.inc(ins, 'se')
                p.s(ev1)

        if S2_PHASES <= 6:
            continue
        for th in range(2):
            tt0 = t0 + 512 * th

            def ldx2(sync, tt0=tt0):
                if S2_V7 == 15:
                    p.wait(sync, 'sd')
                if S2_V7 not in (10, 15):
                    p.wait(sync, 'vps')
                if S2_V7 not in (10, 11, 15):
                    p.wait(sync, 've')
                if S2_V7 not in (10, 11, 12, 15):
                    p.wait(sync, 'st')
                if S2_V7 == 9:
                    return
                dst = ar['XB'][:, 0:4096] if S2_V7 == 7 else q32
                ins = sync.dma_start(
                    out=dst.rearrange("p (t f) -> p t f", t=4),
                    in_=x_in[tt0:tt0 + 512, :].rearrange(
                        "(t p) f -> p t f", p=128))
                p.inc(ins, 'sd', 16)
            p.y(ldx2)

            if S2_V7 >= 6:
                def st2b(sync, tt0=tt0):
                    p.wait(sync, 'sd')
                    src = q32
                    if S2_V7 in (8, 10, 11, 12, 15):
                        pass
                    else:
                        ins = sync.dma_start(
                            out=x_out[tt0:tt0 + 512, :].rearrange(
                                "(t p) f -> p t f", p=128),
                            in_=src.rearrange("p (t f) -> p t f", t=4))
                        p.inc(ins, 'st', 16)
                p.y(st2b)
                continue
            for kg in range(4):
                def ldw2(sync, kg=kg):
                    p.wait(sync, 'pe')
                    p.wait(sync, 'sd')
                    if kg == 0:
                        p.wait(sync, 'se')
                    ins = sync.dma_start(
                        out=wsb[:, 0:8192].rearrange("p (k o) -> p k o", k=8),
                        in_=ar['cache']['L0_attn_b_w2'].rearrange(
                            "(k p) o -> p k o", p=128)[:, 8 * kg:8 * kg + 8, :])
                    p.inc(ins, 'sd', 16)
                p.y(ldw2)

                def mm2(tensor, kg=kg, tt0=tt0, t0=t0):
                    p.wait(tensor, 'gd')
                    p.wait(tensor, 'sd')
                    if kg == 0:
                        p.wait(tensor, 'vps')
                        p.wait(tensor, 'se')
                    if S2_V7 == 5:
                        ps = ar['psf'][0]
                        ins = tensor.matmul(
                            ps, xab[:, 0:128], wsb[:, 0:512],
                            start=True, stop=(kg == 3))
                        p.inc(ins, 'pe')
                        return
                    for tt in range(4):
                        for half in range(2):
                            ps = ar['psf'][tt] if half == 0 else ar['psb'][tt]
                            for k8 in range(8):
                                k = 8 * kg + k8
                                ins = tensor.matmul(
                                    ps,
                                    xbb[:, 1024 * k + (tt0 - t0) + 128 * tt:
                                        1024 * k + (tt0 - t0) + 128 * (tt + 1)],
                                    wsb[:, 1024 * k8 + 512 * half:
                                        1024 * k8 + 512 * (half + 1)],
                                    start=(kg == 0 and k8 == 0),
                                    stop=(S2_V7 == 1 and kg == 3 and k8 == 7))
                    p.inc(ins, 'pe')
                p.t(mm2)

            if S2_V7 != 1:
                def mmb2(tensor, tt0=tt0):
                    for tt in range(4):
                        for half in range(2):
                            ps = ar['psf'][tt] if half == 0 else ar['psb'][tt]
                            ins = tensor.matmul(
                                ps, p.ones2[:, 0:128],
                                b2[:, 512 * half:512 * (half + 1)],
                                start=False, stop=True)
                    p.inc(ins, 'pe')
                p.t(mmb2)

            for rnd in range(4):
                def evg(scalar, rnd=rnd):
                    p.wait(scalar, 'pe')
                    if rnd >= 1:
                        p.wait(scalar, 'vps')
                    for j in range(2):
                        q = 2 * rnd + j
                        tt, half = divmod(q, 2)
                        ps = ar['psf'][tt] if half == 0 else ar['psb'][tt]
                        if S2_V7 == 2:
                            ins = scalar.copy(
                                ar['g512'][:, 512 * j:512 * (j + 1)], ps)
                        else:
                            ins = scalar.activation(
                                out=ar['g512'][:, 512 * j:512 * (j + 1)],
                                in_=ps, func=AF.Gelu)
                    p.inc(ins, 'se')
                p.s(evg)

                def evs(vector, rnd=rnd, tt0=tt0):
                    p.wait(vector, 'se')
                    for j in range(2):
                        q = 2 * rnd + j
                        col = 512 * q
                        if S2_V7 == 3:
                            ins2 = vector.tensor_copy(
                                q32[:, col:col + 512],
                                ar['g512'][:, 512 * j:512 * (j + 1)])
                            continue
                        vector.tensor_tensor(
                            q32[:, col:col + 512], q32[:, col:col + 512],
                            ar['g512'][:, 512 * j:512 * (j + 1)], ALU.add)
                        xbase = (tt0 // 128) * 1024
                        ins2 = vector.tensor_tensor(
                            q32[:, col:col + 512], q32[:, col:col + 512],
                            xa[:, xbase + col:xbase + col + 512], ALU.add)
                    p.inc(ins2, 'vps')
                p.v(evs)

            def st2(sync, tt0=tt0):
                p.wait(sync, 'vps')
                ins = sync.dma_start(
                    out=x_out[tt0:tt0 + 512, :].rearrange(
                        "(t p) f -> p t f", p=128),
                    in_=q32.rearrange("p (t f) -> p t f", t=4))
                p.inc(ins, 'st', 16)
            p.y(st2)






def emit_l1_attn_a(p, ar, x_in, x_out, w):
    xa = ar['XA']
    xab, xbb = ar['XAb'], ar['XBb']
    xcb = ar['XCb']
    wsb = ar['WSb']
    stats = ar['stats']
    q32 = ar['q32']
    q32b = ar['q32'].bitcast(BF16)
    ssb = ar['ssb']
    u512 = ar['u512']
    NEG = -10000.0

    emit_scale_load(p, ar, w['L1_attn_a_Wqkv_b'][0:1024], 1024, 0)
    emit_scale_load(p, ar, w['L1_attn_a_Wqkv_b'][1024:2048], 1024, 8)
    bv = hilo_bias(p, ar, [(w['L1_attn_a_Wqkv_b'][2048:3072], 1024)], 0)
    bo = hilo_bias(p, ar, [(w['L1_attn_a_out_b'], 1024)], 1)

    def scale_kbias(vector):
        p.wait(vector, 've')
        ins = vector.tensor_scalar(ssb[:, 8:16], ssb[:, 8:16], 0.125, None,
                                   ALU.mult)
        p.inc(ins, 've')
    p.v(scale_kbias)

    wsrc = w['L1_attn_a_Wqkv_w']
    for oc in range(24):
        buf = wsb[:, (oc % 2) * 4096:(oc % 2) * 4096 + 1024]

        def cast(gpsimd, oc=oc, buf=buf):
            p.wait(gpsimd, 'sd')
            ins = gpsimd.dma_start(out=buf,
                                   in_=wsrc[128 * oc:128 * (oc + 1), :])
            p.inc(ins, 'gd', 16)
        p.g(cast)

        def wr(sync, oc=oc, buf=buf):
            p.wait(sync, 'gd')
            ins = sync.dma_start(
                out=ar['c_raw'][128 * oc:128 * (oc + 1), 0:1024], in_=buf)
            p.inc(ins, 'sd', 16)
        p.y(wr)
    for kc in range(8):
        buf = wsb[:, (kc % 2) * 4096:(kc % 2) * 4096 + 3072]

        def tr(sync, kc=kc, buf=buf):
            p.wait(sync, 'sd')
            ins = sync.dma_start_transpose(
                out=buf, in_=ar['c_raw'][0:3072, 128 * kc:128 * (kc + 1)])
            p.inc(ins, 'sd', 16)
        p.y(tr)

        def wr2(sync, kc=kc, buf=buf):
            p.wait(sync, 'sd')
            ins = sync.dma_start(out=ar['cT_w1'][128 * kc:128 * (kc + 1), 0:3072],
                                 in_=buf)
            p.inc(ins, 'sd', 16)
        p.y(wr2)
    wosrc = w['L1_attn_a_out_w']
    for oc in range(8):
        buf = wsb[:, (oc % 2) * 4096:(oc % 2) * 4096 + 1024]

        def cast2(gpsimd, oc=oc, buf=buf):
            p.wait(gpsimd, 'sd')
            ins = gpsimd.dma_start(out=buf,
                                   in_=wosrc[128 * oc:128 * (oc + 1), :])
            p.inc(ins, 'gd', 16)
        p.g(cast2)

        def wr3(sync, oc=oc, buf=buf):
            p.wait(sync, 'gd')
            ins = sync.dma_start(
                out=ar['scr_w'][128 * oc:128 * (oc + 1), 0:1024], in_=buf)
            p.inc(ins, 'sd', 16)
        p.y(wr3)
    for kc in range(8):
        buf = wsb[:, (kc % 2) * 4096:(kc % 2) * 4096 + 1024]

        def tr2(sync, kc=kc, buf=buf):
            p.wait(sync, 'sd')
            ins = sync.dma_start_transpose(
                out=buf, in_=ar['scr_w'][0:1024, 128 * kc:128 * (kc + 1)])
            p.inc(ins, 'sd', 16)
        p.y(tr2)

        def wr4(sync, kc=kc, buf=buf):
            p.wait(sync, 'sd')
            ins = sync.dma_start(out=ar['cT_w2'][128 * kc:128 * (kc + 1), 0:1024],
                                 in_=buf)
            p.inc(ins, 'sd', 16)
        p.y(wr4)

    emit_load_x(p, x_in, xa)
    emit_ln_stats(p, xa, ar['dump'], stats, 1e-5)
    emit_normalize(p, xa, stats, xbb[:, 16384:32768])
    emit_tm_to_fm(p, xbb[:, 16384:32768], xbb[:, 16384:32768], ar['scr_tm'],
                  wait_sem='se')
    xn = xbb[:, 16384:32768]

    def ldwv(sync):
        p.wait(sync, 'pe')
        ins = sync.dma_start(
            out=wsb[:, 0:8192].rearrange("p (k o) -> p k o", k=8),
            in_=ar['cT_w1'].rearrange("(k p) o -> p k o", p=128)[:, :, 2048:3072])
        p.inc(ins, 'sd', 16)
    p.y(ldwv)

    def ones_v(gpsimd):
        p.wait(gpsimd, 'se')
        p.wait(gpsimd, 've')
        ins = gpsimd.memset(xab[:, 0:16640], 1.0)
        p.inc(ins, 'gp')
    p.g(ones_v)

    for t in range(NT):
        ps0, ps1 = ar['psf'][(t % 2) * 2], ar['psf'][(t % 2) * 2 + 1]

        def mmv(tensor, t=t, ps0=ps0, ps1=ps1):
            p.wait(tensor, 'sd')
            p.wait(tensor, 'gp')
            if t >= 2:
                p.wait(tensor, 'vps', p.cnt['vps'] - 1)
            for half, ps in ((0, ps0), (1, ps1)):
                bias_mm(p, tensor, ps, bv, 512 * half, 512 * (half + 1),
                        start=True, stop=False)
                for k in range(8):
                    ins = tensor.matmul(
                        ps, xn[:, 2048 * k + 128 * t:2048 * k + 128 * (t + 1)],
                        wsb[:, 1024 * k + 512 * half:1024 * k + 512 * (half + 1)],
                        start=False, stop=(k == 7))
            p.inc(ins, 'pe')
        p.t(mmv)

        def evv(vector, t=t, ps0=ps0, ps1=ps1):
            p.wait(vector, 'pe')
            for half, ps in ((0, ps0), (1, ps1)):
                ins = vector.tensor_copy(
                    _vslots(xab, t, half),
                    ps.rearrange("p (h e) -> p h e", h=8))
            p.inc(ins, 'vps')
        p.v(evv)

    for which, (cols, dst, biascol, scl) in enumerate(
            [(slice(0, 1024), xbb[:, 0:16384], 0, 1.0),
             (slice(1024, 2048), xcb[:, 0:16384], 8, 0.125)]):
        def ldwq(sync, cols=cols):
            p.wait(sync, 'pe')
            ins = sync.dma_start(
                out=wsb[:, 0:8192].rearrange("p (k o) -> p k o", k=8),
                in_=ar['cT_w1'].rearrange("(k p) o -> p k o", p=128)[:, :, cols])
            p.inc(ins, 'sd', 16)
        p.y(ldwq)

        for sub in range(32):
            oc, g = divmod(sub, 4)
            ps = ar['psf'][sub % 2]

            def mmq(tensor, oc=oc, g=g, ps=ps, sub=sub, which=which):
                p.wait(tensor, 'sd')
                if not (which == 0 and sub < 2):
                    p.wait(tensor, 'se', p.cnt['se'] - 1)
                if which == 0 and sub < 2:
                    p.wait(tensor, 'vps')
                for k in range(8):
                    ins = tensor.matmul(
                        ps, wsb[:, 1024 * k + 128 * oc:1024 * k + 128 * (oc + 1)],
                        xn[:, 2048 * k + 512 * g:2048 * k + 512 * (g + 1)],
                        start=(k == 0), stop=(k == 7))
                p.inc(ins, 'pe')
            p.t(mmq)

            def evq(scalar, oc=oc, g=g, ps=ps, dst=dst, biascol=biascol, scl=scl):
                p.wait(scalar, 'pe')
                ins = scalar.activation(
                    out=dst[:, 2048 * oc + 512 * g:2048 * oc + 512 * (g + 1)],
                    in_=ps, func=AF.Identity, scale=scl,
                    bias=ssb[:, biascol + oc:biascol + oc + 1])
                p.inc(ins, 'se')
            p.s(evq)

    sq = q32b[:, 0:2048]
    mq2 = q32[0:1, 3872:3888]
    mk2 = q32[0:1, 3888:3904]
    mv2 = q32[0:1, 3904:3920]
    m1r = q32[0:1, 3920:3936]
    m2r = q32[0:1, 3936:3952]

    def normqk(which, src, dstrow):
        for c in range(NCH):
            def sqr(scalar, c=c, src=src):
                p.wait(scalar, 've')
                if c == 0:
                    p.wait(scalar, 'vps')
                ins = scalar.activation(out=sq, in_=src[:, 2048 * c:2048 * (c + 1)],
                                        func=AF.Square)
                p.inc(ins, 'se')
            p.s(sqr)
            for hh in range(2):
                h = 2 * c + hh
                for g in range(4):
                    def mmn(tensor, hh=hh, g=g):
                        p.wait(tensor, 'se')
                        p.wait(tensor, 've')
                        ins = tensor.matmul(
                            ar['psb'][3][0:1, 0:512],
                            ar['ones128'][64 * hh:64 * hh + 64, 0:1],
                            sq[64 * hh:64 * hh + 64, 512 * g:512 * (g + 1)],
                            start=True, stop=True)
                        p.inc(ins, 'pe')
                    p.t(mmn)

                    def redm(vector, h=h, g=g, dstrow=dstrow):
                        p.wait(vector, 'pe')
                        vector.tensor_reduce(u512[0:1, 0:1],
                                             ar['psb'][3][0:1, 0:512],
                                             AX.X, ALU.max)
                        vector.drain()
                        if g == 0:
                            ins = vector.tensor_copy(dstrow[:, h:h + 1],
                                                     u512[0:1, 0:1])
                        else:
                            ins = vector.tensor_tensor(
                                dstrow[:, h:h + 1], dstrow[:, h:h + 1],
                                u512[0:1, 0:1], ALU.max)
                        vector.drain()
                        p.inc(ins, 've')
                    p.v(redm)

    normqk(0, xbb[:, 0:16384], mq2)
    normqk(1, xcb[:, 0:16384], mk2)

    vn = q32[:, 3584:3840]

    def vnorm(vector):
        p.wait(vector, 'vps')
        for t in range(NT):
            for h in range(H):
                sl = _vslot(xab, t, h)
                vector.scalar_tensor_tensor(
                    out=u512[:, 0:64], in0=sl, scalar=1.0, in1=sl,
                    op0=ALU.mult, op1=ALU.mult,
                    accum_out=vn[:, 16 * h + t:16 * h + t + 1])
        ins = vector.tensor_reduce(
            q32[:, 3840:3856],
            vn.rearrange("p (h t) -> p h t", h=16), AX.X, ALU.max)
        vector.drain()
        p.inc(ins, 've')
    p.v(vnorm)

    def vtr(tensor):
        p.wait(tensor, 've')
        ins = tensor.transpose(ar['psb'][3][0:16, 0:128], q32[:, 3840:3856],
                               ar['ident32'])
        p.inc(ins, 'pe')
    p.t(vtr)

    def vred(vector):
        p.wait(vector, 'pe')
        ins = vector.tensor_reduce(q32[0:16, 3856:3857],
                                   ar['psb'][3][0:16, 0:128], AX.X, ALU.max)
        vector.drain()
        p.inc(ins, 've')
    p.v(vred)

    def vtr2(tensor):
        p.wait(tensor, 've')
        ins = tensor.transpose(ar['psb'][3][0:1, 0:16], q32[0:16, 3856:3857],
                               ar['ident32'][0:16, 0:16])
        p.inc(ins, 'pe')
    p.t(vtr2)

    def vred2(vector):
        p.wait(vector, 'pe')
        ins = vector.tensor_copy(mv2[:, 0:16], ar['psb'][3][0:1, 0:16])
        vector.drain()
        p.inc(ins, 've')
    p.v(vred2)

    nm1b = q32b[0:1, 7936:7952]
    nm2b = q32b[0:1, 7952:7968]

    def mkm(scalar):
        p.wait(scalar, 've')
        scalar.activation(out=mq2, in_=mq2, func=AF.Sqrt)
        scalar.activation(out=mk2, in_=mk2, func=AF.Sqrt)
        ins = scalar.activation(out=mv2, in_=mv2, func=AF.Sqrt)
        p.inc(ins, 'se')
    p.s(mkm)

    def mkm2(vector):
        p.wait(vector, 'se')
        vector.tensor_tensor(m1r, mq2, mk2, ALU.mult)
        vector.tensor_tensor(m2r, mv2, mk2, ALU.mult)
        vector.drain()
        vector.tensor_scalar(m1r, m1r, -1.0, None, ALU.mult)
        vector.tensor_scalar(m2r, m2r, -1.0, None, ALU.mult)
        vector.drain()
        vector.tensor_copy(nm1b, m1r)
        ins = vector.tensor_copy(nm2b, m2r)
        vector.drain()
        p.inc(ins, 've')
    p.v(mkm2)

    reps = [(xbb[:, 0:16384], xbb[:, 16384:32768], nm1b),
            (xbb[:, 16384:32768], xbb[:, 0:16384], nm2b),
            (xbb[:, 0:16384], xbb[:, 16384:32768], nm2b)]
    mbf = q32[:, 3968:3969]
    dscb = q32b[0:1, 2048:2560]
    rbsb = q32[0:64, 1536:2048]
    pstage = [q32b[:, 512 * i:512 * (i + 1)] for i in range(2)]
    last_rd = [0, 0]
    vemk_ctx = [0]

    for rep, (qsrc, cdst, nmb) in enumerate(reps):
        for h in range(H):
            hb = 64 * (h % 2)
            qh = qsrc[hb:hb + 64 if hb == 0 else 128,
                      2048 * (h // 2):2048 * (h // 2 + 1)] if False else \
                qsrc[hb:hb + 64, 2048 * (h // 2):2048 * (h // 2 + 1)]
            kh = xcb[hb:hb + 64, 2048 * (h // 2):2048 * (h // 2 + 1)]

            def bcm(tensor, h=h, nmb=nmb):
                p.wait(tensor, 've')
                ins = tensor.matmul(ar['psb'][2][:, 0:1],
                                    p.ones2[0:1, 0:128], nmb[:, h:h + 1],
                                    start=True, stop=True)
                p.inc(ins, 'pe')
            p.t(bcm)

            def bcme(vector):
                p.wait(vector, 'pe')
                ins = vector.tensor_copy(mbf, ar['psb'][2][:, 0:1])
                vector.drain()
                p.inc(ins, 've')
            p.v(bcme)
            vemk_h = p.cnt['ve']

            for qb in range(4):
                nst = 4 * (qb + 1)
                pscs = (ar['psf'][2], ar['psf'][3])
                pctx = ar['psb'][3][0:65, 0:512]

                def emit_av(st0, semv, h=h, qb=qb, nst=nst, pctx=pctx):
                    diag_off = max(0, 128 * (st0 - 4 * qb))
                    vctx = vemk_ctx[0] if st0 == 0 else 0

                    def mmav(tensor, st0=st0, semv=semv, slot=st0 % 2,
                             diag_off=diag_off, pctx=pctx, h=h, nst=nst,
                             vctx=vctx):
                        p.wait(tensor, 'se', semv)
                        if st0 == 0:
                            p.wait(tensor, 've', vctx)
                        ins = tensor.matmul(
                            pctx[:, diag_off:512],
                            _vslot65(xab, st0, h),
                            pstage[slot][:, diag_off:512],
                            start=(st0 == 0), stop=(st0 == nst - 1),
                            skip_group_check=True)
                        p.inc(ins, 'pe')
                    p.t(mmav)

                pend_av = []
                for st in range(nst):
                    bank = st % 2

                    def mms(tensor, st=st, qb=qb, qh=qh, kh=kh,
                            psc=pscs[bank], rd=last_rd[bank]):
                        if st == 0:
                            p.wait(tensor, 've')
                            p.wait(tensor, 'se')
                        p.wait(tensor, 'se', rd)
                        ins = tensor.matmul(
                            psc, kh[:, 128 * st:128 * (st + 1)],
                            qh[:, 512 * qb:512 * (qb + 1)],
                            start=True, stop=True)
                        p.inc(ins, 'pe')
                    p.t(mms)
                    pemk = p.cnt['pe']

                    if st >= 4 * qb:
                        def trimask(vector, st=st, qb=qb, psc=pscs[bank],
                                    pv=pemk):
                            p.wait(vector, 'pe', pv)
                            co = 128 * (st - 4 * qb)
                            ins = vector.tensor_tensor(
                                psc[:, co:co + 128], psc[:, co:co + 128],
                                ar['ctri'], ALU.add)
                            p.inc(ins, 've')
                        p.v(trimask)
                        vemk = p.cnt['ve']
                    else:
                        vemk = vemk_h

                    def expp(scalar, psc=pscs[bank], slot=st % 2, pv=pemk,
                             vv=vemk):
                        p.wait(scalar, 'pe', pv)
                        p.wait(scalar, 've', vv)
                        ins = scalar.activation(out=pstage[slot], in_=psc,
                                                func=AF.Exp, bias=mbf[:, 0:1])
                        p.inc(ins, 'se')
                    p.s(expp)
                    last_rd[bank] = p.cnt['se']
                    pend_av.append((st, p.cnt['se']))
                    if st >= 1:
                        emit_av(*pend_av.pop(0))
                while pend_av:
                    emit_av(*pend_av.pop(0))

                def evctx(vector, h=h, qb=qb, pctx=pctx, cdst=cdst):
                    p.wait(vector, 'pe')
                    vector.reciprocal(u512[0:1, 0:512], pctx[64:65, :])
                    vector.drain()
                    ins = vector.tensor_copy(dscb, u512[0:1, 0:512])
                    vector.drain()
                    p.inc(ins, 've')
                p.v(evctx)

                def bcd(tensor):
                    p.wait(tensor, 've')
                    ins = tensor.matmul(ar['psb'][2][0:64, 0:512],
                                        p.ones2[0:1, 0:64], dscb,
                                        start=True, stop=True)
                    p.inc(ins, 'pe')
                p.t(bcd)

                def evctx2(vector, h=h, qb=qb, pctx=pctx, cdst=cdst):
                    p.wait(vector, 'pe')
                    vector.tensor_copy(rbsb, ar['psb'][2][0:64, 0:512])
                    vector.drain()
                    hb = 64 * (h % 2)
                    ins = vector.scalar_tensor_tensor(
                        out=cdst[hb:hb + 64,
                                 2048 * (h // 2) + 512 * qb:
                                 2048 * (h // 2) + 512 * (qb + 1)],
                        in0=pctx[0:64, :], scalar=1.0, in1=rbsb,
                        op0=ALU.mult, op1=ALU.mult)
                    p.inc(ins, 've')
                p.v(evctx2)
                vemk_ctx[0] = p.cnt['ve']

    ctx = xbb[:, 16384:32768]

    def ldwo(sync):
        p.wait(sync, 'pe')
        ins = sync.dma_start(
            out=wsb[:, 0:8192].rearrange("p (k o) -> p k o", k=8),
            in_=ar['cT_w2'].rearrange("(k p) o -> p k o", p=128)[0:128, 0:8, 0:1024])
        p.inc(ins, 'sd', 16)
    p.y(ldwo)

    for t in range(NT):
        ps0, ps1 = ar['psf'][(t % 2) * 2], ar['psf'][(t % 2) * 2 + 1]
        xsl = q32[:, 1024 * (t % 4):1024 * (t % 4 + 1)]

        def ldxt(sync, t=t, xsl=xsl):
            p.wait(sync, 'st')
            p.wait(sync, 've')
            p.wait(sync, 'gd')
            ins = sync.dma_start(
                out=xsl, in_=x_in[128 * t:128 * (t + 1), :])
            p.inc(ins, 'sd', 16)
        p.y(ldxt)

        def mmo(tensor, t=t, ps0=ps0, ps1=ps1):
            p.wait(tensor, 'sd')
            p.wait(tensor, 've')
            if t >= 2:
                p.wait(tensor, 'vps', p.cnt['vps'] - 1)
            for half, ps in ((0, ps0), (1, ps1)):
                bias_mm(p, tensor, ps, bo, 512 * half, 512 * (half + 1),
                        start=True, stop=False)
                for k in range(8):
                    ins = tensor.matmul(
                        ps, ctx[:, 2048 * k + 128 * t:2048 * k + 128 * (t + 1)],
                        wsb[:, 1024 * k + 512 * half:1024 * k + 512 * (half + 1)],
                        start=False, stop=(k == 7))
            p.inc(ins, 'pe')
        p.t(mmo)

        def evo(vector, t=t, ps0=ps0, ps1=ps1, xsl=xsl):
            p.wait(vector, 'pe')
            vector.scalar_tensor_tensor(
                out=xsl[:, 0:512], in0=ps0, scalar=1.0, in1=xsl[:, 0:512],
                op0=ALU.mult, op1=ALU.add)
            ins = vector.scalar_tensor_tensor(
                out=xsl[:, 512:1024], in0=ps1, scalar=1.0, in1=xsl[:, 512:1024],
                op0=ALU.mult, op1=ALU.add)
            p.inc(ins, 'vps')
        p.v(evo)

        def sto(sync, t=t, xsl=xsl):
            p.wait(sync, 'vps')
            ins = sync.dma_start(out=x_out[128 * t:128 * (t + 1), :], in_=xsl)
            p.inc(ins, 'st', 16)
        p.y(sto)


def _vslot(xab, t, h):
    return xab[:, 1040 * h + 65 * t:1040 * h + 65 * t + 64]


def _vslot65(xab, t, h):
    return xab[:, 1040 * h + 65 * t:1040 * h + 65 * t + 65]


def _vslots(xab, t, half):
    base = 1040 * (8 * half) + 65 * t
    return xab[:, base:base + 8 * 1040].rearrange(
        "p (h r) -> p h r", h=8)[:, :, 0:64]





def emit_l0_attn_a(p, ar, x_in, x_out, w):
    xa = ar['XA']
    xbb = ar['XBb']
    xcb = ar['XCb']
    wsb = ar['WSb']
    stats = ar['stats']
    q32 = ar['q32']
    q32b = ar['q32'].bitcast(BF16)
    ssb = ar['ssb']
    u512 = ar['u512']
    g512 = ar['g512']

    emit_scale_load(p, ar, w['L0_attn_a_g1'], 1024, 0)
    emit_scale_load(p, ar, w['L0_attn_a_b1'], 1024, 8)
    emit_scale_load(p, ar, w['L0_attn_a_bqkv_a'][0:1024], 1024, 16)
    emit_scale_load(p, ar, w['L0_attn_a_bqkv_a'][1024:2048], 1024, 24)
    emit_scale_load(p, ar, w['L0_attn_a_gm'], 1024, 32)
    emit_scale_load(p, ar, w['L0_attn_a_bm'], 1024, 40)
    emit_scale_load(p, ar, w['L0_attn_a_b1m'], 4096, 48)
    emit_scale_load(p, ar, w['L0_attn_a_bqkv_s'][0:1024], 1024, 80)
    emit_scale_load(p, ar, w['L0_attn_a_bqkv_s'][1024:2048], 1024, 88)

    def scale_qbias(vector):
        p.wait(vector, 've')
        vector.tensor_scalar(ssb[:, 16:24], ssb[:, 16:24], 0.125, None,
                             ALU.mult)
        ins = vector.tensor_scalar(ssb[:, 80:88], ssb[:, 80:88], 0.125, None,
                                   ALU.mult)
        vector.drain()
        p.inc(ins, 've')
    p.v(scale_qbias)

    bva = hilo_bias(p, ar, [(w['L0_attn_a_bqkv_a'][2048:3072], 1024)], 0)
    bvs = hilo_bias(p, ar, [(w['L0_attn_a_bqkv_s'][2048:3072], 1024)], 1)

    emit_load_x(p, x_in, xa)
    emit_ln_stats(p, xa, ar['dump'], stats, 1e-5, double_eps=1e-5)
    emit_normalize(p, xa, stats, xcb[:, 0:16384])
    emit_tm_to_fm(p, xcb[:, 0:16384], xcb, ar['scr_tm'], wait_sem='se')

    def affine(vector):
        p.wait(vector, 'sd')
        for c in range(NCH):
            ins = vector.tensor_scalar(
                xcb[:, 2048 * c:2048 * (c + 1)], xcb[:, 2048 * c:2048 * (c + 1)],
                ssb[:, c:c + 1], ssb[:, 8 + c:9 + c], ALU.mult, ALU.add)
        p.inc(ins, 've')
    p.v(affine)
    i1 = xcb

    q2 = q32b[:, 0:2048]
    k2 = q32b[:, 2048:4096]
    v2 = q32b[:, 4096:6144]
    pstg = q32b[:, 6144:6400]
    ptst = q32b[:, 6400:6656]

    for hp in range(8):
        def ldwa(sync, hp=hp):
            p.wait(sync, 'pe')
            ins = sync.dma_start(
                out=wsb[:, 0:1024].rearrange("p (k o) -> p k o", k=8),
                in_=ar['cache']['L0_attn_a_Wqkv_a'].rearrange("(k p) o -> p k o", p=128)[
                    :, :, 128 * hp:128 * (hp + 1)])
            p.inc(ins, 'sd', 16)
            ins = sync.dma_start(
                out=wsb[:, 1024:2048].rearrange("p (k o) -> p k o", k=8),
                in_=ar['cache']['L0_attn_a_Wqkv_a'].rearrange("(k p) o -> p k o", p=128)[
                    :, :, 1024 + 128 * hp:1024 + 128 * (hp + 1)])
            p.inc(ins, 'sd', 16)
            ins = sync.dma_start(
                out=wsb[:, 2048:3072].rearrange("p (k o) -> p k o", k=8),
                in_=ar['cache']['L0_attn_a_Wqkv_a'].rearrange("(k p) o -> p k o", p=128)[
                    :, :, 2048 + 128 * hp:2048 + 128 * (hp + 1)])
            p.inc(ins, 'sd', 16)
        p.y(ldwa)

        for which, dst, bcol, scl in ((0, q2, 16, 0.125), (1, k2, 24, 1.0)):
            for g in range(4):
                ps = ar['psf'][g % 2]

                def mmqk(tensor, which=which, g=g, ps=ps, hp=hp):
                    p.wait(tensor, 'gd')
                    p.wait(tensor, 'sd')
                    p.wait(tensor, 'se')
                    p.wait(tensor, 've')
                    for k in range(8):
                        ins = tensor.matmul(
                            ps[:, 0:512],
                            wsb[:, 1024 * which + 128 * k:
                                1024 * which + 128 * (k + 1)],
                            i1[:, 2048 * k + 512 * g:2048 * k + 512 * (g + 1)],
                            start=(k == 0), stop=(k == 7))
                    p.inc(ins, 'pe')
                p.t(mmqk)

                def evqk(scalar, which=which, g=g, ps=ps, dst=dst, bcol=bcol,
                         scl=scl, hp=hp):
                    p.wait(scalar, 'pe')
                    ins = scalar.activation(
                        out=dst[:, 512 * g:512 * (g + 1)], in_=ps[:, 0:512],
                        func=AF.Identity, scale=scl,
                        bias=ssb[:, bcol + hp:bcol + hp + 1])
                    p.inc(ins, 'se')
                p.s(evqk)

        for t in range(NT):
            ps = ar['psf'][2 + (t % 2)]

            def mmva(tensor, t=t, ps=ps, hp=hp):
                p.wait(tensor, 'gd')
                p.wait(tensor, 'sd')
                p.wait(tensor, 've')
                bias_mm(p, tensor, ps[:, 0:128], bva, 128 * hp, 128 * (hp + 1),
                        start=True, stop=False)
                for k in range(8):
                    ins = tensor.matmul(
                        ps[:, 0:128],
                        i1[:, 2048 * k + 128 * t:2048 * k + 128 * (t + 1)],
                        wsb[:, 2048 + 128 * k:2048 + 128 * (k + 1)],
                        start=False, stop=(k == 7))
                p.inc(ins, 'pe')
            p.t(mmva)

            def evva(vector, t=t, ps=ps):
                p.wait(vector, 'pe')
                ins = vector.tensor_copy(v2[:, 128 * t:128 * (t + 1)],
                                         ps[:, 0:128])
                p.inc(ins, 've')
            p.v(evva)

        for h in (2 * hp, 2 * hp + 1):
            hb = 64 * (h % 2)
            for n in range(NT):
                psc = ar['psf'][0][:, 0:256]
                pav = ar['psb'][0][0:64, 0:512]
                ptp = ar['psf'][1][:, 0:512].bitcast(BF16)[:, 0:256]
                wid = 128 if n == 0 else 256
                klo = 128 * (n - 1) if n >= 1 else 0

                def mmsc(tensor, h=h, n=n, hb=hb, psc=psc, wid=wid, klo=klo):
                    p.wait(tensor, 'se')
                    p.wait(tensor, 've')
                    ins = tensor.matmul(
                        psc[:, 0:wid], q2[hb:hb + 64, 128 * n:128 * (n + 1)],
                        k2[hb:hb + 64, klo:klo + wid],
                        start=True, stop=True)
                    p.inc(ins, 'pe')
                p.t(mmsc)

                def smax(vector, n=n, psc=psc, wid=wid):
                    p.wait(vector, 'pe')
                    mask = ar['cswa0'] if n == 0 else ar['cswa1']
                    vector.tensor_tensor(psc[:, 0:wid], psc[:, 0:wid],
                                         mask[:, 0:wid], ALU.add)
                    vector.tensor_reduce(u512[:, 0:1], psc[:, 0:wid], AX.X,
                                         ALU.max)
                    vector.drain()
                    ins = vector.tensor_scalar(u512[:, 1:2], u512[:, 0:1],
                                               -1.0, None, ALU.mult)
                    vector.drain()
                    p.inc(ins, 've')
                p.v(smax)

                def sexp(scalar, psc=psc, wid=wid):
                    p.wait(scalar, 've')
                    ins = scalar.activation(out=pstg[:, 0:wid], in_=psc[:, 0:wid],
                                            func=AF.Exp, bias=u512[:, 1:2],
                                            accum_out=u512[:, 2:3])
                    p.inc(ins, 'se')
                p.s(sexp)

                def pnorm(vector, wid=wid):
                    p.wait(vector, 'se')
                    vector.reciprocal(u512[:, 3:4], u512[:, 2:3])
                    vector.drain()
                    ins = vector.tensor_scalar(pstg[:, 0:wid], pstg[:, 0:wid],
                                               u512[:, 3:4], None, ALU.mult)
                    vector.drain()
                    p.inc(ins, 've')
                p.v(pnorm)

                def ptr(tensor, wid=wid, ptp=ptp):
                    p.wait(tensor, 've')
                    for j in range(wid // 128):
                        ins = tensor.transpose(
                            ptp[:, 128 * j:128 * (j + 1)],
                            pstg[:, 128 * j:128 * (j + 1)], ar['ident'])
                    p.inc(ins, 'pe')
                p.t(ptr)

                def pte(vector, wid=wid, ptp=ptp):
                    p.wait(vector, 'pe')
                    ins = vector.tensor_copy(ptst[:, 0:wid], ptp[:, 0:wid])
                    p.inc(ins, 've')
                p.v(pte)

                def mmav(tensor, h=h, n=n, hb=hb, wid=wid, klo=klo, pav=pav):
                    p.wait(tensor, 've')
                    for j in range(wid // 128):
                        ins = tensor.matmul(
                            pav[:, 128 * (n % 4):128 * (n % 4) + 128],
                            v2[:, 128 * (klo // 128 + j) + hb:
                                128 * (klo // 128 + j) + hb + 64],
                            ptst[:, 128 * j:128 * (j + 1)],
                            start=(n % 4 == 0 and j == 0),
                            stop=(n % 4 == 3 and j == wid // 128 - 1),
                            skip_group_check=True)
                    p.inc(ins, 'pe')
                p.t(mmav)

                if n % 4 == 3:
                    def evav(vector, h=h, n=n, hb=hb, pav=pav):
                        p.wait(vector, 'pe')
                        ins = vector.tensor_copy(
                            xbb[hb:hb + 64, 2048 * (h // 2) + 128 * (n - 3):
                                2048 * (h // 2) + 128 * (n + 1)],
                            pav)
                        p.inc(ins, 've')
                    p.v(evav)

    if S0_PHASES <= 1:
        emit_store_x(p, xa, x_out, wait='ve')
        return
    q2l, k2l = q2, k2
    lstg = q32b[:, 6176:6304]
    lAT = q32b[:, 6304:6432]
    v2l = q32b[:, 4096:6176]
    ssl = [ar['w23'][0:64, 0:65], ar['w23'][64:128, 0:65]]

    for hp in range(8):
        def ldws(sync, hp=hp):
            p.wait(sync, 'pe')
            ins = sync.dma_start(
                out=wsb[:, 0:1024].rearrange("p (k o) -> p k o", k=8),
                in_=ar['cache']['L0_attn_a_Wqkv_s'].rearrange("(k p) o -> p k o", p=128)[
                    :, :, 128 * hp:128 * (hp + 1)])
            p.inc(ins, 'sd', 16)
            ins = sync.dma_start(
                out=wsb[:, 1024:2048].rearrange("p (k o) -> p k o", k=8),
                in_=ar['cache']['L0_attn_a_Wqkv_s'].rearrange("(k p) o -> p k o", p=128)[
                    :, :, 1024 + 128 * hp:1024 + 128 * (hp + 1)])
            p.inc(ins, 'sd', 16)
            ins = sync.dma_start(
                out=wsb[:, 2048:3072].rearrange("p (k o) -> p k o", k=8),
                in_=ar['cache']['L0_attn_a_Wqkv_s'].rearrange("(k p) o -> p k o", p=128)[
                    :, :, 2048 + 128 * hp:2048 + 128 * (hp + 1)])
            p.inc(ins, 'sd', 16)
        p.y(ldws)

        def onesl(gpsimd, hp=hp):
            p.wait(gpsimd, 've')
            p.wait(gpsimd, 'pe')
            ins = gpsimd.memset(v2l, 1.0)
            p.inc(ins, 'gp')
        p.g(onesl)

        for which, dst, bcol, scl in ((0, q2l, 80, 0.125), (1, k2l, 88, 1.0)):
            for g in range(4):
                ps = ar['psf'][g % 2]

                def mmqs(tensor, which=which, g=g, ps=ps):
                    p.wait(tensor, 'gd')
                    p.wait(tensor, 'sd')
                    p.wait(tensor, 've')
                    p.wait(tensor, 'se')
                    for k in range(8):
                        ins = tensor.matmul(
                            ps[:, 0:512],
                            wsb[:, 1024 * which + 128 * k:
                                1024 * which + 128 * (k + 1)],
                            i1[:, 2048 * k + 512 * g:2048 * k + 512 * (g + 1)],
                            start=(k == 0), stop=(k == 7))
                    p.inc(ins, 'pe')
                p.t(mmqs)

                def evz(scalar, which=which, g=g, ps=ps, bcol=bcol, scl=scl,
                        hp=hp):
                    p.wait(scalar, 'pe')
                    scalar.activation(out=u512, in_=ps[:, 0:512],
                                      func=AF.Identity, scale=scl,
                                      bias=ssb[:, bcol + hp:bcol + hp + 1])
                    ins = scalar.activation(out=g512[:, 0:512], in_=u512,
                                            func=AF.Exp)
                    p.inc(ins, 'se')
                p.s(evz)

                def evf(vector, which=which, g=g, dst=dst):
                    p.wait(vector, 'se')
                    vector.tensor_scalar(g512[:, 0:512], g512[:, 0:512],
                                         1.0, 0.0001, ALU.min, ALU.add)
                    vector.tensor_scalar(u512, u512, 0.0, None, ALU.max)
                    ins = vector.tensor_tensor(dst[:, 512 * g:512 * (g + 1)],
                                               u512, g512[:, 0:512], ALU.add)
                    p.inc(ins, 've')
                p.v(evf)

        for t in range(NT):
            ps = ar['psf'][2 + (t % 2)]

            def mmvs(tensor, t=t, ps=ps, hp=hp):
                p.wait(tensor, 'gd')
                p.wait(tensor, 'sd')
                p.wait(tensor, 've')
                p.wait(tensor, 'gp')
                bias_mm(p, tensor, ps[:, 0:128], bvs, 128 * hp, 128 * (hp + 1),
                        start=True, stop=False)
                for k in range(8):
                    ins = tensor.matmul(
                        ps[:, 0:128],
                        i1[:, 2048 * k + 128 * t:2048 * k + 128 * (t + 1)],
                        wsb[:, 2048 + 128 * k:2048 + 128 * (k + 1)],
                        start=False, stop=(k == 7))
                p.inc(ins, 'pe')
            p.t(mmvs)

            def evvs(vector, t=t, ps=ps):
                p.wait(vector, 'pe')
                ins = vector.tensor_copy(
                    _lslots(v2l, t),
                    ps[:, 0:128].rearrange("p (a e) -> p a e", a=2))
                p.inc(ins, 'vps')
            p.v(evvs)

        for h in (2 * hp, 2 * hp + 1):
            hb = 64 * (h % 2)
            sext = ssl[h % 2]
            psS = ar['psb'][2][0:64, 0:65] if h % 2 == 0 else \
                ar['psb'][3][0:64, 0:65]

            def sinit(vector, sext=sext):
                p.wait(vector, 'vps')
                ins = vector.memset(sext, 0.0)
                vector.drain()
                p.inc(ins, 've')
            p.v(sinit)

            for n in range(NT):
                psA = ar['psf'][0][:, 0:128]
                pAT = ar['psf'][1][:, 0:512].bitcast(BF16)[:, 0:128]
                pKT = ar['psf'][2][:, 0:512].bitcast(BF16)[:, 0:64]
                pnum = ar['psf'][3][:, 0:65]
                pot = ar['psb'][1][:, 0:512].bitcast(BF16)[:, 0:128]

                def mmA(tensor, h=h, n=n, hb=hb, psA=psA):
                    p.wait(tensor, 've')
                    p.wait(tensor, 'se')
                    ins = tensor.matmul(
                        psA, q2l[hb:hb + 64, 128 * n:128 * (n + 1)],
                        k2l[hb:hb + 64, 128 * n:128 * (n + 1)],
                        start=True, stop=True)
                    p.inc(ins, 'pe')
                p.t(mmA)

                def evA(vector, psA=psA):
                    p.wait(vector, 'pe')
                    ins = vector.tensor_tensor(lstg[:, 0:128], psA,
                                               ar['ctril'], ALU.mult)
                    p.inc(ins, 've')
                p.v(evA)

                def trAK(tensor, n=n, hb=hb, pAT=pAT, pKT=pKT):
                    p.wait(tensor, 've')
                    tensor.transpose(pAT, lstg[:, 0:128], ar['ident'])
                    ins = tensor.transpose(
                        pKT, k2l[hb:hb + 64, 128 * n:128 * (n + 1)],
                        ar['ident'][hb:hb + 64, hb:hb + 64])
                    p.inc(ins, 'pe')
                p.t(trAK)

                def evAT(vector, pAT=pAT, pKT=pKT):
                    p.wait(vector, 'pe')
                    vector.tensor_copy(lAT[:, 0:128], pAT)
                    ins = vector.tensor_copy(q32b[:, 6432:6496], pKT)
                    p.inc(ins, 've')
                p.v(evAT)

                def mmnum(tensor, h=h, n=n, hb=hb, pnum=pnum, sext=sext):
                    p.wait(tensor, 've')
                    tensor.matmul(pnum, lAT[:, 0:128], _lslot65(v2l, h, n),
                                  start=True, stop=False,
                                  skip_group_check=True)
                    ins = tensor.matmul(
                        pnum, q2l[hb:hb + 64, 128 * n:128 * (n + 1)], sext,
                        start=False, stop=True, skip_group_check=True)
                    p.inc(ins, 'pe')
                p.t(mmnum)

                def mmS(tensor, h=h, n=n, psS=psS):
                    ins = tensor.matmul(
                        psS, q32b[:, 6432:6496], _lslot65(v2l, h, n),
                        start=(n == 0), stop=(n == NT - 1),
                        skip_group_check=True)
                    p.inc(ins, 'pe')
                p.t(mmS)

                def evnum(vector, n=n, pnum=pnum, psS=psS, sext=sext):
                    p.wait(vector, 'pe')
                    vector.tensor_scalar(u512[:, 0:1], pnum[:, 64:65], 1e-6,
                                         None, ALU.add)
                    vector.drain()
                    vector.reciprocal(u512[:, 1:2], u512[:, 0:1])
                    vector.drain()
                    vector.tensor_scalar(lstg[:, 0:64], pnum[:, 0:64],
                                         u512[:, 1:2], None, ALU.mult)
                    ins = vector.tensor_copy(sext, psS)
                    vector.drain()
                    p.inc(ins, 've')
                p.v(evnum)

                def trO(tensor, pot=pot):
                    p.wait(tensor, 've')
                    ins = tensor.transpose(pot[0:64, :], lstg[:, 0:64],
                                           ar['ident'])
                    p.inc(ins, 'pe')
                p.t(trO)

                def evO(vector, h=h, n=n, hb=hb, pot=pot):
                    p.wait(vector, 'pe')
                    ins = vector.tensor_copy(
                        xbb[hb:hb + 64, 16384 + 2048 * (h // 2) + 128 * n:
                            16384 + 2048 * (h // 2) + 128 * (n + 1)],
                        pot[0:64, :])
                    p.inc(ins, 've')
                p.v(evO)

    if S0_PHASES <= 2:
        emit_store_x(p, xa, x_out, wait='ve')
        return
    swa_o = xbb[:, 0:16384]
    lin_o = xbb[:, 16384:32768]
    emit_scale_load(p, ar, w['L0_attn_a_bp_a'], 1024, 96)
    emit_scale_load(p, ar, w['L0_attn_a_bo_s'], 1024, 104)

    def combb(vector):
        p.wait(vector, 've')
        ins = vector.tensor_tensor(ssb[:, 96:104], ssb[:, 96:104],
                                   ssb[:, 104:112], ALU.add)
        vector.drain()
        p.inc(ins, 've')
    p.v(combb)

    for og in range(2):
        def ldmx(sync, og=og):
            p.wait(sync, 'pe')
            ins = sync.dma_start(
                out=wsb[:, 0:4096].rearrange("p (k o) -> p k o", k=8),
                in_=ar['cache']['L0_attn_a_Wp_a'].rearrange("(k p) o -> p k o", p=128)[
                    :, :, 512 * og:512 * (og + 1)])
            p.inc(ins, 'sd', 16)
            ins = sync.dma_start(
                out=wsb[:, 4096:8192].rearrange("p (k o) -> p k o", k=8),
                in_=ar['cache']['L0_attn_a_Wo_s'].rearrange("(k p) o -> p k o", p=128)[
                    :, :, 512 * og:512 * (og + 1)])
            p.inc(ins, 'sd', 16)
        p.y(ldmx)

        for sub in range(16):
            oc, g = divmod(sub, 4)
            oabs = 4 * og + oc
            ps = ar['psf'][sub % 2]

            def mmmx(tensor, oc=oc, g=g, ps=ps, sub=sub, og=og, oabs=oabs):
                p.wait(tensor, 'gd')
                p.wait(tensor, 'sd')
                p.wait(tensor, 've')
                if not (og == 0 and sub < 2):
                    p.wait(tensor, 'se', p.cnt['se'] - 1)
                for k in range(8):
                    tensor.matmul(
                        ps[:, 0:512],
                        wsb[:, 0:4096].rearrange("p (k o) -> p k o", k=8)[
                            :, k, 128 * oc:128 * (oc + 1)],
                        swa_o[:, 2048 * k + 512 * g:2048 * k + 512 * (g + 1)],
                        start=(k == 0), stop=False)
                for k in range(8):
                    ins = tensor.matmul(
                        ps[:, 0:512],
                        wsb[:, 4096:8192].rearrange("p (k o) -> p k o", k=8)[
                            :, k, 128 * oc:128 * (oc + 1)],
                        lin_o[:, 2048 * k + 512 * g:2048 * k + 512 * (g + 1)],
                        start=False, stop=(k == 7))
                p.inc(ins, 'pe')
            p.t(mmmx)

            def evmx(scalar, oabs=oabs, g=g, ps=ps):
                p.wait(scalar, 'pe')
                ins = scalar.activation(
                    out=xcb[:, 2048 * oabs + 512 * g:2048 * oabs + 512 * (g + 1)],
                    in_=ps[:, 0:512], func=AF.Identity,
                    scale=1.0, bias=ssb[:, 96 + oabs:97 + oabs])
                p.inc(ins, 'se')
            p.s(evmx)

    bo = hilo_bias(p, ar, [(w['L0_attn_a_bo'], 1024)], 0)

    def ldwo2(sync):
        p.wait(sync, 'pe')
        ins = sync.dma_start(
            out=wsb[:, 0:8192].rearrange("p (k o) -> p k o", k=8),
            in_=ar['cache']['L0_attn_a_Wo'].rearrange("(k p) o -> p k o", p=128))
        p.inc(ins, 'sd', 16)
    p.y(ldwo2)

    for t in range(NT):
        ps0, ps1 = ar['psf'][(t % 2) * 2], ar['psf'][(t % 2) * 2 + 1]

        def mmwo(tensor, t=t, ps0=ps0, ps1=ps1):
            p.wait(tensor, 'gd')
            p.wait(tensor, 'sd')
            p.wait(tensor, 'se')
            p.wait(tensor, 've')
            if t >= 2:
                p.wait(tensor, 'vps', p.cnt['vps'] - 1)
            for half, ps in ((0, ps0), (1, ps1)):
                bias_mm(p, tensor, ps, bo, 512 * half, 512 * (half + 1),
                        start=True, stop=False)
                for k in range(8):
                    ins = tensor.matmul(
                        ps, xcb[:, 2048 * k + 128 * t:2048 * k + 128 * (t + 1)],
                        wsb[:, 1024 * k + 512 * half:1024 * k + 512 * (half + 1)],
                        start=False, stop=(k == 7))
            p.inc(ins, 'pe')
        p.t(mmwo)

        def evwo(vector, t=t, ps0=ps0, ps1=ps1):
            p.wait(vector, 'pe')
            for half, ps in ((0, ps0), (1, ps1)):
                vector.tensor_scalar(
                    xa[:, 1024 * t + 512 * half:1024 * t + 512 * (half + 1)],
                    xa[:, 1024 * t + 512 * half:1024 * t + 512 * (half + 1)],
                    stats[:, 64 + t:65 + t], stats[:, 80 + t:81 + t],
                    ALU.mult, ALU.add)
                ins = vector.tensor_tensor(
                    xa[:, 1024 * t + 512 * half:1024 * t + 512 * (half + 1)],
                    xa[:, 1024 * t + 512 * half:1024 * t + 512 * (half + 1)],
                    ps, ALU.add)
            p.inc(ins, 'vps')
        p.v(evwo)

    if S0_PHASES <= 3:
        def gate3(vector):
            p.wait(vector, 'vps')
            ins = vector.tensor_copy(u512[:, 0:1], xa[:, 0:1])
            p.inc(ins, 'vps')
        p.v(gate3)
        emit_store_x(p, xa, x_out, wait='vps')
        return
    def gate_mlp(scalar):
        p.wait(scalar, 'vps')
    p.s(gate_mlp)
    emit_ln_stats(p, xa, ar['dump'], stats, 1e-5)
    emit_normalize(p, xa, stats, xcb[:, 0:16384])
    emit_tm_to_fm(p, xcb[:, 0:16384], xcb, ar['scr_tm'], wait_sem='se')

    def affinem(vector):
        p.wait(vector, 'sd')
        for c in range(NCH):
            ins = vector.tensor_scalar(
                xcb[:, 2048 * c:2048 * (c + 1)], xcb[:, 2048 * c:2048 * (c + 1)],
                ssb[:, 32 + c:33 + c], ssb[:, 40 + c:41 + c], ALU.mult, ALU.add)
        p.inc(ins, 've')
    p.v(affinem)

    b2m = hilo_bias(p, ar, [(w['L0_attn_a_b2m'], 1024)], 1)

    for tb in range(2):
        t0 = 1024 * tb
        for og in range(8):
            o0 = 512 * og

            def ldw1(sync, o0=o0):
                p.wait(sync, 'pe')
                p.wait(sync, 've')
                p.wait(sync, 'se')
                p.wait(sync, 'vps')
                ins = sync.dma_start(
                    out=wsb[:, 0:4096].rearrange("p (k o) -> p k o", k=8),
                    in_=ar['cache']['L0_attn_a_W1'].rearrange(
                        "(k p) o -> p k o", p=128)[:, :, o0:o0 + 512])
                p.inc(ins, 'sd', 16)
            p.y(ldw1)

            for sub in range(8):
                oc, g = divmod(sub, 2)
                oabs = 4 * og + oc
                ps = ar['psf'][sub % 4]

                def mm1(tensor, oc=oc, g=g, ps=ps, t0=t0):
                    p.wait(tensor, 'gd')
                    p.wait(tensor, 'sd')
                    p.wait(tensor, 'se')
                    for k in range(8):
                        ins = tensor.matmul(
                            ps,
                            wsb[:, 512 * k + 128 * oc:512 * k + 128 * (oc + 1)],
                            xcb[:, 2048 * k + t0 + 512 * g:
                                2048 * k + t0 + 512 * (g + 1)],
                            start=(k == 0), stop=(k == 7))
                    p.inc(ins, 'pe')
                p.t(mm1)

                def ev1(scalar, ps=ps, oabs=oabs, g=g):
                    p.wait(scalar, 'pe')
                    ins = scalar.activation(
                        out=xbb[:, 1024 * oabs + 512 * g:
                                1024 * oabs + 512 * (g + 1)],
                        in_=ps, func=AF.Gelu,
                        scale=1.0, bias=ssb[:, 48 + oabs:49 + oabs])
                    p.inc(ins, 'se')
                p.s(ev1)

        for th in range(2):
            tt0 = t0 + 512 * th

            def ldx2(sync, tt0=tt0):
                p.wait(sync, 'vps')
                p.wait(sync, 've')
                p.wait(sync, 'st')
                p.wait(sync, 'gd')
                ins = sync.dma_start(
                    out=q32.rearrange("p (t f) -> p t f", t=4),
                    in_=x_in[tt0:tt0 + 512, :].rearrange(
                        "(t p) f -> p t f", p=128))
                p.inc(ins, 'sd', 16)
            p.y(ldx2)

            for kg in range(4):
                def ldw2(sync, kg=kg):
                    p.wait(sync, 'pe')
                    p.wait(sync, 'sd')
                    if kg == 0:
                        p.wait(sync, 'se')
                    ins = sync.dma_start(
                        out=wsb[:, 0:8192].rearrange("p (k o) -> p k o", k=8),
                        in_=ar['cache']['L0_attn_a_W2'].rearrange(
                            "(k p) o -> p k o", p=128)[:, 8 * kg:8 * kg + 8, :])
                    p.inc(ins, 'sd', 16)
                p.y(ldw2)

                def mm2(tensor, kg=kg, tt0=tt0, t0=t0):
                    p.wait(tensor, 'gd')
                    p.wait(tensor, 'sd')
                    if kg == 0:
                        p.wait(tensor, 'vps')
                        p.wait(tensor, 'se')
                    for tt in range(4):
                        for half in range(2):
                            ps = ar['psf'][tt] if half == 0 else ar['psb'][tt]
                            for k8 in range(8):
                                k = 8 * kg + k8
                                ins = tensor.matmul(
                                    ps,
                                    xbb[:, 1024 * k + (tt0 - t0) + 128 * tt:
                                        1024 * k + (tt0 - t0) + 128 * (tt + 1)],
                                    wsb[:, 1024 * k8 + 512 * half:
                                        1024 * k8 + 512 * (half + 1)],
                                    start=(kg == 0 and k8 == 0), stop=False)
                    p.inc(ins, 'pe')
                p.t(mm2)

            def mmb2(tensor, tt0=tt0):
                for tt in range(4):
                    for half in range(2):
                        ps = ar['psf'][tt] if half == 0 else ar['psb'][tt]
                        ins = tensor.matmul(
                            ps, p.ones2[:, 0:128],
                            b2m[:, 512 * half:512 * (half + 1)],
                            start=False, stop=True)
                p.inc(ins, 'pe')
            p.t(mmb2)

            def ev2(vector, tt0=tt0):
                p.wait(vector, 'pe')
                xbase = (tt0 // 128) * 1024
                for tt in range(4):
                    for half in range(2):
                        ps = ar['psf'][tt] if half == 0 else ar['psb'][tt]
                        col = 1024 * tt + 512 * half
                        vector.scalar_tensor_tensor(
                            out=q32[:, col:col + 512], in0=ps, scalar=1.0,
                            in1=q32[:, col:col + 512], op0=ALU.mult,
                            op1=ALU.add)
                        ins = vector.tensor_tensor(
                            q32[:, col:col + 512], q32[:, col:col + 512],
                            xa[:, xbase + col:xbase + col + 512], ALU.add)
                p.inc(ins, 'vps')
            p.v(ev2)

            def st2(sync, tt0=tt0):
                p.wait(sync, 'vps')
                ins = sync.dma_start(
                    out=x_out[tt0:tt0 + 512, :].rearrange(
                        "(t p) f -> p t f", p=128),
                    in_=q32.rearrange("p (t f) -> p t f", t=4))
                p.inc(ins, 'st', 16)
            p.y(st2)


def _lslots(v2l, t):
    return v2l.rearrange("p (a r) -> p a r", a=2)[:, :, 65 * t:65 * t + 64]


def _lslot65(v2l, h, n):
    return v2l[:, 1040 * (h % 2) + 65 * n:1040 * (h % 2) + 65 * n + 65]





CACHE_SPECS = [
    ('L0_attn_a_Wqkv_a', 1024, 3072),
    ('L0_attn_a_Wqkv_s', 1024, 3072),
    ('L0_attn_a_Wp_a', 1024, 1024),
    ('L0_attn_a_Wo_s', 1024, 1024),
    ('L0_attn_a_Wo', 1024, 1024),
    ('L0_attn_a_W1', 1024, 4096),
    ('L0_attn_a_W2', 4096, 1024),
    ('L0_attn_b_wq', 1024, 1024),
    ('L0_attn_b_wk', 1024, 1024),
    ('L0_attn_b_wv', 1024, 1024),
    ('L0_attn_b_wo', 1024, 1024),
    ('L0_attn_b_w1', 1024, 4096),
    ('L0_attn_b_w2', 4096, 1024),
    ('L1_attn_b_W1', 1024, 1024),
]


def emit_cache_build(p, ar, w, needed):
    wsb = ar['WSb']
    for name, K, O in CACHE_SPECS:
        if name not in needed:
            continue
        src = w[name]
        if src.ndim == 3:
            src = src.rearrange("a b c -> (a b) c") if src.shape[0] * \
                src.shape[1] == K else src.rearrange("a b c -> a (b c)")
        cache = ar['cache'][name]
        cols = min(O, 4096)
        rows_per = 128
        for r0 in range(0, K, rows_per):
            for c0 in range(0, O, cols):
                half = ((r0 // rows_per + c0 // cols) % 2) * 4096
                buf = wsb[:, half:half + cols]

                def cast(gpsimd, buf=buf, src=src, r0=r0, c0=c0, cols=cols):
                    p.wait(gpsimd, 'sd')
                    ins = gpsimd.dma_start(
                        out=buf, in_=src[r0:r0 + 128, c0:c0 + cols])
                    p.inc(ins, 'gd', 16)
                p.g(cast)

                def wr(sync, buf=buf, cache=cache, r0=r0, c0=c0, cols=cols):
                    p.wait(sync, 'gd')
                    ins = sync.dma_start(
                        out=cache[r0:r0 + 128, c0:c0 + cols], in_=buf)
                    p.inc(ins, 'sd', 16)
                p.y(wr)



def build_arenas(nc, es):
    ar = {}
    sb = lambda n, s, d=F32: es.enter_context(nc.sbuf_tensor(n, s, d))
    XA = sb("XA", [128, 16384])
    XB = sb("XB", [128, 16384])
    XC = sb("XC", [128, 8192])
    WS = sb("WS", [128, 4096])
    ar['XA'] = XA[:]
    ar['XAb'] = XA[:].bitcast(BF16)
    ar['XB'] = XB[:]
    ar['XBb'] = XB[:].bitcast(BF16)
    ar['XCb'] = XC[:].bitcast(BF16)
    ar['WSb'] = WS[:].bitcast(BF16)
    ar['stats'] = sb("stats", [128, 96])[:]
    ar['q16'] = sb("q16", [128, 96])[:]
    ar['q32'] = sb("q32", [128, 4096])[:]
    ar['dump'] = ar['q32'][:, 3072:4096]
    ar['brow'] = ar['q32'][0:1, 0:2048]
    ar['s32'] = ar['q32'][0:32, 2048:2176]
    bias2 = sb("bias2", [2, 2176], BF16)
    ar['bias2'] = [bias2[:, 0:1024], bias2[:, 1024:2048], bias2[:, 2048:2176]]
    ar['ident'] = sb("ident", [128, 128], BF16)[:]
    ar['ones2'] = sb("ones2", [2, 128], BF16)[:]
    ar['ones128'] = sb("ones128", [128, 2], BF16)[:]
    ar['w23'] = sb("w23", [128, 256], BF16)
    ar['u512'] = sb("u512", [128, 512])[:]
    ar['g512'] = sb("g512", [128, 1024])[:]
    ar['ssb'] = sb("ssb", [128, 136])[:]

    ar['ident32'] = sb("ident32", [128, 128])[:]
    ar['ctri'] = sb("ctri", [128, 128])[:]
    ar['cswa0'] = sb("cswa0", [128, 128])[:]
    ar['cswa1'] = sb("cswa1", [128, 256])[:]
    ar['ctril'] = sb("ctril", [128, 128])[:]
    ar['psf'] = [es.enter_context(nc.psum_tensor(f"psf{i}", [128, 512], F32))[:]
                 for i in range(4)]
    ar['psb'] = [es.enter_context(nc.psum_tensor(f"psb{i}", [128, 512], F32))[:]
                 for i in range(4)]
    ar['scr_tm'] = nc.dram_tensor("scr_tm", [L, D], BF16, kind="Internal").ap()
    if DEBUG_ATTNB:
        ar['dbg_out'] = nc.dram_tensor("dbg", [128, 1056], F32,
                                       kind="ExternalOutput").ap()
        ar['dbg2_out'] = nc.dram_tensor("dbg2", [128, 1152], F32,
                                        kind="ExternalOutput").ap()
        ar['dbg3_out'] = nc.dram_tensor("dbg3", [128, 2048], BF16,
                                        kind="ExternalOutput").ap()
    ar['scr_w'] = nc.dram_tensor("scr_w", [1024, 1024], BF16, kind="Internal").ap()
    ar['pst'] = ar['psb'][0][:, 0:128]
    ar['c_raw'] = nc.dram_tensor("c_raw", [4096, 4096], BF16, kind="Internal").ap()
    ar['cT_w1'] = nc.dram_tensor("cT_w1", [1024, 4096], BF16, kind="Internal").ap()
    ar['cT_w2'] = nc.dram_tensor("cT_w2", [1024, 4096], BF16, kind="Internal").ap()
    ar['cT_h1'] = nc.dram_tensor("cT_h1", [4096, 4096], BF16, kind="Internal").ap()
    ar['cT_h2'] = nc.dram_tensor("cT_h2", [4096, 4096], BF16, kind="Internal").ap()
    ar['cT_w3'] = nc.dram_tensor("cT_w3", [4096, 1024], BF16, kind="Internal").ap()
    return ar


SUBLAYERS = ['l0_attn_a', 'l0_ffn', 'l0_attn_b', 'l0_ffn',
             'l1_attn_a', 'l1_ffn', 'l1_attn_b', 'l1_ffn']


def build_program(lo, hi, weights_meta):
    _, wlog = _build(lo, hi, weights_meta, 'record')
    nc, _ = _build(lo, hi, weights_meta, 'replay', wlog)
    return nc


def _build(lo, hi, weights_meta, mode, wlog=None):
    nc = bass.Bass("TRN2", target_bir_lowering=False)
    x_in = nc.dram_tensor("x", [L, D], F32, kind="ExternalInput").ap()
    x_out = nc.dram_tensor("y", [L, D], F32, kind="ExternalOutput").ap()
    consts = {'c_ident': nc.dram_tensor("c_ident", [128, 128], BF16,
                                        kind="ExternalInput").ap(),
              'c_tri': nc.dram_tensor("c_tri", [128, 128], F32,
                                      kind="ExternalInput").ap(),
              'c_swa0': nc.dram_tensor("c_swa0", [128, 128], F32,
                                       kind="ExternalInput").ap(),
              'c_swa1': nc.dram_tensor("c_swa1", [128, 256], F32,
                                       kind="ExternalInput").ap(),
              'c_tril': nc.dram_tensor("c_tril", [128, 128], F32,
                                       kind="ExternalInput").ap()}
    w = {name: nc.dram_tensor(name, list(shape), F32, kind="ExternalInput").ap()
         for name, shape in weights_meta}
    xbufs = [nc.dram_tensor(f"xbuf{i}", [L, D], F32, kind="Internal").ap()
             for i in range(2)]

    es = ExitStack()
    with es:
        ar = build_arenas(nc, es)
        ar['cache'] = {}
        needed = set()
        for nm, K, O in CACHE_SPECS:
            pref = nm[:nm.index('_', 3)]
            ar['cache'][nm] = nc.dram_tensor(
                "cch_" + nm, [K, O], BF16, kind="Internal").ap()
            needed.add(nm)
        with nc.Block() as block:
            p = P(nc, block, es, mode=mode, wlog=wlog)
            p.ones2 = ar['ones2']

            def init(gpsimd):
                gpsimd.memset(ar['ones2'], 1.0)
                ins = gpsimd.memset(ar['ones128'], 1.0)
                p.inc(ins, 'gp')
            p.g(init)
            p.ones128 = ar['ones128'][:, 0:1]

            def initc(sync):
                ins = sync.dma_start(out=ar['ident'], in_=consts['c_ident'])
                p.inc(ins, 'sd', 16)
                ins = sync.dma_start(out=ar['ctri'], in_=consts['c_tri'])
                p.inc(ins, 'sd', 16)
                ins = sync.dma_start(out=ar['cswa0'], in_=consts['c_swa0'])
                p.inc(ins, 'sd', 16)
                ins = sync.dma_start(out=ar['cswa1'], in_=consts['c_swa1'])
                p.inc(ins, 'sd', 16)
                ins = sync.dma_start(out=ar['ctril'], in_=consts['c_tril'])
                p.inc(ins, 'sd', 16)
            p.y(initc)

            def initc2(vector):
                p.wait(vector, 'sd')
                ins = vector.tensor_copy(ar['ident32'], ar['ident'])
                p.inc(ins, 've')
            p.v(initc2)

            emit_cache_build(p, ar, w, needed)
            first_l0ffn = True
            cur = x_in
            for i in range(lo, hi):
                name = SUBLAYERS[i]
                nxt = x_out if i == hi - 1 else xbufs[i % 2]
                if name == 'l1_ffn':
                    emit_l1_ffn(p, ar, cur, nxt, w, first=True)
                elif name == 'l1_attn_b':
                    emit_l1_attn_b(p, ar, cur, nxt, w)
                elif name == 'l0_ffn':
                    emit_l0_ffn(p, ar, cur, nxt, w, first=first_l0ffn)
                    first_l0ffn = False
                elif name == 'l0_attn_b':
                    emit_l0_attn_b(p, ar, cur, nxt, w)
                elif name == 'l1_attn_a':
                    emit_l1_attn_a(p, ar, cur, nxt, w)
                elif name == 'l0_attn_a':
                    emit_l0_attn_a(p, ar, cur, nxt, w)
                else:
                    raise NotImplementedError(name)
                cur = nxt
                if i != hi - 1:
                    p.barrier()

            def final(sync):
                p.wait(sync, 'st')
            p.y(final)
            if mode == 'replay':
                p.flush()
    return nc, p.wlog


def make_per_core(inputs):
    x = np.ascontiguousarray(inputs['L0_attn_a_x'])
    r_ = np.arange(128)[:, None]
    c_ = np.arange(128)[None, :]
    c2_ = np.arange(256)[None, :]
    consts = {
        'c_ident': np.eye(128, dtype=ml_dtypes.bfloat16),
        'c_tri': np.where(r_ > c_, np.float32(-10000.0), np.float32(0.0)),
        'c_swa0': np.where(c_ <= r_, np.float32(0.0), np.float32(-1e30)),
        'c_swa1': np.where((c2_ >= r_ + 1) & (c2_ <= r_ + 128),
                           np.float32(0.0), np.float32(-1e30)),
        'c_tril': np.where(r_ >= c_, np.float32(1.0), np.float32(0.0)),
    }
    per_core = []
    for b in range(8):
        m = {'x': x[b]}
        m.update(consts)
        for k, v in inputs.items():
            if k != 'L0_attn_a_x':
                m[k] = np.ascontiguousarray(v)
        per_core.append(m)
    return per_core


def kernel(**inputs):
    weights_meta = [(k, v.shape) for k, v in inputs.items() if k != 'L0_attn_a_x']
    per_core = make_per_core(inputs)
    nc = build_program(0, 8, weights_meta)
    res = run_bass_kernel_spmd(nc, per_core, core_ids=list(range(8)))
    return np.stack([r['y'] for r in res.results], axis=0)


# version 05 (final)
# speedup vs baseline: 1.1616x; 1.1616x over previous
import numpy as np
import ml_dtypes
from contextlib import ExitStack
import concourse.bass as bass
import concourse.mybir as mybir
from concourse.bass_utils import run_bass_kernel_spmd

F32 = mybir.dt.float32
BF16 = mybir.dt.bfloat16
AF = mybir.ActivationFunctionType
ALU = mybir.AluOpType
AX = mybir.AxisListType

L, D, H, E = 2048, 1024, 16, 64
DEBUG_ATTNB = False
S2_PHASES = 99
S2_V7 = 0
S0_PHASES = 99
NT = L // 128
NCH = D // 128
MAGIC = 12582912.0

ENGINES = ('tensor', 'vector', 'scalar', 'gpsimd', 'sync')



_PROXY_SKIP = {'wait_ge', 'drain', 'engine_nop', 'nop'}


class _EngProxy:

    def __init__(self, p, eng, semname):
        self._p, self._eng, self._sem = p, eng, semname

    def __getattr__(self, name):
        val = getattr(self._eng, name)
        if not callable(val) or name.startswith('_') or name in _PROXY_SKIP:
            return val

        def wrapper(*a, **k):
            p, eng, semname = self._p, self._eng, self._sem
            if p.cnt.get(semname, 0) > 0:
                eng.wait_ge(p._sem(semname), p.cnt[semname])
            ins = val(*a, **k)
            if ins is not None and hasattr(ins, 'then_inc'):
                p.inc(ins, semname)
            return ins
        return wrapper


class P:

    def __init__(self, nc, block, es, mode='record', wlog=None):
        self.nc, self.block, self.es = nc, block, es
        self.sems, self.cnt = {}, {}
        self.marks = {}
        self.ones2 = None
        self.mode = mode
        self.wlog = [] if wlog is None else wlog
        self.nsec = -1
        self.cur = -1
        self.queue = {e: [] for e in ENGINES}
        self.wptr = {}
        for n in ('sd', 'gd', 'st', 'pe', 'gp', 've', 'vps', 'se',
                  'vself', 'sself'):
            self.cnt[n] = 0

    def _sem(self, name):
        if name not in self.sems:
            self.sems[name] = self.es.enter_context(self.nc.semaphore(name))
            self.cnt.setdefault(name, 0)
        return self.sems[name]

    def _group(self, eng, pairs_fn):
        if self.mode == 'record':
            pairs = [(n, t) for (n, t) in pairs_fn() if t > 0]
            self.wlog[self.cur].append(pairs)
        else:
            i = self.wptr.get(self.cur, 0)
            self.wptr[self.cur] = i + 1
            pairs = self.wlog[self.cur][i]
        for n, t in pairs:
            eng.wait_ge(self._sem(n), t)

    def _section(self, engname, fn):
        self.nsec += 1
        if self.mode == 'record':
            self.cur = self.nsec
            self.wlog.append([])
            getattr(self.block, engname)(fn)
        else:
            self.queue[engname].append((self.nsec, fn))

    def flush(self):
        assert self.mode == 'replay'
        for engname in ENGINES:
            items = self.queue[engname]
            if not items:
                continue

            def runner(eng, items=items):
                for sec, fn in items:
                    self.cur = sec
                    fn(eng)
                    assert self.wptr.get(sec, 0) == len(self.wlog[sec]), \
                        f"section {sec}: wait groups not fully consumed"
            getattr(self.block, engname)(runner)

    ALIAS = {'ve': 'vself', 'vps': 'vself', 'se': 'sself'}

    def inc(self, instr, name, n=1):
        if name in self.ALIAS:
            base = self.ALIAS[name]
            self.cnt[name] = self.cnt.get(name, 0) + n
            self.marks.setdefault(name, {})[self.cnt[name]] = \
                self.cnt.get(base, 0)
            return self.cnt[name]
        instr.then_inc(self._sem(name), n)
        self.cnt[name] = self.cnt.get(name, 0) + n
        return self.cnt[name]

    def wait(self, eng, name, val=None):
        def pairs():
            if name in self.ALIAS:
                base = self.ALIAS[name]
                if val is None:
                    tgt = self.cnt.get(base, 0)
                else:
                    if val <= 0:
                        return []
                    tgt = self.marks[name][val]
                return [(base, tgt)]
            tgt = self.cnt.get(name, 0) if val is None else val
            return [(name, tgt)]
        self._group(eng, pairs)

    def t(self, fn): self._section('tensor', fn)

    def v(self, fn):
        self._section('vector',
                      lambda eng: fn(_EngProxy(self, eng, 'vself')))

    def s(self, fn):
        self._section('scalar',
                      lambda eng: fn(_EngProxy(self, eng, 'sself')))

    def _dma_section(self, engname, fn, sems):
        before = {n: self.cnt.get(n, 0) for n in sems}

        def wrapped(eng):
            fn(eng)
            self._group(eng, lambda: [(n, self.cnt[n]) for n in sems
                                      if self.cnt.get(n, 0) > before[n]])
        self._section(engname, wrapped)

    def g(self, fn): self._dma_section('gpsimd', fn, ('gd',))
    def y(self, fn): self._dma_section('sync', fn, ('sd', 'st'))

    def barrier(self):
        def mk(engname):
            def f(eng):
                self._group(eng, lambda: [(n, self.cnt[n])
                                          for n in list(self.sems)
                                          if self.cnt[n] > 0])
            return f
        for e in ENGINES:
            self._section(e, mk(e))



def hilo_bias(p, ar, vecs, slot):
    ntot = sum(n for _, n in vecs)
    f32row = ar['brow'][:, 0:ntot]
    f32row2 = ar['brow'][:, 1024:1024 + ntot]
    dst = ar['bias2'][slot][:, 0:ntot]

    def ld(sync):
        p.wait(sync, 'gd')
        p.wait(sync, 've')
        p.wait(sync, 'pe')
        off = 0
        for v, n in vecs:
            src = v if v.ndim == 2 else v.unsqueeze(0)
            ins = sync.dma_start(out=f32row[:, off:off + n], in_=src)
            p.inc(ins, 'sd', 16)
            off += n
    p.y(ld)

    def mk(vector):
        p.wait(vector, 'sd')
        vector.tensor_copy(dst[0:1, :], f32row)
        vector.drain()
        vector.tensor_copy(f32row2, dst[0:1, :])
        vector.drain()
        ins = vector.tensor_tensor(f32row2, f32row, f32row2, ALU.subtract)
        vector.drain()
        p.inc(ins, 've')
    p.v(mk)

    def place_lo(gpsimd):
        p.wait(gpsimd, 've')
        ins = gpsimd.dma_start(out=dst[1:2, :], in_=f32row2)
        p.inc(ins, 'gd', 16)
    p.g(place_lo)
    return dst


def bias_mm(p, tensor, psum, bias2, c0, c1, start=False, stop=False):
    tensor.matmul(psum, p.ones2[:, 0:psum.shape[0]], bias2[:, c0:c1],
                  start=start, stop=stop)


def emit_load_x(p, x_in, xa):
    def load(sync):
        p.wait(sync, 'st')
        ins = sync.dma_start(out=xa.rearrange("p (t f) -> p t f", t=NT),
                             in_=x_in.rearrange("(t p) f -> p t f", p=128))
        p.inc(ins, 'sd', 16)
    p.y(load)


def emit_ln_stats(p, xa, st_dump, stats, eps, double_eps=None, ntiles=NT):
    sums, sq = stats[:, 0:16], stats[:, 16:32]
    sc, nb = stats[:, 32:48], stats[:, 48:64]

    def pass1(scalar):
        p.wait(scalar, 'sd')
        for t in range(ntiles):
            ins = scalar.activation(out=st_dump, in_=xa[:, 1024 * t:1024 * (t + 1)],
                                    func=AF.Identity, accum_out=sums[:, t:t + 1])
        p.inc(ins, 'se')
    p.s(pass1)

    def negm(vector):
        p.wait(vector, 'se')
        ins = vector.tensor_scalar(sums[:, 0:ntiles], sums[:, 0:ntiles],
                                   -1.0 / 1024.0, None, ALU.mult)
        vector.drain()
        p.inc(ins, 've')
    p.v(negm)

    def pass2(scalar):
        p.wait(scalar, 've')
        for t in range(ntiles):
            ins = scalar.activation(out=st_dump, in_=xa[:, 1024 * t:1024 * (t + 1)],
                                    func=AF.Square, bias=sums[:, t:t + 1], scale=1.0,
                                    accum_out=sq[:, t:t + 1])
        p.inc(ins, 'se')
    p.s(pass2)

    def var_recip(vector):
        p.wait(vector, 'se')
        vector.tensor_scalar(sc[:, 0:ntiles], sq[:, 0:ntiles], 1.0 / 1024.0, eps,
                             ALU.mult, ALU.add)
        vector.drain()
        ins = vector.reciprocal(sc[:, 0:ntiles], sc[:, 0:ntiles])
        vector.drain()
        p.inc(ins, 've')
    p.v(var_recip)

    def sqrt1(scalar):
        p.wait(scalar, 've')
        ins = scalar.activation(out=sc[:, 0:ntiles], in_=sc[:, 0:ntiles],
                                func=AF.Sqrt)
        p.inc(ins, 'se')
    p.s(sqrt1)

    if double_eps is not None:
        def comb(vector):
            p.wait(vector, 'se')
            vector.tensor_scalar(nb[:, 0:ntiles], sq[:, 0:ntiles], 1.0 / 1024.0,
                                 None, ALU.mult)
            vector.drain()
            vector.tensor_tensor(nb[:, 0:ntiles], nb[:, 0:ntiles],
                                 sc[:, 0:ntiles], ALU.mult)
            vector.drain()
            vector.tensor_tensor(nb[:, 0:ntiles], nb[:, 0:ntiles],
                                 sc[:, 0:ntiles], ALU.mult)
            vector.drain()
            vector.tensor_scalar(nb[:, 0:ntiles], nb[:, 0:ntiles], double_eps,
                                 None, ALU.add)
            vector.drain()
            ins = vector.reciprocal(nb[:, 0:ntiles], nb[:, 0:ntiles])
            vector.drain()
            p.inc(ins, 've')
        p.v(comb)

        def sqrt2(scalar):
            p.wait(scalar, 've')
            ins = scalar.activation(out=nb[:, 0:ntiles], in_=nb[:, 0:ntiles],
                                    func=AF.Sqrt)
            p.inc(ins, 'se')
        p.s(sqrt2)

        def comb2(vector):
            p.wait(vector, 'se')
            vector.tensor_tensor(stats[:, 80:80 + ntiles], sums[:, 0:ntiles],
                                 sc[:, 0:ntiles], ALU.mult)
            vector.tensor_copy(stats[:, 64:64 + ntiles], sc[:, 0:ntiles])
            vector.tensor_tensor(sc[:, 0:ntiles], sc[:, 0:ntiles],
                                 nb[:, 0:ntiles], ALU.mult)
            vector.drain()
            ins = vector.tensor_tensor(nb[:, 0:ntiles], sums[:, 0:ntiles],
                                       sc[:, 0:ntiles], ALU.mult)
            vector.drain()
            p.inc(ins, 've')
        p.v(comb2)
    else:
        def finish(vector):
            p.wait(vector, 'se')
            ins = vector.tensor_tensor(nb[:, 0:ntiles], sums[:, 0:ntiles],
                                       sc[:, 0:ntiles], ALU.mult)
            vector.drain()
            p.inc(ins, 've')
        p.v(finish)


def emit_normalize(p, xa, stats, dst_bf, dst_f32=None, ntiles=NT):
    sc, nb = stats[:, 32:48], stats[:, 48:64]

    def norm(scalar):
        p.wait(scalar, 've')
        for t in range(ntiles):
            if dst_bf is not None:
                ins = scalar.activation(out=dst_bf[:, 1024 * t:1024 * (t + 1)],
                                        in_=xa[:, 1024 * t:1024 * (t + 1)],
                                        func=AF.Identity, bias=nb[:, t:t + 1],
                                        scale=sc[:, t:t + 1])
            if dst_f32 is not None:
                ins = scalar.activation(out=dst_f32[:, 1024 * t:1024 * (t + 1)],
                                        in_=xa[:, 1024 * t:1024 * (t + 1)],
                                        func=AF.Identity, bias=nb[:, t:t + 1],
                                        scale=sc[:, t:t + 1])
        p.inc(ins, 'se')
    p.s(norm)


def emit_tm_to_fm(p, src_bf_tm, fm_dst, scr, wait_sem, ntiles=NT, chunks=NCH):
    def to_dram(sync):
        p.wait(sync, wait_sem)
        ins = sync.dma_start(out=scr.rearrange("(t p) f -> p t f", p=128),
                             in_=src_bf_tm.rearrange("p (t f) -> p t f", t=ntiles))
        p.inc(ins, 'sd', 16)
    p.y(to_dram)

    def from_dram(sync):
        p.wait(sync, 'sd')
        for c in range(chunks):
            ins = sync.dma_start_transpose(
                out=fm_dst[:, ntiles * 128 * c:ntiles * 128 * (c + 1)],
                in_=scr[:, 128 * c:128 * (c + 1)])
            p.inc(ins, 'sd', 16)
    p.y(from_dram)


def emit_quant_tile(p, vector, src_f32, dst_bf, q16, q32):
    bm, scl, rs = q16[:, 0:32], q16[:, 32:64], q16[:, 64:96]
    aq = q32[:, 0:1024]
    t1 = q32[:, 1024:2048]
    t2 = q32[:, 2048:3072]
    t3 = q32[:, 3072:4096]
    seg = lambda ap: ap.rearrange("p (n c) -> p n c", c=32)
    vector.tensor_reduce(bm, seg(src_f32), AX.X, ALU.max, apply_absolute_value=True)
    vector.drain()
    vector.tensor_scalar(scl, bm, 1.0 / 7.5, 1e-30, ALU.mult, ALU.max)
    vector.drain()
    vector.reciprocal(rs, scl)
    vector.drain()
    vector.tensor_tensor(seg(aq), seg(src_f32),
                         rs.unsqueeze(2).broadcast_to([128, 32, 32]), ALU.mult)
    vector.tensor_tensor(t3, aq, aq, ALU.mult)
    vector.tensor_scalar(t1, t3, 4.0, None, ALU.is_ge)
    vector.tensor_scalar(t2, t3, 16.0, None, ALU.is_ge)
    vector.tensor_scalar(t3, t1, 0.125, 0.125, ALU.mult, ALU.add)
    vector.tensor_scalar(t2, t2, 1.0, None, ALU.add)
    vector.tensor_tensor(t2, t2, t3, ALU.mult)
    vector.tensor_tensor(t3, aq, aq, ALU.mult)
    vector.tensor_scalar(t3, t3, 16.0, None, ALU.is_ge)
    vector.tensor_scalar(t1, t1, -4.0, 8.0, ALU.mult, ALU.add)
    vector.tensor_scalar(t3, t3, -0.5, 1.0, ALU.mult, ALU.add)
    vector.tensor_tensor(t1, t1, t3, ALU.mult)
    vector.tensor_tensor(t1, aq, t1, ALU.mult)
    vector.tensor_scalar(t3, t1, MAGIC, MAGIC, ALU.add, ALU.subtract)
    vector.tensor_tensor(t1, t1, t3, ALU.subtract)
    vector.tensor_scalar(t1, t1, -0.5, None, ALU.is_equal)
    vector.tensor_tensor(t1, t3, t1, ALU.subtract)
    vector.tensor_tensor(t1, t1, t2, ALU.mult)
    ins = vector.tensor_tensor(seg(dst_bf), seg(t1),
                               scl.unsqueeze(2).broadcast_to([128, 32, 32]),
                               ALU.mult)
    return ins


def emit_store_x(p, xsrc, x_out, wait='vps'):
    def store(sync):
        p.wait(sync, wait)
        ins = sync.dma_start(out=x_out.rearrange("(t p) f -> p t f", p=128),
                             in_=xsrc.rearrange("p (t f) -> p t f", t=NT))
        p.inc(ins, 'st', 16)
    p.y(store)



def emit_l1_ffn(p, ar, x_in, x_out, w, first):
    xa, xb = ar['XA'], ar['XB']
    xc = ar['XCb']
    wdT = ar['WSb'][:, 0:8192]
    stats = ar['stats']
    q16, q32 = ar['q16'], ar['q32']

    emit_load_x(p, x_in, xa)
    b2 = hilo_bias(p, ar, [(w['L1_ffn_bias'], 1024)], 0)

    if first:
        def loadw(sync):
            ins = sync.dma_start(out=xb[:, 0:8192].rearrange("p (o f) -> p o f", o=8),
                                 in_=w['L1_ffn_weight'].rearrange(
                                     "(o p) f -> p o f", p=128))
            p.inc(ins, 'sd', 16)
        p.y(loadw)

        def quantw(vector):
            p.wait(vector, 'sd')
            p.wait(vector, 'gd')
            for o in range(8):
                ins = emit_quant_tile(p, vector, xb[:, 1024 * o:1024 * (o + 1)],
                                      xc[:, 1024 * o:1024 * (o + 1)], q16, q32)
            vector.drain()
            p.inc(ins, 've')
        p.v(quantw)

        def wtod(sync):
            p.wait(sync, 've')
            ins = sync.dma_start(
                out=ar['scr_w'].rearrange("(t p) f -> p t f", p=128),
                in_=xc[:, 0:8192].rearrange("p (t f) -> p t f", t=8))
            p.inc(ins, 'sd', 16)
        p.y(wtod)

    def wfrd(sync):
        p.wait(sync, 'sd')
        for c in range(8):
            ins = sync.dma_start_transpose(
                out=wdT[:, 1024 * c:1024 * (c + 1)],
                in_=ar['scr_w'][:, 128 * c:128 * (c + 1)])
            p.inc(ins, 'sd', 16)
    p.y(wfrd)

    emit_ln_stats(p, xa, ar['dump'], stats, 1e-5)
    emit_normalize(p, xa, stats, None, dst_f32=xb)

    def quantx(vector):
        p.wait(vector, 'se')
        p.wait(vector, 'gd')
        for t in range(NT):
            ins = emit_quant_tile(p, vector, xb[:, 1024 * t:1024 * (t + 1)],
                                  xc[:, 1024 * t:1024 * (t + 1)], q16, q32)
        vector.drain()
        p.inc(ins, 've')
    p.v(quantx)

    emit_tm_to_fm(p, xc[:, 0:16384], xc, ar['scr_tm'], wait_sem='ve')

    sd_ready = p.cnt['sd']
    ve_ready = p.cnt['ve']
    vps_base = p.cnt.get('vps', 0)

    gd_ready = p.cnt.get('gd', 0)

    def gate(tensor):
        p.wait(tensor, 'sd', sd_ready)
        p.wait(tensor, 've', ve_ready)
        p.wait(tensor, 'gd', gd_ready)
        p.wait(tensor, 'gp')
    p.t(gate)

    for t in range(NT):
        pb0 = ar['psf'][(t % 2) * 2 + 0]
        pb1 = ar['psf'][(t % 2) * 2 + 1]

        def mm_t(tensor, t=t, pb0=pb0, pb1=pb1):
            if t >= 2:
                p.wait(tensor, 'vps', vps_base + t - 1)
            for half, pb in ((0, pb0), (1, pb1)):
                bias_mm(p, tensor, pb, b2, 512 * half, 512 * (half + 1),
                        start=True, stop=False)
                for k in range(8):
                    ins = tensor.matmul(
                        pb,
                        xc[:, 2048 * k + 128 * t:2048 * k + 128 * (t + 1)],
                        wdT[:, 1024 * k + 512 * half:1024 * k + 512 * (half + 1)],
                        start=False, stop=(k == 7))
            p.inc(ins, 'pe')
        p.t(mm_t)

        def evac_t(vector, t=t, pb0=pb0, pb1=pb1):
            p.wait(vector, 'pe')
            vector.scalar_tensor_tensor(
                out=xb[:, 1024 * t:1024 * t + 512], in0=pb0, scalar=1.0,
                in1=xa[:, 1024 * t:1024 * t + 512], op0=ALU.mult, op1=ALU.add)
            ins = vector.scalar_tensor_tensor(
                out=xb[:, 1024 * t + 512:1024 * (t + 1)], in0=pb1, scalar=1.0,
                in1=xa[:, 1024 * t + 512:1024 * (t + 1)], op0=ALU.mult,
                op1=ALU.add)
            p.inc(ins, 'vps')
        p.v(evac_t)

    emit_store_x(p, xb, x_out)



def emit_l1_attn_b(p, ar, x_in, x_out, w):
    xa, xb = ar['XA'], ar['XB']
    xc = ar['XCb']
    stats = ar['stats']
    q32 = ar['q32']
    w1b = ar['WSb'][:, 0:8192]
    w23b = ar['w23']

    emit_load_x(p, x_in, xa)
    b1 = hilo_bias(p, ar, [(w['L1_attn_b_b1'], 1024)], 0)
    b23 = hilo_bias(p, ar, [(w['L1_attn_b_b2'], 16), (w['L1_attn_b_b3'], 16)], 1)

    def loadw1(sync):
        p.wait(sync, 'pe')
        ins = sync.dma_start(out=w1b.rearrange("p (k f) -> p k f", k=8),
                             in_=ar['cache']['L1_attn_b_W1'].rearrange(
                                 "(k p) f -> p k f", p=128))
        p.inc(ins, 'sd', 16)
    p.y(loadw1)

    def loadw23(gpsimd):
        ins = gpsimd.dma_start(
            out=w23b[:].rearrange("p (k a f) -> p k a f", k=8, a=2)[:, :, 0, :],
            in_=w['L1_attn_b_W2'].rearrange("(k p) f -> p k f", p=128))
        p.inc(ins, 'gd', 16)
        ins = gpsimd.dma_start(
            out=w23b[:].rearrange("p (k a f) -> p k a f", k=8, a=2)[:, :, 1, :],
            in_=w['L1_attn_b_W3'].rearrange("(k p) f -> p k f", p=128))
        p.inc(ins, 'gd', 16)
    p.g(loadw23)

    emit_ln_stats(p, xa, ar['dump'], stats, 1e-5)
    emit_normalize(p, xa, stats, xc[:, 0:16384], dst_f32=xb)
    emit_tm_to_fm(p, xc[:, 0:16384], xc, ar['scr_tm'], wait_sem='se')

    sd_ready = p.cnt['sd']
    gd_ready = p.cnt['gd']
    ve_ready = p.cnt['ve']
    vps_base = p.cnt.get('vps', 0)

    def gate(tensor):
        p.wait(tensor, 'sd', sd_ready)
        p.wait(tensor, 'gd', gd_ready)
        p.wait(tensor, 've', ve_ready)
        p.wait(tensor, 'gp')
    p.t(gate)

    for t in range(NT):
        pb0 = ar['psf'][(t % 2) * 2 + 0]
        pb1 = ar['psf'][(t % 2) * 2 + 1]
        pbc = ar['psb'][t % 2][:, 0:32]

        def mm_t(tensor, t=t, pb0=pb0, pb1=pb1, pbc=pbc):
            if t >= 2:
                p.wait(tensor, 'vps', vps_base + t - 1)
            for half, pb in ((0, pb0), (1, pb1)):
                bias_mm(p, tensor, pb, b1, 512 * half, 512 * (half + 1),
                        start=True, stop=False)
                for k in range(8):
                    tensor.matmul(
                        pb,
                        xc[:, 2048 * k + 128 * t:2048 * k + 128 * (t + 1)],
                        w1b[:, 1024 * k + 512 * half:1024 * k + 512 * (half + 1)],
                        start=False, stop=(k == 7))
            bias_mm(p, tensor, pbc, b23, 0, 32, start=True, stop=False)
            for k in range(8):
                ins = tensor.matmul(
                    pbc,
                    xc[:, 2048 * k + 128 * t:2048 * k + 128 * (t + 1)],
                    w23b[:, 32 * k:32 * (k + 1)],
                    start=False, stop=(k == 7))
            p.inc(ins, 'pe')
        p.t(mm_t)

        d0 = q32[:, 0:1024]
        d1 = q32[:, 1024:2048]
        d2 = q32[:, 2048:3072]
        rd = ar['q16'][:, 0:16]

        def evac_v(vector, t=t, pb0=pb0, pb1=pb1, pbc=pbc, d0=d0, d1=d1, rd=rd):
            p.wait(vector, 'pe')
            vector.tensor_copy(d0[:, 0:512], pb0)
            vector.tensor_copy(d0[:, 512:1024], pb1)
            vector.tensor_copy(d1[:, 0:32], pbc)
            vector.drain()
            ins = vector.scalar_tensor_tensor(
                out=d1[:, 32:48], in0=d1[:, 0:16], scalar=1.0,
                in1=d1[:, 16:32], op0=ALU.mult, op1=ALU.mult,
                accum_out=rd[:, 0:1])
            vector.drain()
            p.inc(ins, 've')
        p.v(evac_v)

        def absz(vector, d0=d0, d1=d1):
            vector.tensor_scalar(d1, d0, -1.0, None, ALU.mult)
            ins = vector.tensor_tensor(d1, d0, d1, ALU.max)
            p.inc(ins, 've')
        p.v(absz)

        def sp_a(scalar, d1=d1, d2=d2):
            p.wait(scalar, 've')
            scalar.activation(out=d2, in_=d1, func=AF.Exp, scale=-1.0)
            ins = scalar.activation(out=d2, in_=d2, func=AF.Ln, bias=1.0)
            p.inc(ins, 'se')
        p.s(sp_a)

        def fin_v(vector, t=t, d0=d0, d1=d1, d2=d2, rd=rd):
            p.wait(vector, 'se')
            if DEBUG_ATTNB and t == 0:
                vector.tensor_copy(ar['dump'], d0)
                vector.tensor_copy(stats[:, 0:1], rd[:, 0:1])
            vector.tensor_scalar(d0, d0, 0.0, None, ALU.max)
            vector.tensor_tensor(d0, d0, d2, ALU.add)
            vector.scalar_tensor_tensor(
                out=d0, in0=d0, scalar=rd[:, 0:1],
                in1=xb[:, 1024 * t:1024 * (t + 1)], op0=ALU.mult, op1=ALU.mult)
            ins = vector.tensor_tensor(
                xb[:, 1024 * t:1024 * (t + 1)], d0,
                xa[:, 1024 * t:1024 * (t + 1)], ALU.add)
            p.inc(ins, 'vps')
        p.v(fin_v)

    emit_store_x(p, xb, x_out)
    if DEBUG_ATTNB:
        def dbgst(sync):
            p.wait(sync, 'vps')
            ins = sync.dma_start(out=ar['dbg_out'][:, 0:1024], in_=ar['dump'])
            p.inc(ins, 'st', 16)
            ins = sync.dma_start(out=ar['dbg_out'][:, 1024:1056],
                                 in_=stats[:, 0:32])
            p.inc(ins, 'st', 16)
            ins = sync.dma_start(out=ar['dbg2_out'][:, 0:1024],
                                 in_=xb[:, 0:1024])
            p.inc(ins, 'st', 16)
            ins = sync.dma_start(out=ar['dbg3_out'], in_=xc[:, 0:2048])
            p.inc(ins, 'st', 16)
        p.y(dbgst)





FFN_SPECS = [
    ('L0_ffn_w1_w', 'L0_ffn_w1_s', 4096, 1024, 'cT_w1'),
    ('L0_ffn_w2_w', 'L0_ffn_w2_s', 4096, 1024, 'cT_w2'),
    ('L0_ffn_h1_w', 'L0_ffn_h1_s', 4096, 4096, 'cT_h1'),
    ('L0_ffn_h2_w', 'L0_ffn_h2_s', 4096, 4096, 'cT_h2'),
    ('L0_ffn_w3_w', 'L0_ffn_w3_s', 1024, 4096, 'cT_w3'),
]


def emit_scale_load(p, ar, dram_vec, n, dst_cols):
    nch = n // 128
    tmp = ar['s32']
    dst = ar['ssb'][:, dst_cols:dst_cols + nch]

    def ld(gpsimd):
        p.wait(gpsimd, 'pe')
        ins = gpsimd.dma_start(out=tmp[0:nch, :],
                               in_=dram_vec.rearrange("(c q) -> c q", q=128))
        p.inc(ins, 'gd', 16)
    p.g(ld)



    def tr(tensor):
        p.wait(tensor, 'gd')
        p.wait(tensor, 've')
        ins = tensor.transpose(ar['pst'][:, 0:nch], tmp[0:nch, :],
                               ar['ident32'][0:nch, 0:nch])
        p.inc(ins, 'pe')
    p.t(tr)

    def ev(vector):
        p.wait(vector, 'pe')
        ins = vector.tensor_copy(dst, ar['pst'][:, 0:nch])
        p.inc(ins, 've')
    p.v(ev)


def emit_ffn_cache_build(p, ar, w):
    wsb = ar['WSb']
    for wi, (wname, sname, O, K, cname) in enumerate(FFN_SPECS):
        cT = ar[cname]
        craw = ar['c_raw']
        is_w3 = (wname == 'L0_ffn_w3_w')
        for oc in range(O // 128):
            buf = wsb[:, (oc % 2) * 4096:(oc % 2) * 4096 + K]

            def cast(gpsimd, oc=oc, buf=buf, wname=wname, K=K):
                p.wait(gpsimd, 'sd')
                ins = gpsimd.dma_start(
                    out=buf, in_=w[wname][128 * oc:128 * (oc + 1), 0:K])
                p.inc(ins, 'gd', 16)
            p.g(cast)

            if is_w3:
                def scale(vector, oc=oc, buf=buf):
                    p.wait(vector, 'gd')
                    ins = vector.tensor_scalar(
                        buf, buf, ar['ssb'][:, 128 + oc:129 + oc], None, ALU.mult)
                    p.inc(ins, 've')
                p.v(scale)

            def wr(sync, oc=oc, buf=buf, craw=craw, K=K, is_w3=is_w3):
                p.wait(sync, 'gd')
                if is_w3:
                    p.wait(sync, 've')
                ins = sync.dma_start(out=craw[128 * oc:128 * (oc + 1), 0:K],
                                     in_=buf)
                p.inc(ins, 'sd', 16)
            p.y(wr)
        for kc in range(K // 128):
            buf = wsb[:, (kc % 2) * 4096:(kc % 2) * 4096 + O]

            def tr(sync, kc=kc, buf=buf, craw=craw, O=O):
                p.wait(sync, 'sd')
                ins = sync.dma_start_transpose(
                    out=buf, in_=craw[0:O, 128 * kc:128 * (kc + 1)])
                p.inc(ins, 'sd', 16)
                return ins
            p.y(tr)

            def wr2(sync, kc=kc, buf=buf, cT=cT, O=O):
                p.wait(sync, 'sd')
                ins = sync.dma_start(out=cT[128 * kc:128 * (kc + 1), 0:O],
                                     in_=buf)
                p.inc(ins, 'sd', 16)
            p.y(wr2)


def emit_l0_ffn(p, ar, x_in, x_out, w, first):
    xa, xb = ar['XA'], ar['XB']
    xab, xbb = ar['XAb'], ar['XBb']
    xc = ar['XCb']
    wsb = ar['WSb']
    stats = ar['stats']
    q32 = ar['q32']
    ssb = ar['ssb']

    emit_scale_load(p, ar, w['L0_ffn_w1_s'], 4096, 0)
    emit_scale_load(p, ar, w['L0_ffn_w2_s'], 4096, 32)
    emit_scale_load(p, ar, w['L0_ffn_h1_s'], 4096, 64)
    emit_scale_load(p, ar, w['L0_ffn_h2_s'], 4096, 96)
    emit_scale_load(p, ar, w['L0_ffn_w3_s'], 1024, 128)

    if first:
        emit_ffn_cache_build(p, ar, w)

    emit_load_x(p, x_in, xa)
    emit_ln_stats(p, xa, ar['dump'], stats, 1e-5)
    emit_normalize(p, xa, stats, xc[:, 0:16384])
    emit_tm_to_fm(p, xc[:, 0:16384], xc, ar['scr_tm'], wait_sem='se')

    ve_ready = p.cnt['ve']
    vps_base = p.cnt.get('vps', 0)

    for tb in range(2):
        t0 = 1024 * tb
        for og in range(8):
            o0 = 512 * og

            def lda(sync, o0=o0):
                p.wait(sync, 'pe')
                p.wait(sync, 'sd')
                p.wait(sync, 'vps')
                p.wait(sync, 'st')
                ins = sync.dma_start(
                    out=wsb[:, 0:4096].rearrange("p (k o) -> p k o", k=8),
                    in_=ar['cT_w1'].rearrange("(k p) o -> p k o", p=128)[
                        :, :, o0:o0 + 512])
                p.inc(ins, 'sd', 16)
                ins = sync.dma_start(
                    out=wsb[:, 4096:8192].rearrange("p (k o) -> p k o", k=8),
                    in_=ar['cT_w2'].rearrange("(k p) o -> p k o", p=128)[
                        :, :, o0:o0 + 512])
                p.inc(ins, 'sd', 16)
            p.y(lda)

            for sub in range(8):
                oc, g = divmod(sub, 2)
                oabs = 4 * og + oc
                par = sub % 2
                ps1, ps2 = ar['psf'][par * 2], ar['psf'][par * 2 + 1]

                def mma(tensor, oc=oc, g=g, ps1=ps1, ps2=ps2, sub=sub, t0=t0):
                    p.wait(tensor, 'sd')
                    v0 = p.cnt.get('vps', 0)
                    p.wait(tensor, 'vps',
                           max(vps_base, v0 - 1) if v0 > vps_base else 0)
                    for wsl, ps in ((0, ps1), (4096, ps2)):
                        for k in range(8):
                            ins = tensor.matmul(
                                ps,
                                wsb[:, wsl + 512 * k + 128 * oc:
                                    wsl + 512 * k + 128 * (oc + 1)],
                                xc[:, 2048 * k + t0 + 512 * g:
                                    2048 * k + t0 + 512 * (g + 1)],
                                start=(k == 0), stop=(k == 7))
                    p.inc(ins, 'pe')
                p.t(mma)

                def eva_s(scalar, ps1=ps1, oabs=oabs):
                    p.wait(scalar, 'pe')
                    p.wait(scalar, 'vps')
                    ins = scalar.activation(out=ar['u512'], in_=ps1,
                                            func=AF.Silu,
                                            scale=ssb[:, oabs:oabs + 1])
                    p.inc(ins, 'se')
                p.s(eva_s)

                def eva_v(vector, ps2=ps2, oabs=oabs, g=g):
                    p.wait(vector, 'se')
                    ins = vector.scalar_tensor_tensor(
                        out=xbb[:, 1024 * oabs + 512 * g:
                                1024 * oabs + 512 * (g + 1)],
                        in0=ps2, scalar=ssb[:, 32 + oabs:33 + oabs],
                        in1=ar['u512'], op0=ALU.mult, op1=ALU.mult)
                    p.inc(ins, 'vps')
                p.v(eva_v)

        for stage, (hname, scol, src, dst) in enumerate(
                [('cT_h1', 64, xbb, xab), ('cT_h2', 96, xab, xbb)]):
            for og in range(16):
                o0 = 256 * og

                def ldb(sync, o0=o0, hname=hname):
                    p.wait(sync, 'pe')
                    p.wait(sync, 'vps')
                    p.wait(sync, 'sd')
                    ins = sync.dma_start(
                        out=wsb[:, 0:8192].rearrange("p (k o) -> p k o", k=32),
                        in_=ar[hname].rearrange("(k p) o -> p k o", p=128)[
                            :, :, o0:o0 + 256])
                    p.inc(ins, 'sd', 16)
                p.y(ldb)

                for sub in range(4):
                    oc, g = divmod(sub, 2)
                    oabs = 2 * og + oc
                    ps = ar['psf'][sub]

                    def mmb(tensor, oc=oc, g=g, ps=ps, src=src):
                        p.wait(tensor, 'sd')
                        p.wait(tensor, 'se')
                        for k in range(32):
                            ins = tensor.matmul(
                                ps,
                                wsb[:, 256 * k + 128 * oc:256 * k + 128 * (oc + 1)],
                                src[:, 1024 * k + 512 * g:
                                    1024 * k + 512 * (g + 1)],
                                start=(k == 0), stop=(k == 31))
                        p.inc(ins, 'pe')
                    p.t(mmb)

                    def evb(scalar, ps=ps, oabs=oabs, g=g, dst=dst, scol=scol):
                        p.wait(scalar, 'pe')
                        ins = scalar.activation(
                            out=dst[:, 1024 * oabs + 512 * g:
                                    1024 * oabs + 512 * (g + 1)],
                            in_=ps, func=AF.Gelu,
                            scale=ssb[:, scol + oabs:scol + oabs + 1])
                        p.inc(ins, 'se')
                    p.s(evb)

        for th in range(2):
            tt0 = t0 + 512 * th

            def ldx(sync, tt0=tt0):
                p.wait(sync, 'vps')
                p.wait(sync, 've')
                p.wait(sync, 'st')
                p.wait(sync, 'gd')
                ins = sync.dma_start(
                    out=q32.rearrange("p (t f) -> p t f", t=4),
                    in_=x_in[tt0:tt0 + 512, :].rearrange(
                        "(t p) f -> p t f", p=128))
                p.inc(ins, 'sd', 16)
            p.y(ldx)

            for kg in range(4):
                def ldw3(sync, kg=kg):
                    p.wait(sync, 'pe')
                    p.wait(sync, 'sd')
                    if kg == 0:
                        p.wait(sync, 'se')
                    ins = sync.dma_start(
                        out=wsb[:, 0:8192].rearrange("p (k o) -> p k o", k=8),
                        in_=ar['cT_w3'].rearrange("(k p) o -> p k o", p=128)[
                            :, 8 * kg:8 * kg + 8, :])
                    p.inc(ins, 'sd', 16)
                p.y(ldw3)

                def mmd(tensor, kg=kg, tt0=tt0, t0=t0):
                    p.wait(tensor, 'sd')
                    if kg == 0:
                        p.wait(tensor, 'vps')
                    for tt in range(4):
                        for half in range(2):
                            ps = ar['psf'][tt] if half == 0 else ar['psb'][tt]
                            for k8 in range(8):
                                k = 8 * kg + k8
                                ins = tensor.matmul(
                                    ps,
                                    xbb[:, 1024 * k + (tt0 - t0) + 128 * tt:
                                        1024 * k + (tt0 - t0) + 128 * (tt + 1)],
                                    wsb[:, 1024 * k8 + 512 * half:
                                        1024 * k8 + 512 * (half + 1)],
                                    start=(kg == 0 and k8 == 0),
                                    stop=(kg == 3 and k8 == 7))
                    p.inc(ins, 'pe')
                p.t(mmd)

            def evd(vector, tt0=tt0):
                p.wait(vector, 'pe')
                for tt in range(4):
                    for half in range(2):
                        ps = ar['psf'][tt] if half == 0 else ar['psb'][tt]
                        ins = vector.scalar_tensor_tensor(
                            out=q32[:, 1024 * tt + 512 * half:
                                    1024 * tt + 512 * (half + 1)],
                            in0=ps, scalar=1.0,
                            in1=q32[:, 1024 * tt + 512 * half:
                                    1024 * tt + 512 * (half + 1)],
                            op0=ALU.mult, op1=ALU.add)
                p.inc(ins, 'vps')
            p.v(evd)

            def std(sync, tt0=tt0):
                p.wait(sync, 'vps')
                ins = sync.dma_start(
                    out=x_out[tt0:tt0 + 512, :].rearrange(
                        "(t p) f -> p t f", p=128),
                    in_=q32.rearrange("p (t f) -> p t f", t=4))
                p.inc(ins, 'st', 16)
            p.y(std)





def emit_l0_attn_b(p, ar, x_in, x_out, w):
    xa, xb = ar['XA'], ar['XB']
    xab, xbb = ar['XAb'], ar['XBb']
    xc = ar['XCb']
    wsb = ar['WSb']
    stats = ar['stats']
    q32 = ar['q32']
    q32b = ar['q32'].bitcast(BF16)
    ssb = ar['ssb']
    u512 = ar['u512']

    emit_scale_load(p, ar, w['L0_attn_b_gamma1'], 1024, 0)
    emit_scale_load(p, ar, w['L0_attn_b_beta1'], 1024, 8)
    emit_scale_load(p, ar, w['L0_attn_b_b1'], 4096, 16)
    b2 = hilo_bias(p, ar, [(w['L0_attn_b_b2'], 1024)], 0)

    emit_load_x(p, x_in, xa)
    emit_ln_stats(p, xa, ar['dump'], stats, 1e-5, double_eps=1e-6)
    emit_normalize(p, xa, stats, xc[:, 0:16384])
    emit_tm_to_fm(p, xc[:, 0:16384], xc, ar['scr_tm'], wait_sem='se')

    def affine(vector):
        p.wait(vector, 'sd')
        for c in range(NCH):
            ins = vector.tensor_scalar(
                xc[:, 2048 * c:2048 * (c + 1)], xc[:, 2048 * c:2048 * (c + 1)],
                ssb[:, c:c + 1], ssb[:, 8 + c:9 + c], ALU.mult, ALU.add)
        p.inc(ins, 've')
    p.v(affine)


    for wi, wname in enumerate(['L0_attn_b_wk', 'L0_attn_b_wv']):
        def ldw(sync, wname=wname):
            p.wait(sync, 'pe')
            ins = sync.dma_start(
                out=wsb[:, 0:8192].rearrange("p (k o) -> p k o", k=8),
                in_=ar['cache'][wname].rearrange("(k p) o -> p k o", p=128))
            p.inc(ins, 'sd', 16)
        p.y(ldw)

        for t in range(NT):
            ps0, ps1 = ar['psf'][(t % 2) * 2], ar['psf'][(t % 2) * 2 + 1]

            def mmp(tensor, t=t, ps0=ps0, ps1=ps1, wi=wi):
                p.wait(tensor, 'gd')
                p.wait(tensor, 'sd')
                p.wait(tensor, 've')
                if not (wi == 0 and t < 2):
                    p.wait(tensor, 'vps', p.cnt['vps'] - 1)
                for half, ps in ((0, ps0), (1, ps1)):
                    for k in range(8):
                        ins = tensor.matmul(
                            ps, xc[:, 2048 * k + 128 * t:2048 * k + 128 * (t + 1)],
                            wsb[:, 1024 * k + 512 * half:1024 * k + 512 * (half + 1)],
                            start=(k == 0), stop=(k == 7))
                p.inc(ins, 'pe')
            p.t(mmp)

            if wi == 0:
                def evk(vector, t=t, ps0=ps0, ps1=ps1):
                    p.wait(vector, 'pe')
                    vector.tensor_scalar(xbb[:, 1024 * t:1024 * t + 512],
                                         ps0, 0.0, 0.001, ALU.max, ALU.add)
                    ins = vector.tensor_scalar(
                        xbb[:, 1024 * t + 512:1024 * (t + 1)],
                        ps1, 0.0, 0.001, ALU.max, ALU.add)
                    p.inc(ins, 'vps')
                p.v(evk)
            else:
                def evv(vector, t=t, ps0=ps0, ps1=ps1):
                    p.wait(vector, 'pe')
                    vector.tensor_copy(xbb[:, 16384 + 1024 * t:
                                           16384 + 1024 * t + 512], ps0)
                    ins = vector.tensor_copy(
                        xbb[:, 16384 + 1024 * t + 512:16384 + 1024 * (t + 1)],
                        ps1)
                    p.inc(ins, 'vps')
                p.v(evv)

    if S2_PHASES == 2:
        emit_store_x(p, xb, x_out, wait='vps')
        return
    kvs = q32b[0:64, 0:1040]
    klist = []
    for h in range(H):
        psk = ar['psb'][1 + (h % 2)][0:64, 0:65]

        def mmk(tensor, h=h, psk=psk):
            if h < 2:
                p.wait(tensor, 'vps')
                p.wait(tensor, 'gp')
            if h >= 2:
                p.wait(tensor, 've', klist[h - 2])
            for t in range(NT):
                tensor.matmul(
                    psk[:, 0:64],
                    xbb[:, 1024 * t + 64 * h:1024 * t + 64 * (h + 1)],
                    xbb[:, 16384 + 1024 * t + 64 * h:16384 + 1024 * t + 64 * (h + 1)],
                    start=(t == 0), stop=False, skip_group_check=True)
                ins = tensor.matmul(
                    psk[:, 64:65],
                    xbb[:, 1024 * t + 64 * h:1024 * t + 64 * (h + 1)],
                    p.ones128,
                    start=False, stop=(t == NT - 1), skip_group_check=True)
            p.inc(ins, 'pe')
        p.t(mmk)

        def evkv(vector, h=h, psk=psk):
            p.wait(vector, 'pe')
            ins = vector.tensor_copy(kvs[:, 65 * h:65 * (h + 1)], psk)
            p.inc(ins, 've')
        p.v(evkv)
        klist.append(p.cnt['ve'])

    if S2_PHASES == 25:
        def gate25(vector):
            p.wait(vector, 've')
            ins = vector.tensor_copy(xb[0:64, 0:520], kvs[0:64, 0:520])
            p.inc(ins, 'vps')
        p.v(gate25)
        emit_store_x(p, xb, x_out, wait='vps')
        return
    def ldwq(sync):
        p.wait(sync, 'pe')
        p.wait(sync, 'vps')
        ins = sync.dma_start(
            out=wsb[:, 0:8192].rearrange("p (k o) -> p k o", k=8),
            in_=ar['cache']['L0_attn_b_wq'].rearrange("(k p) o -> p k o", p=128))
        p.inc(ins, 'sd', 16)
    p.y(ldwq)

    qpf = q32b[0:64, 1664:3712]
    for t in range(NT):
        ps0, ps1 = ar['psf'][0], ar['psf'][1]
        pstA = ar['psf'][2][:, 0:512].bitcast(BF16)
        pstB = ar['psb'][0][:, 0:512].bitcast(BF16)

        def mmq(tensor, t=t, ps0=ps0, ps1=ps1):
            p.wait(tensor, 'gd')
            p.wait(tensor, 'sd')
            p.wait(tensor, 've')
            for half, ps in ((0, ps0), (1, ps1)):
                for k in range(8):
                    ins = tensor.matmul(
                        ps, xc[:, 2048 * k + 128 * t:2048 * k + 128 * (t + 1)],
                        wsb[:, 1024 * k + 512 * half:1024 * k + 512 * (half + 1)],
                        start=(k == 0), stop=(k == 7))
            p.inc(ins, 'pe')
        p.t(mmq)

        def evq(vector, t=t, ps0=ps0, ps1=ps1):
            p.wait(vector, 'pe')
            vector.tensor_scalar(xbb[:, 1024 * t:1024 * t + 512],
                                 ps0, 0.0, 0.001, ALU.max, ALU.add)
            ins = vector.tensor_scalar(
                xbb[:, 1024 * t + 512:1024 * (t + 1)],
                ps1, 0.0, 0.001, ALU.max, ALU.add)
            p.inc(ins, 've')
        p.v(evq)

        if S2_PHASES == 3.1:
            continue

        def trq(tensor, t=t, pstA=pstA, pstB=pstB):
            p.wait(tensor, 've')
            for h in range(H):
                pst = pstA if h < 8 else pstB
                ins = tensor.transpose(
                    pst[0:64, 128 * (h % 8):128 * (h % 8 + 1)],
                    xbb[:, 1024 * t + 64 * h:1024 * t + 64 * (h + 1)],
                    ar['ident'])
            p.inc(ins, 'pe')
        p.t(trq)

        def evt(vector, t=t, pstA=pstA, pstB=pstB):
            p.wait(vector, 'pe')
            vector.tensor_copy(qpf[:, 0:1024], pstA[0:64, :])
            ins = vector.tensor_copy(qpf[:, 1024:2048], pstB[0:64, :])
            p.inc(ins, 've')
        p.v(evt)

        if S2_PHASES == 3.2:
            continue
        for hg in range(4):
            psn = ar['psb'][3][:, 0:264] if hg % 2 == 0 else ar['psf'][3][:, 0:264]

            def mmn(tensor, t=t, hg=hg, psn=psn):
                p.wait(tensor, 've')
                for hh in range(4):
                    h = 4 * hg + hh
                    ins = tensor.matmul(
                        psn[:, 66 * hh:66 * hh + 65],
                        qpf[:, 128 * h:128 * (h + 1)],
                        kvs[:, 65 * h:65 * (h + 1)],
                        start=(hh == 0), stop=(hh == 3), skip_group_check=True)
                p.inc(ins, 'pe')
            p.t(mmn)

            def evn(vector, t=t, hg=hg, psn=psn):
                p.wait(vector, 'pe')
                if S2_PHASES == 3.3:
                    ins = vector.tensor_copy(u512[:, 0:64], u512[:, 64:128])
                    p.inc(ins, 've')
                    return
                if S2_PHASES == 3.4:
                    for hh in range(4):
                        h = 4 * hg + hh
                        ins = vector.tensor_scalar(
                            xbb[:, 1024 * t + 64 * h:1024 * t + 64 * (h + 1)],
                            psn[:, 66 * hh:66 * hh + 64],
                            1.0, None, ALU.mult)
                    p.inc(ins, 've')
                    return
                for hh in range(4):
                    h = 4 * hg + hh
                    vector.reciprocal(u512[:, h:h + 1],
                                      psn[:, 66 * hh + 64:66 * hh + 65])
                vector.drain()
                for hh in range(4):
                    h = 4 * hg + hh
                    ins = vector.tensor_scalar(
                        xbb[:, 1024 * t + 64 * h:1024 * t + 64 * (h + 1)],
                        psn[:, 66 * hh:66 * hh + 64],
                        u512[:, h:h + 1], None, ALU.mult)
                p.inc(ins, 've')
            p.v(evn)

    if S2_PHASES in (3, 3.1, 3.2, 3.3, 3.4):
        def gate3(vector):
            p.wait(vector, 've')
            ins = vector.tensor_copy(xb[:, 0:1024], xab[:, 0:1024])
            p.inc(ins, 'vps')
        p.v(gate3)
        emit_store_x(p, xb, x_out, wait='vps')
        return

    def ldwo(sync):
        p.wait(sync, 'pe')
        p.wait(sync, 've')
        ins = sync.dma_start(
            out=wsb[:, 0:8192].rearrange("p (k o) -> p k o", k=8),
            in_=ar['cache']['L0_attn_b_wo'].rearrange("(k p) o -> p k o", p=128))
        p.inc(ins, 'sd', 16)
    p.y(ldwo)

    attf = q32b[:, 1664:2688]
    for t in range(NT):
        pst = ar['psf'][2][:, 0:512].bitcast(BF16)

        def tra(tensor, t=t, pst=pst):
            p.wait(tensor, 've')
            p.wait(tensor, 'gd')
            for pr in range(8):
                ins = tensor.transpose(
                    pst[:, 128 * pr:128 * (pr + 1)],
                    xbb[:, 1024 * t + 128 * pr:1024 * t + 128 * (pr + 1)],
                    ar['ident'])
            p.inc(ins, 'pe')
        p.t(tra)

        def eva(vector, t=t, pst=pst):
            p.wait(vector, 'pe')
            ins = vector.tensor_copy(attf, pst)
            p.inc(ins, 've')
        p.v(eva)

        ps0, ps1 = ar['psf'][0], ar['psf'][1]

        def mmo(tensor, t=t, ps0=ps0, ps1=ps1):
            p.wait(tensor, 've')
            p.wait(tensor, 'sd')
            if t >= 1:
                p.wait(tensor, 'vps', p.cnt['vps'] - 0)
            for half, ps in ((0, ps0), (1, ps1)):
                for k in range(8):
                    ins = tensor.matmul(
                        ps, attf[:, 128 * k:128 * (k + 1)],
                        wsb[:, 1024 * k + 512 * half:1024 * k + 512 * (half + 1)],
                        start=(k == 0), stop=(k == 7))
            p.inc(ins, 'pe')
        p.t(mmo)

        def evo(vector, t=t, ps0=ps0, ps1=ps1):
            p.wait(vector, 'pe')
            vector.tensor_scalar(xa[:, 1024 * t:1024 * t + 512],
                                 xa[:, 1024 * t:1024 * t + 512],
                                 stats[:, 64 + t:65 + t], stats[:, 80 + t:81 + t],
                                 ALU.mult, ALU.add)
            vector.tensor_scalar(xa[:, 1024 * t + 512:1024 * (t + 1)],
                                 xa[:, 1024 * t + 512:1024 * (t + 1)],
                                 stats[:, 64 + t:65 + t], stats[:, 80 + t:81 + t],
                                 ALU.mult, ALU.add)
            vector.tensor_tensor(xa[:, 1024 * t:1024 * t + 512],
                                 xa[:, 1024 * t:1024 * t + 512], ps0, ALU.add)
            ins = vector.tensor_tensor(
                xa[:, 1024 * t + 512:1024 * (t + 1)],
                xa[:, 1024 * t + 512:1024 * (t + 1)], ps1, ALU.add)
            p.inc(ins, 'vps')
        p.v(evo)

    if S2_PHASES <= 4.5:
        emit_store_x(p, xb, x_out, wait='vps')
        return
    def gate5(scalar):
        p.wait(scalar, 'vps')
    p.s(gate5)
    emit_ln_stats(p, xa, ar['dump'], stats, 1e-6)
    emit_normalize(p, xa, stats, xc[:, 0:16384])
    emit_tm_to_fm(p, xc[:, 0:16384], xc, ar['scr_tm'], wait_sem='se')

    def affine2(vector):
        p.wait(vector, 'sd')
        for c in range(NCH):
            ins = vector.tensor_scalar(
                xc[:, 2048 * c:2048 * (c + 1)], xc[:, 2048 * c:2048 * (c + 1)],
                ssb[:, c:c + 1], ssb[:, 8 + c:9 + c], ALU.mult, ALU.add)
        p.inc(ins, 've')
    p.v(affine2)

    if S2_PHASES <= 5.5:
        emit_store_x(p, xb, x_out, wait='vps')
        return
    for tb in range(2):
        t0 = 1024 * tb
        for og in range(8):
            o0 = 512 * og

            def ldw1(sync, o0=o0):
                p.wait(sync, 'pe')
                p.wait(sync, 've')
                p.wait(sync, 'se')
                p.wait(sync, 'vps')
                ins = sync.dma_start(
                    out=wsb[:, 0:4096].rearrange("p (k o) -> p k o", k=8),
                    in_=ar['cache']['L0_attn_b_w1'].rearrange(
                        "(k p) o -> p k o", p=128)[:, :, o0:o0 + 512])
                p.inc(ins, 'sd', 16)
            p.y(ldw1)

            for sub in range(8):
                oc, g = divmod(sub, 2)
                oabs = 4 * og + oc
                ps = ar['psf'][sub % 4]

                def mm1(tensor, oc=oc, g=g, ps=ps, t0=t0):
                    p.wait(tensor, 'gd')
                    p.wait(tensor, 'sd')
                    p.wait(tensor, 'se')
                    for k in range(8):
                        ins = tensor.matmul(
                            ps,
                            wsb[:, 512 * k + 128 * oc:512 * k + 128 * (oc + 1)],
                            xc[:, 2048 * k + t0 + 512 * g:
                                2048 * k + t0 + 512 * (g + 1)],
                            start=(k == 0), stop=(k == 7))
                    p.inc(ins, 'pe')
                p.t(mm1)

                def ev1(scalar, ps=ps, oabs=oabs, g=g):
                    p.wait(scalar, 'pe')
                    ins = scalar.activation(
                        out=xbb[:, 1024 * oabs + 512 * g:
                                1024 * oabs + 512 * (g + 1)],
                        in_=ps, func=AF.Gelu,
                        scale=1.0, bias=ssb[:, 16 + oabs:17 + oabs])
                    p.inc(ins, 'se')
                p.s(ev1)

        if S2_PHASES <= 6:
            continue
        for th in range(2):
            tt0 = t0 + 512 * th

            def ldx2(sync, tt0=tt0):
                if S2_V7 == 15:
                    p.wait(sync, 'sd')
                if S2_V7 not in (10, 15):
                    p.wait(sync, 'vps')
                if S2_V7 not in (10, 11, 15):
                    p.wait(sync, 've')
                if S2_V7 not in (10, 11, 12, 15):
                    p.wait(sync, 'st')
                if S2_V7 == 9:
                    return
                dst = ar['XB'][:, 0:4096] if S2_V7 == 7 else q32
                ins = sync.dma_start(
                    out=dst.rearrange("p (t f) -> p t f", t=4),
                    in_=x_in[tt0:tt0 + 512, :].rearrange(
                        "(t p) f -> p t f", p=128))
                p.inc(ins, 'sd', 16)
            p.y(ldx2)

            if S2_V7 >= 6:
                def st2b(sync, tt0=tt0):
                    p.wait(sync, 'sd')
                    src = q32
                    if S2_V7 in (8, 10, 11, 12, 15):
                        pass
                    else:
                        ins = sync.dma_start(
                            out=x_out[tt0:tt0 + 512, :].rearrange(
                                "(t p) f -> p t f", p=128),
                            in_=src.rearrange("p (t f) -> p t f", t=4))
                        p.inc(ins, 'st', 16)
                p.y(st2b)
                continue
            for kg in range(4):
                def ldw2(sync, kg=kg):
                    p.wait(sync, 'pe')
                    p.wait(sync, 'sd')
                    if kg == 0:
                        p.wait(sync, 'se')
                    ins = sync.dma_start(
                        out=wsb[:, 0:8192].rearrange("p (k o) -> p k o", k=8),
                        in_=ar['cache']['L0_attn_b_w2'].rearrange(
                            "(k p) o -> p k o", p=128)[:, 8 * kg:8 * kg + 8, :])
                    p.inc(ins, 'sd', 16)
                p.y(ldw2)

                def mm2(tensor, kg=kg, tt0=tt0, t0=t0):
                    p.wait(tensor, 'gd')
                    p.wait(tensor, 'sd')
                    if kg == 0:
                        p.wait(tensor, 'vps')
                        p.wait(tensor, 'se')
                    if S2_V7 == 5:
                        ps = ar['psf'][0]
                        ins = tensor.matmul(
                            ps, xab[:, 0:128], wsb[:, 0:512],
                            start=True, stop=(kg == 3))
                        p.inc(ins, 'pe')
                        return
                    for tt in range(4):
                        for half in range(2):
                            ps = ar['psf'][tt] if half == 0 else ar['psb'][tt]
                            for k8 in range(8):
                                k = 8 * kg + k8
                                ins = tensor.matmul(
                                    ps,
                                    xbb[:, 1024 * k + (tt0 - t0) + 128 * tt:
                                        1024 * k + (tt0 - t0) + 128 * (tt + 1)],
                                    wsb[:, 1024 * k8 + 512 * half:
                                        1024 * k8 + 512 * (half + 1)],
                                    start=(kg == 0 and k8 == 0),
                                    stop=(S2_V7 == 1 and kg == 3 and k8 == 7))
                    p.inc(ins, 'pe')
                p.t(mm2)

            if S2_V7 != 1:
                def mmb2(tensor, tt0=tt0):
                    for tt in range(4):
                        for half in range(2):
                            ps = ar['psf'][tt] if half == 0 else ar['psb'][tt]
                            ins = tensor.matmul(
                                ps, p.ones2[:, 0:128],
                                b2[:, 512 * half:512 * (half + 1)],
                                start=False, stop=True)
                    p.inc(ins, 'pe')
                p.t(mmb2)

            for rnd in range(4):
                def evg(scalar, rnd=rnd):
                    p.wait(scalar, 'pe')
                    if rnd >= 1:
                        p.wait(scalar, 'vps')
                    for j in range(2):
                        q = 2 * rnd + j
                        tt, half = divmod(q, 2)
                        ps = ar['psf'][tt] if half == 0 else ar['psb'][tt]
                        if S2_V7 == 2:
                            ins = scalar.copy(
                                ar['g512'][:, 512 * j:512 * (j + 1)], ps)
                        else:
                            ins = scalar.activation(
                                out=ar['g512'][:, 512 * j:512 * (j + 1)],
                                in_=ps, func=AF.Gelu)
                    p.inc(ins, 'se')
                p.s(evg)

                def evs(vector, rnd=rnd, tt0=tt0):
                    p.wait(vector, 'se')
                    for j in range(2):
                        q = 2 * rnd + j
                        col = 512 * q
                        if S2_V7 == 3:
                            ins2 = vector.tensor_copy(
                                q32[:, col:col + 512],
                                ar['g512'][:, 512 * j:512 * (j + 1)])
                            continue
                        vector.tensor_tensor(
                            q32[:, col:col + 512], q32[:, col:col + 512],
                            ar['g512'][:, 512 * j:512 * (j + 1)], ALU.add)
                        xbase = (tt0 // 128) * 1024
                        ins2 = vector.tensor_tensor(
                            q32[:, col:col + 512], q32[:, col:col + 512],
                            xa[:, xbase + col:xbase + col + 512], ALU.add)
                    p.inc(ins2, 'vps')
                p.v(evs)

            def st2(sync, tt0=tt0):
                p.wait(sync, 'vps')
                ins = sync.dma_start(
                    out=x_out[tt0:tt0 + 512, :].rearrange(
                        "(t p) f -> p t f", p=128),
                    in_=q32.rearrange("p (t f) -> p t f", t=4))
                p.inc(ins, 'st', 16)
            p.y(st2)






def emit_l1_attn_a(p, ar, x_in, x_out, w):
    xa = ar['XA']
    xab, xbb = ar['XAb'], ar['XBb']
    xcb = ar['XCb']
    wsb = ar['WSb']
    stats = ar['stats']
    q32 = ar['q32']
    q32b = ar['q32'].bitcast(BF16)
    ssb = ar['ssb']
    u512 = ar['u512']
    NEG = -10000.0

    emit_scale_load(p, ar, w['L1_attn_a_Wqkv_b'][0:1024], 1024, 0)
    emit_scale_load(p, ar, w['L1_attn_a_Wqkv_b'][1024:2048], 1024, 8)
    bv = hilo_bias(p, ar, [(w['L1_attn_a_Wqkv_b'][2048:3072], 1024)], 0)
    bo = hilo_bias(p, ar, [(w['L1_attn_a_out_b'], 1024)], 1)

    def scale_kbias(vector):
        p.wait(vector, 've')
        ins = vector.tensor_scalar(ssb[:, 8:16], ssb[:, 8:16], 0.125, None,
                                   ALU.mult)
        p.inc(ins, 've')
    p.v(scale_kbias)

    wsrc = w['L1_attn_a_Wqkv_w']
    for oc in range(24):
        buf = wsb[:, (oc % 2) * 4096:(oc % 2) * 4096 + 1024]

        def cast(gpsimd, oc=oc, buf=buf):
            p.wait(gpsimd, 'sd')
            ins = gpsimd.dma_start(out=buf,
                                   in_=wsrc[128 * oc:128 * (oc + 1), :])
            p.inc(ins, 'gd', 16)
        p.g(cast)

        def wr(sync, oc=oc, buf=buf):
            p.wait(sync, 'gd')
            ins = sync.dma_start(
                out=ar['c_raw'][128 * oc:128 * (oc + 1), 0:1024], in_=buf)
            p.inc(ins, 'sd', 16)
        p.y(wr)
    for kc in range(8):
        buf = wsb[:, (kc % 2) * 4096:(kc % 2) * 4096 + 3072]

        def tr(sync, kc=kc, buf=buf):
            p.wait(sync, 'sd')
            ins = sync.dma_start_transpose(
                out=buf, in_=ar['c_raw'][0:3072, 128 * kc:128 * (kc + 1)])
            p.inc(ins, 'sd', 16)
        p.y(tr)

        def wr2(sync, kc=kc, buf=buf):
            p.wait(sync, 'sd')
            ins = sync.dma_start(out=ar['cT_w1'][128 * kc:128 * (kc + 1), 0:3072],
                                 in_=buf)
            p.inc(ins, 'sd', 16)
        p.y(wr2)
    wosrc = w['L1_attn_a_out_w']
    for oc in range(8):
        buf = wsb[:, (oc % 2) * 4096:(oc % 2) * 4096 + 1024]

        def cast2(gpsimd, oc=oc, buf=buf):
            p.wait(gpsimd, 'sd')
            ins = gpsimd.dma_start(out=buf,
                                   in_=wosrc[128 * oc:128 * (oc + 1), :])
            p.inc(ins, 'gd', 16)
        p.g(cast2)

        def wr3(sync, oc=oc, buf=buf):
            p.wait(sync, 'gd')
            ins = sync.dma_start(
                out=ar['scr_w'][128 * oc:128 * (oc + 1), 0:1024], in_=buf)
            p.inc(ins, 'sd', 16)
        p.y(wr3)
    for kc in range(8):
        buf = wsb[:, (kc % 2) * 4096:(kc % 2) * 4096 + 1024]

        def tr2(sync, kc=kc, buf=buf):
            p.wait(sync, 'sd')
            ins = sync.dma_start_transpose(
                out=buf, in_=ar['scr_w'][0:1024, 128 * kc:128 * (kc + 1)])
            p.inc(ins, 'sd', 16)
        p.y(tr2)

        def wr4(sync, kc=kc, buf=buf):
            p.wait(sync, 'sd')
            ins = sync.dma_start(out=ar['cT_w2'][128 * kc:128 * (kc + 1), 0:1024],
                                 in_=buf)
            p.inc(ins, 'sd', 16)
        p.y(wr4)

    emit_load_x(p, x_in, xa)
    emit_ln_stats(p, xa, ar['dump'], stats, 1e-5)
    emit_normalize(p, xa, stats, xbb[:, 16384:32768])
    emit_tm_to_fm(p, xbb[:, 16384:32768], xbb[:, 16384:32768], ar['scr_tm'],
                  wait_sem='se')
    xn = xbb[:, 16384:32768]

    def ldwv(sync):
        p.wait(sync, 'pe')
        ins = sync.dma_start(
            out=wsb[:, 0:8192].rearrange("p (k o) -> p k o", k=8),
            in_=ar['cT_w1'].rearrange("(k p) o -> p k o", p=128)[:, :, 2048:3072])
        p.inc(ins, 'sd', 16)
    p.y(ldwv)

    def ones_v(gpsimd):
        p.wait(gpsimd, 'se')
        p.wait(gpsimd, 've')
        ins = gpsimd.memset(xab[:, 0:16640], 1.0)
        p.inc(ins, 'gp')
    p.g(ones_v)

    for t in range(NT):
        ps0, ps1 = ar['psf'][(t % 2) * 2], ar['psf'][(t % 2) * 2 + 1]

        def mmv(tensor, t=t, ps0=ps0, ps1=ps1):
            p.wait(tensor, 'sd')
            p.wait(tensor, 'gp')
            if t >= 2:
                p.wait(tensor, 'vps', p.cnt['vps'] - 1)
            for half, ps in ((0, ps0), (1, ps1)):
                bias_mm(p, tensor, ps, bv, 512 * half, 512 * (half + 1),
                        start=True, stop=False)
                for k in range(8):
                    ins = tensor.matmul(
                        ps, xn[:, 2048 * k + 128 * t:2048 * k + 128 * (t + 1)],
                        wsb[:, 1024 * k + 512 * half:1024 * k + 512 * (half + 1)],
                        start=False, stop=(k == 7))
            p.inc(ins, 'pe')
        p.t(mmv)

        def evv(vector, t=t, ps0=ps0, ps1=ps1):
            p.wait(vector, 'pe')
            for half, ps in ((0, ps0), (1, ps1)):
                ins = vector.tensor_copy(
                    _vslots(xab, t, half),
                    ps.rearrange("p (h e) -> p h e", h=8))
            p.inc(ins, 'vps')
        p.v(evv)

    for which, (cols, dst, biascol, scl) in enumerate(
            [(slice(0, 1024), xbb[:, 0:16384], 0, 1.0),
             (slice(1024, 2048), xcb[:, 0:16384], 8, 0.125)]):
        def ldwq(sync, cols=cols):
            p.wait(sync, 'pe')
            ins = sync.dma_start(
                out=wsb[:, 0:8192].rearrange("p (k o) -> p k o", k=8),
                in_=ar['cT_w1'].rearrange("(k p) o -> p k o", p=128)[:, :, cols])
            p.inc(ins, 'sd', 16)
        p.y(ldwq)

        for sub in range(32):
            oc, g = divmod(sub, 4)
            ps = ar['psf'][sub % 2]

            def mmq(tensor, oc=oc, g=g, ps=ps, sub=sub, which=which):
                p.wait(tensor, 'sd')
                if not (which == 0 and sub < 2):
                    p.wait(tensor, 'se', p.cnt['se'] - 1)
                if which == 0 and sub < 2:
                    p.wait(tensor, 'vps')
                for k in range(8):
                    ins = tensor.matmul(
                        ps, wsb[:, 1024 * k + 128 * oc:1024 * k + 128 * (oc + 1)],
                        xn[:, 2048 * k + 512 * g:2048 * k + 512 * (g + 1)],
                        start=(k == 0), stop=(k == 7))
                p.inc(ins, 'pe')
            p.t(mmq)

            def evq(scalar, oc=oc, g=g, ps=ps, dst=dst, biascol=biascol, scl=scl):
                p.wait(scalar, 'pe')
                ins = scalar.activation(
                    out=dst[:, 2048 * oc + 512 * g:2048 * oc + 512 * (g + 1)],
                    in_=ps, func=AF.Identity, scale=scl,
                    bias=ssb[:, biascol + oc:biascol + oc + 1])
                p.inc(ins, 'se')
            p.s(evq)

    sq = q32b[:, 0:2048]
    mq2 = q32[0:1, 3872:3888]
    mk2 = q32[0:1, 3888:3904]
    mv2 = q32[0:1, 3904:3920]
    m1r = q32[0:1, 3920:3936]
    m2r = q32[0:1, 3936:3952]

    def normqk(which, src, dstrow):
        for c in range(NCH):
            def sqr(scalar, c=c, src=src):
                p.wait(scalar, 've')
                if c == 0:
                    p.wait(scalar, 'vps')
                ins = scalar.activation(out=sq, in_=src[:, 2048 * c:2048 * (c + 1)],
                                        func=AF.Square)
                p.inc(ins, 'se')
            p.s(sqr)
            for hh in range(2):
                h = 2 * c + hh
                for g in range(4):
                    def mmn(tensor, hh=hh, g=g):
                        p.wait(tensor, 'se')
                        p.wait(tensor, 've')
                        ins = tensor.matmul(
                            ar['psb'][3][0:1, 0:512],
                            ar['ones128'][64 * hh:64 * hh + 64, 0:1],
                            sq[64 * hh:64 * hh + 64, 512 * g:512 * (g + 1)],
                            start=True, stop=True)
                        p.inc(ins, 'pe')
                    p.t(mmn)

                    def redm(vector, h=h, g=g, dstrow=dstrow):
                        p.wait(vector, 'pe')
                        vector.tensor_reduce(u512[0:1, 0:1],
                                             ar['psb'][3][0:1, 0:512],
                                             AX.X, ALU.max)
                        vector.drain()
                        if g == 0:
                            ins = vector.tensor_copy(dstrow[:, h:h + 1],
                                                     u512[0:1, 0:1])
                        else:
                            ins = vector.tensor_tensor(
                                dstrow[:, h:h + 1], dstrow[:, h:h + 1],
                                u512[0:1, 0:1], ALU.max)
                        vector.drain()
                        p.inc(ins, 've')
                    p.v(redm)

    normqk(0, xbb[:, 0:16384], mq2)
    normqk(1, xcb[:, 0:16384], mk2)

    vn = q32[:, 3584:3840]

    def vnorm(vector):
        p.wait(vector, 'vps')
        for t in range(NT):
            for h in range(H):
                sl = _vslot(xab, t, h)
                vector.scalar_tensor_tensor(
                    out=u512[:, 0:64], in0=sl, scalar=1.0, in1=sl,
                    op0=ALU.mult, op1=ALU.mult,
                    accum_out=vn[:, 16 * h + t:16 * h + t + 1])
        ins = vector.tensor_reduce(
            q32[:, 3840:3856],
            vn.rearrange("p (h t) -> p h t", h=16), AX.X, ALU.max)
        vector.drain()
        p.inc(ins, 've')
    p.v(vnorm)

    def vtr(tensor):
        p.wait(tensor, 've')
        ins = tensor.transpose(ar['psb'][3][0:16, 0:128], q32[:, 3840:3856],
                               ar['ident32'])
        p.inc(ins, 'pe')
    p.t(vtr)

    def vred(vector):
        p.wait(vector, 'pe')
        ins = vector.tensor_reduce(q32[0:16, 3856:3857],
                                   ar['psb'][3][0:16, 0:128], AX.X, ALU.max)
        vector.drain()
        p.inc(ins, 've')
    p.v(vred)

    def vtr2(tensor):
        p.wait(tensor, 've')
        ins = tensor.transpose(ar['psb'][3][0:1, 0:16], q32[0:16, 3856:3857],
                               ar['ident32'][0:16, 0:16])
        p.inc(ins, 'pe')
    p.t(vtr2)

    def vred2(vector):
        p.wait(vector, 'pe')
        ins = vector.tensor_copy(mv2[:, 0:16], ar['psb'][3][0:1, 0:16])
        vector.drain()
        p.inc(ins, 've')
    p.v(vred2)

    nm1b = q32b[0:1, 7936:7952]
    nm2b = q32b[0:1, 7952:7968]

    def mkm(scalar):
        p.wait(scalar, 've')
        scalar.activation(out=mq2, in_=mq2, func=AF.Sqrt)
        scalar.activation(out=mk2, in_=mk2, func=AF.Sqrt)
        ins = scalar.activation(out=mv2, in_=mv2, func=AF.Sqrt)
        p.inc(ins, 'se')
    p.s(mkm)

    def mkm2(vector):
        p.wait(vector, 'se')
        vector.tensor_tensor(m1r, mq2, mk2, ALU.mult)
        vector.tensor_tensor(m2r, mv2, mk2, ALU.mult)
        vector.drain()
        vector.tensor_scalar(m1r, m1r, -1.0, None, ALU.mult)
        vector.tensor_scalar(m2r, m2r, -1.0, None, ALU.mult)
        vector.drain()
        vector.tensor_copy(nm1b, m1r)
        ins = vector.tensor_copy(nm2b, m2r)
        vector.drain()
        p.inc(ins, 've')
    p.v(mkm2)

    reps = [(xbb[:, 0:16384], xbb[:, 16384:32768], nm1b),
            (xbb[:, 16384:32768], xbb[:, 0:16384], nm2b),
            (xbb[:, 0:16384], xbb[:, 16384:32768], nm2b)]
    mbf = q32[:, 3968:3969]
    dscb = q32b[0:1, 2048:2560]
    rbsb = q32[0:64, 1536:2048]
    pstage = [q32b[:, 512 * i:512 * (i + 1)] for i in range(2)]
    last_rd = [0, 0]
    vemk_ctx = [0]

    for rep, (qsrc, cdst, nmb) in enumerate(reps):
        for h in range(H):
            hb = 64 * (h % 2)
            qh = qsrc[hb:hb + 64 if hb == 0 else 128,
                      2048 * (h // 2):2048 * (h // 2 + 1)] if False else \
                qsrc[hb:hb + 64, 2048 * (h // 2):2048 * (h // 2 + 1)]
            kh = xcb[hb:hb + 64, 2048 * (h // 2):2048 * (h // 2 + 1)]

            def bcm(tensor, h=h, nmb=nmb):
                p.wait(tensor, 've')
                ins = tensor.matmul(ar['psb'][2][:, 0:1],
                                    p.ones2[0:1, 0:128], nmb[:, h:h + 1],
                                    start=True, stop=True)
                p.inc(ins, 'pe')
            p.t(bcm)

            def bcme(vector):
                p.wait(vector, 'pe')
                ins = vector.tensor_copy(mbf, ar['psb'][2][:, 0:1])
                vector.drain()
                p.inc(ins, 've')
            p.v(bcme)
            vemk_h = p.cnt['ve']

            for qb in range(4):
                nst = 4 * (qb + 1)
                pscs = (ar['psf'][2], ar['psf'][3])
                pctx = ar['psb'][3][0:65, 0:512]

                def emit_av(st0, semv, h=h, qb=qb, nst=nst, pctx=pctx):
                    diag_off = max(0, 128 * (st0 - 4 * qb))
                    vctx = vemk_ctx[0] if st0 == 0 else 0

                    def mmav(tensor, st0=st0, semv=semv, slot=st0 % 2,
                             diag_off=diag_off, pctx=pctx, h=h, nst=nst,
                             vctx=vctx):
                        p.wait(tensor, 'se', semv)
                        if st0 == 0:
                            p.wait(tensor, 've', vctx)
                        ins = tensor.matmul(
                            pctx[:, diag_off:512],
                            _vslot65(xab, st0, h),
                            pstage[slot][:, diag_off:512],
                            start=(st0 == 0), stop=(st0 == nst - 1),
                            skip_group_check=True)
                        p.inc(ins, 'pe')
                    p.t(mmav)

                pend_av = []
                for st in range(nst):
                    bank = st % 2

                    def mms(tensor, st=st, qb=qb, qh=qh, kh=kh,
                            psc=pscs[bank], rd=last_rd[bank]):
                        if st == 0:
                            p.wait(tensor, 've')
                            p.wait(tensor, 'se')
                        p.wait(tensor, 'se', rd)
                        ins = tensor.matmul(
                            psc, kh[:, 128 * st:128 * (st + 1)],
                            qh[:, 512 * qb:512 * (qb + 1)],
                            start=True, stop=True)
                        p.inc(ins, 'pe')
                    p.t(mms)
                    pemk = p.cnt['pe']

                    if st >= 4 * qb:
                        def trimask(vector, st=st, qb=qb, psc=pscs[bank],
                                    pv=pemk):
                            p.wait(vector, 'pe', pv)
                            co = 128 * (st - 4 * qb)
                            ins = vector.tensor_tensor(
                                psc[:, co:co + 128], psc[:, co:co + 128],
                                ar['ctri'], ALU.add)
                            p.inc(ins, 've')
                        p.v(trimask)
                        vemk = p.cnt['ve']
                    else:
                        vemk = vemk_h

                    def expp(scalar, psc=pscs[bank], slot=st % 2, pv=pemk,
                             vv=vemk, do=max(0, 128 * (st - 4 * qb))):
                        p.wait(scalar, 'pe', pv)
                        p.wait(scalar, 've', vv)
                        ins = scalar.activation(
                            out=pstage[slot][:, do:512], in_=psc[:, do:512],
                            func=AF.Exp, bias=mbf[:, 0:1])
                        p.inc(ins, 'se')
                    p.s(expp)
                    last_rd[bank] = p.cnt['se']
                    pend_av.append((st, p.cnt['se']))
                    if st >= 1:
                        emit_av(*pend_av.pop(0))
                while pend_av:
                    emit_av(*pend_av.pop(0))

                def evctx(vector, h=h, qb=qb, pctx=pctx, cdst=cdst):
                    p.wait(vector, 'pe')
                    vector.reciprocal(u512[0:1, 0:512], pctx[64:65, :])
                    vector.drain()
                    ins = vector.tensor_copy(dscb, u512[0:1, 0:512])
                    vector.drain()
                    p.inc(ins, 've')
                p.v(evctx)

                def bcd(tensor):
                    p.wait(tensor, 've')
                    ins = tensor.matmul(ar['psb'][2][0:64, 0:512],
                                        p.ones2[0:1, 0:64], dscb,
                                        start=True, stop=True)
                    p.inc(ins, 'pe')
                p.t(bcd)

                def evctx2(vector, h=h, qb=qb, pctx=pctx, cdst=cdst):
                    p.wait(vector, 'pe')
                    vector.tensor_copy(rbsb, ar['psb'][2][0:64, 0:512])
                    vector.drain()
                    hb = 64 * (h % 2)
                    ins = vector.scalar_tensor_tensor(
                        out=cdst[hb:hb + 64,
                                 2048 * (h // 2) + 512 * qb:
                                 2048 * (h // 2) + 512 * (qb + 1)],
                        in0=pctx[0:64, :], scalar=1.0, in1=rbsb,
                        op0=ALU.mult, op1=ALU.mult)
                    p.inc(ins, 've')
                p.v(evctx2)
                vemk_ctx[0] = p.cnt['ve']

    ctx = xbb[:, 16384:32768]

    def ldwo(sync):
        p.wait(sync, 'pe')
        ins = sync.dma_start(
            out=wsb[:, 0:8192].rearrange("p (k o) -> p k o", k=8),
            in_=ar['cT_w2'].rearrange("(k p) o -> p k o", p=128)[0:128, 0:8, 0:1024])
        p.inc(ins, 'sd', 16)
    p.y(ldwo)

    for t in range(NT):
        ps0, ps1 = ar['psf'][(t % 2) * 2], ar['psf'][(t % 2) * 2 + 1]
        xsl = q32[:, 1024 * (t % 4):1024 * (t % 4 + 1)]

        def ldxt(sync, t=t, xsl=xsl):
            p.wait(sync, 'st')
            p.wait(sync, 've')
            p.wait(sync, 'gd')
            ins = sync.dma_start(
                out=xsl, in_=x_in[128 * t:128 * (t + 1), :])
            p.inc(ins, 'sd', 16)
        p.y(ldxt)

        def mmo(tensor, t=t, ps0=ps0, ps1=ps1):
            p.wait(tensor, 'sd')
            p.wait(tensor, 've')
            if t >= 2:
                p.wait(tensor, 'vps', p.cnt['vps'] - 1)
            for half, ps in ((0, ps0), (1, ps1)):
                bias_mm(p, tensor, ps, bo, 512 * half, 512 * (half + 1),
                        start=True, stop=False)
                for k in range(8):
                    ins = tensor.matmul(
                        ps, ctx[:, 2048 * k + 128 * t:2048 * k + 128 * (t + 1)],
                        wsb[:, 1024 * k + 512 * half:1024 * k + 512 * (half + 1)],
                        start=False, stop=(k == 7))
            p.inc(ins, 'pe')
        p.t(mmo)

        def evo(vector, t=t, ps0=ps0, ps1=ps1, xsl=xsl):
            p.wait(vector, 'pe')
            vector.scalar_tensor_tensor(
                out=xsl[:, 0:512], in0=ps0, scalar=1.0, in1=xsl[:, 0:512],
                op0=ALU.mult, op1=ALU.add)
            ins = vector.scalar_tensor_tensor(
                out=xsl[:, 512:1024], in0=ps1, scalar=1.0, in1=xsl[:, 512:1024],
                op0=ALU.mult, op1=ALU.add)
            p.inc(ins, 'vps')
        p.v(evo)

        def sto(sync, t=t, xsl=xsl):
            p.wait(sync, 'vps')
            ins = sync.dma_start(out=x_out[128 * t:128 * (t + 1), :], in_=xsl)
            p.inc(ins, 'st', 16)
        p.y(sto)


def _vslot(xab, t, h):
    return xab[:, 1040 * h + 65 * t:1040 * h + 65 * t + 64]


def _vslot65(xab, t, h):
    return xab[:, 1040 * h + 65 * t:1040 * h + 65 * t + 65]


def _vslots(xab, t, half):
    base = 1040 * (8 * half) + 65 * t
    return xab[:, base:base + 8 * 1040].rearrange(
        "p (h r) -> p h r", h=8)[:, :, 0:64]





def emit_l0_attn_a(p, ar, x_in, x_out, w):
    xa = ar['XA']
    xbb = ar['XBb']
    xcb = ar['XCb']
    wsb = ar['WSb']
    stats = ar['stats']
    q32 = ar['q32']
    q32b = ar['q32'].bitcast(BF16)
    ssb = ar['ssb']
    u512 = ar['u512']
    g512 = ar['g512']

    emit_scale_load(p, ar, w['L0_attn_a_g1'], 1024, 0)
    emit_scale_load(p, ar, w['L0_attn_a_b1'], 1024, 8)
    emit_scale_load(p, ar, w['L0_attn_a_bqkv_a'][0:1024], 1024, 16)
    emit_scale_load(p, ar, w['L0_attn_a_bqkv_a'][1024:2048], 1024, 24)
    emit_scale_load(p, ar, w['L0_attn_a_gm'], 1024, 32)
    emit_scale_load(p, ar, w['L0_attn_a_bm'], 1024, 40)
    emit_scale_load(p, ar, w['L0_attn_a_b1m'], 4096, 48)
    emit_scale_load(p, ar, w['L0_attn_a_bqkv_s'][0:1024], 1024, 80)
    emit_scale_load(p, ar, w['L0_attn_a_bqkv_s'][1024:2048], 1024, 88)

    def scale_qbias(vector):
        p.wait(vector, 've')
        vector.tensor_scalar(ssb[:, 16:24], ssb[:, 16:24], 0.125, None,
                             ALU.mult)
        ins = vector.tensor_scalar(ssb[:, 80:88], ssb[:, 80:88], 0.125, None,
                                   ALU.mult)
        vector.drain()
        p.inc(ins, 've')
    p.v(scale_qbias)

    bva = hilo_bias(p, ar, [(w['L0_attn_a_bqkv_a'][2048:3072], 1024)], 0)
    bvs = hilo_bias(p, ar, [(w['L0_attn_a_bqkv_s'][2048:3072], 1024)], 1)

    emit_load_x(p, x_in, xa)
    emit_ln_stats(p, xa, ar['dump'], stats, 1e-5, double_eps=1e-5)
    emit_normalize(p, xa, stats, xcb[:, 0:16384])
    emit_tm_to_fm(p, xcb[:, 0:16384], xcb, ar['scr_tm'], wait_sem='se')

    def affine(vector):
        p.wait(vector, 'sd')
        for c in range(NCH):
            ins = vector.tensor_scalar(
                xcb[:, 2048 * c:2048 * (c + 1)], xcb[:, 2048 * c:2048 * (c + 1)],
                ssb[:, c:c + 1], ssb[:, 8 + c:9 + c], ALU.mult, ALU.add)
        p.inc(ins, 've')
    p.v(affine)
    i1 = xcb

    q2 = q32b[:, 0:2048]
    k2 = q32b[:, 2048:4096]
    v2 = q32b[:, 4096:6144]
    pstg = q32b[:, 6144:6400]
    ptst = q32b[:, 6400:6656]

    for hp in range(8):
        def ldwa(sync, hp=hp):
            p.wait(sync, 'pe')
            ins = sync.dma_start(
                out=wsb[:, 0:1024].rearrange("p (k o) -> p k o", k=8),
                in_=ar['cache']['L0_attn_a_Wqkv_a'].rearrange("(k p) o -> p k o", p=128)[
                    :, :, 128 * hp:128 * (hp + 1)])
            p.inc(ins, 'sd', 16)
            ins = sync.dma_start(
                out=wsb[:, 1024:2048].rearrange("p (k o) -> p k o", k=8),
                in_=ar['cache']['L0_attn_a_Wqkv_a'].rearrange("(k p) o -> p k o", p=128)[
                    :, :, 1024 + 128 * hp:1024 + 128 * (hp + 1)])
            p.inc(ins, 'sd', 16)
            ins = sync.dma_start(
                out=wsb[:, 2048:3072].rearrange("p (k o) -> p k o", k=8),
                in_=ar['cache']['L0_attn_a_Wqkv_a'].rearrange("(k p) o -> p k o", p=128)[
                    :, :, 2048 + 128 * hp:2048 + 128 * (hp + 1)])
            p.inc(ins, 'sd', 16)
        p.y(ldwa)

        for which, dst, bcol, scl in ((0, q2, 16, 0.125), (1, k2, 24, 1.0)):
            for g in range(4):
                ps = ar['psf'][g % 2]

                def mmqk(tensor, which=which, g=g, ps=ps, hp=hp):
                    p.wait(tensor, 'gd')
                    p.wait(tensor, 'sd')
                    p.wait(tensor, 'se')
                    p.wait(tensor, 've')
                    for k in range(8):
                        ins = tensor.matmul(
                            ps[:, 0:512],
                            wsb[:, 1024 * which + 128 * k:
                                1024 * which + 128 * (k + 1)],
                            i1[:, 2048 * k + 512 * g:2048 * k + 512 * (g + 1)],
                            start=(k == 0), stop=(k == 7))
                    p.inc(ins, 'pe')
                p.t(mmqk)

                def evqk(scalar, which=which, g=g, ps=ps, dst=dst, bcol=bcol,
                         scl=scl, hp=hp):
                    p.wait(scalar, 'pe')
                    ins = scalar.activation(
                        out=dst[:, 512 * g:512 * (g + 1)], in_=ps[:, 0:512],
                        func=AF.Identity, scale=scl,
                        bias=ssb[:, bcol + hp:bcol + hp + 1])
                    p.inc(ins, 'se')
                p.s(evqk)

        for t in range(NT):
            ps = ar['psf'][2 + (t % 2)]

            def mmva(tensor, t=t, ps=ps, hp=hp):
                p.wait(tensor, 'gd')
                p.wait(tensor, 'sd')
                p.wait(tensor, 've')
                bias_mm(p, tensor, ps[:, 0:128], bva, 128 * hp, 128 * (hp + 1),
                        start=True, stop=False)
                for k in range(8):
                    ins = tensor.matmul(
                        ps[:, 0:128],
                        i1[:, 2048 * k + 128 * t:2048 * k + 128 * (t + 1)],
                        wsb[:, 2048 + 128 * k:2048 + 128 * (k + 1)],
                        start=False, stop=(k == 7))
                p.inc(ins, 'pe')
            p.t(mmva)

            def evva(vector, t=t, ps=ps):
                p.wait(vector, 'pe')
                ins = vector.tensor_copy(v2[:, 128 * t:128 * (t + 1)],
                                         ps[:, 0:128])
                p.inc(ins, 've')
            p.v(evva)

        for h in (2 * hp, 2 * hp + 1):
            hb = 64 * (h % 2)
            for n in range(NT):
                psc = ar['psf'][0][:, 0:256]
                pav = ar['psb'][0][0:64, 0:512]
                ptp = ar['psf'][1][:, 0:512].bitcast(BF16)[:, 0:256]
                wid = 128 if n == 0 else 256
                klo = 128 * (n - 1) if n >= 1 else 0

                def mmsc(tensor, h=h, n=n, hb=hb, psc=psc, wid=wid, klo=klo):
                    p.wait(tensor, 'se')
                    p.wait(tensor, 've')
                    ins = tensor.matmul(
                        psc[:, 0:wid], q2[hb:hb + 64, 128 * n:128 * (n + 1)],
                        k2[hb:hb + 64, klo:klo + wid],
                        start=True, stop=True)
                    p.inc(ins, 'pe')
                p.t(mmsc)

                def smax(vector, n=n, psc=psc, wid=wid):
                    p.wait(vector, 'pe')
                    mask = ar['cswa0'] if n == 0 else ar['cswa1']
                    vector.tensor_tensor(psc[:, 0:wid], psc[:, 0:wid],
                                         mask[:, 0:wid], ALU.add)
                    vector.tensor_reduce(u512[:, 0:1], psc[:, 0:wid], AX.X,
                                         ALU.max)
                    vector.drain()
                    ins = vector.tensor_scalar(u512[:, 1:2], u512[:, 0:1],
                                               -1.0, None, ALU.mult)
                    vector.drain()
                    p.inc(ins, 've')
                p.v(smax)

                def sexp(scalar, psc=psc, wid=wid):
                    p.wait(scalar, 've')
                    ins = scalar.activation(out=pstg[:, 0:wid], in_=psc[:, 0:wid],
                                            func=AF.Exp, bias=u512[:, 1:2],
                                            accum_out=u512[:, 2:3])
                    p.inc(ins, 'se')
                p.s(sexp)

                def pnorm(vector, wid=wid):
                    p.wait(vector, 'se')
                    vector.reciprocal(u512[:, 3:4], u512[:, 2:3])
                    vector.drain()
                    ins = vector.tensor_scalar(pstg[:, 0:wid], pstg[:, 0:wid],
                                               u512[:, 3:4], None, ALU.mult)
                    vector.drain()
                    p.inc(ins, 've')
                p.v(pnorm)

                def ptr(tensor, wid=wid, ptp=ptp):
                    p.wait(tensor, 've')
                    for j in range(wid // 128):
                        ins = tensor.transpose(
                            ptp[:, 128 * j:128 * (j + 1)],
                            pstg[:, 128 * j:128 * (j + 1)], ar['ident'])
                    p.inc(ins, 'pe')
                p.t(ptr)

                def pte(vector, wid=wid, ptp=ptp):
                    p.wait(vector, 'pe')
                    ins = vector.tensor_copy(ptst[:, 0:wid], ptp[:, 0:wid])
                    p.inc(ins, 've')
                p.v(pte)

                def mmav(tensor, h=h, n=n, hb=hb, wid=wid, klo=klo, pav=pav):
                    p.wait(tensor, 've')
                    for j in range(wid // 128):
                        ins = tensor.matmul(
                            pav[:, 128 * (n % 4):128 * (n % 4) + 128],
                            v2[:, 128 * (klo // 128 + j) + hb:
                                128 * (klo // 128 + j) + hb + 64],
                            ptst[:, 128 * j:128 * (j + 1)],
                            start=(n % 4 == 0 and j == 0),
                            stop=(n % 4 == 3 and j == wid // 128 - 1),
                            skip_group_check=True)
                    p.inc(ins, 'pe')
                p.t(mmav)

                if n % 4 == 3:
                    def evav(vector, h=h, n=n, hb=hb, pav=pav):
                        p.wait(vector, 'pe')
                        ins = vector.tensor_copy(
                            xbb[hb:hb + 64, 2048 * (h // 2) + 128 * (n - 3):
                                2048 * (h // 2) + 128 * (n + 1)],
                            pav)
                        p.inc(ins, 've')
                    p.v(evav)

    if S0_PHASES <= 1:
        emit_store_x(p, xa, x_out, wait='ve')
        return
    q2l, k2l = q2, k2
    lstg = q32b[:, 6176:6304]
    lAT = q32b[:, 6304:6432]
    v2l = q32b[:, 4096:6176]
    ssl = [ar['w23'][0:64, 0:65], ar['w23'][64:128, 0:65]]

    for hp in range(8):
        def ldws(sync, hp=hp):
            p.wait(sync, 'pe')
            ins = sync.dma_start(
                out=wsb[:, 0:1024].rearrange("p (k o) -> p k o", k=8),
                in_=ar['cache']['L0_attn_a_Wqkv_s'].rearrange("(k p) o -> p k o", p=128)[
                    :, :, 128 * hp:128 * (hp + 1)])
            p.inc(ins, 'sd', 16)
            ins = sync.dma_start(
                out=wsb[:, 1024:2048].rearrange("p (k o) -> p k o", k=8),
                in_=ar['cache']['L0_attn_a_Wqkv_s'].rearrange("(k p) o -> p k o", p=128)[
                    :, :, 1024 + 128 * hp:1024 + 128 * (hp + 1)])
            p.inc(ins, 'sd', 16)
            ins = sync.dma_start(
                out=wsb[:, 2048:3072].rearrange("p (k o) -> p k o", k=8),
                in_=ar['cache']['L0_attn_a_Wqkv_s'].rearrange("(k p) o -> p k o", p=128)[
                    :, :, 2048 + 128 * hp:2048 + 128 * (hp + 1)])
            p.inc(ins, 'sd', 16)
        p.y(ldws)

        def onesl(gpsimd, hp=hp):
            p.wait(gpsimd, 've')
            p.wait(gpsimd, 'pe')
            ins = gpsimd.memset(v2l, 1.0)
            p.inc(ins, 'gp')
        p.g(onesl)

        for which, dst, bcol, scl in ((0, q2l, 80, 0.125), (1, k2l, 88, 1.0)):
            for g in range(4):
                ps = ar['psf'][g % 2]

                def mmqs(tensor, which=which, g=g, ps=ps):
                    p.wait(tensor, 'gd')
                    p.wait(tensor, 'sd')
                    p.wait(tensor, 've')
                    p.wait(tensor, 'se')
                    for k in range(8):
                        ins = tensor.matmul(
                            ps[:, 0:512],
                            wsb[:, 1024 * which + 128 * k:
                                1024 * which + 128 * (k + 1)],
                            i1[:, 2048 * k + 512 * g:2048 * k + 512 * (g + 1)],
                            start=(k == 0), stop=(k == 7))
                    p.inc(ins, 'pe')
                p.t(mmqs)

                def evz(scalar, which=which, g=g, ps=ps, bcol=bcol, scl=scl,
                        hp=hp):
                    p.wait(scalar, 'pe')
                    scalar.activation(out=u512, in_=ps[:, 0:512],
                                      func=AF.Identity, scale=scl,
                                      bias=ssb[:, bcol + hp:bcol + hp + 1])
                    ins = scalar.activation(out=g512[:, 0:512], in_=u512,
                                            func=AF.Exp)
                    p.inc(ins, 'se')
                p.s(evz)

                def evf(vector, which=which, g=g, dst=dst):
                    p.wait(vector, 'se')
                    vector.tensor_scalar(g512[:, 0:512], g512[:, 0:512],
                                         1.0, 0.0001, ALU.min, ALU.add)
                    vector.tensor_scalar(u512, u512, 0.0, None, ALU.max)
                    ins = vector.tensor_tensor(dst[:, 512 * g:512 * (g + 1)],
                                               u512, g512[:, 0:512], ALU.add)
                    p.inc(ins, 've')
                p.v(evf)

        for t in range(NT):
            ps = ar['psf'][2 + (t % 2)]

            def mmvs(tensor, t=t, ps=ps, hp=hp):
                p.wait(tensor, 'gd')
                p.wait(tensor, 'sd')
                p.wait(tensor, 've')
                p.wait(tensor, 'gp')
                bias_mm(p, tensor, ps[:, 0:128], bvs, 128 * hp, 128 * (hp + 1),
                        start=True, stop=False)
                for k in range(8):
                    ins = tensor.matmul(
                        ps[:, 0:128],
                        i1[:, 2048 * k + 128 * t:2048 * k + 128 * (t + 1)],
                        wsb[:, 2048 + 128 * k:2048 + 128 * (k + 1)],
                        start=False, stop=(k == 7))
                p.inc(ins, 'pe')
            p.t(mmvs)

            def evvs(vector, t=t, ps=ps):
                p.wait(vector, 'pe')
                ins = vector.tensor_copy(
                    _lslots(v2l, t),
                    ps[:, 0:128].rearrange("p (a e) -> p a e", a=2))
                p.inc(ins, 'vps')
            p.v(evvs)

        for h in (2 * hp, 2 * hp + 1):
            hb = 64 * (h % 2)
            sext = ssl[h % 2]
            psS = ar['psb'][2][0:64, 0:65] if h % 2 == 0 else \
                ar['psb'][3][0:64, 0:65]

            def sinit(vector, sext=sext):
                p.wait(vector, 'vps')
                ins = vector.memset(sext, 0.0)
                vector.drain()
                p.inc(ins, 've')
            p.v(sinit)

            for n in range(NT):
                psA = ar['psf'][0][:, 0:128]
                pAT = ar['psf'][1][:, 0:512].bitcast(BF16)[:, 0:128]
                pKT = ar['psf'][2][:, 0:512].bitcast(BF16)[:, 0:64]
                pnum = ar['psf'][3][:, 0:65]
                pot = ar['psb'][1][:, 0:512].bitcast(BF16)[:, 0:128]

                def mmA(tensor, h=h, n=n, hb=hb, psA=psA):
                    p.wait(tensor, 've')
                    p.wait(tensor, 'se')
                    ins = tensor.matmul(
                        psA, q2l[hb:hb + 64, 128 * n:128 * (n + 1)],
                        k2l[hb:hb + 64, 128 * n:128 * (n + 1)],
                        start=True, stop=True)
                    p.inc(ins, 'pe')
                p.t(mmA)

                def evA(vector, psA=psA):
                    p.wait(vector, 'pe')
                    ins = vector.tensor_tensor(lstg[:, 0:128], psA,
                                               ar['ctril'], ALU.mult)
                    p.inc(ins, 've')
                p.v(evA)

                def trAK(tensor, n=n, hb=hb, pAT=pAT, pKT=pKT):
                    p.wait(tensor, 've')
                    tensor.transpose(pAT, lstg[:, 0:128], ar['ident'])
                    ins = tensor.transpose(
                        pKT, k2l[hb:hb + 64, 128 * n:128 * (n + 1)],
                        ar['ident'][hb:hb + 64, hb:hb + 64])
                    p.inc(ins, 'pe')
                p.t(trAK)

                def evAT(vector, pAT=pAT, pKT=pKT):
                    p.wait(vector, 'pe')
                    vector.tensor_copy(lAT[:, 0:128], pAT)
                    ins = vector.tensor_copy(q32b[:, 6432:6496], pKT)
                    p.inc(ins, 've')
                p.v(evAT)

                def mmnum(tensor, h=h, n=n, hb=hb, pnum=pnum, sext=sext):
                    p.wait(tensor, 've')
                    tensor.matmul(pnum, lAT[:, 0:128], _lslot65(v2l, h, n),
                                  start=True, stop=False,
                                  skip_group_check=True)
                    ins = tensor.matmul(
                        pnum, q2l[hb:hb + 64, 128 * n:128 * (n + 1)], sext,
                        start=False, stop=True, skip_group_check=True)
                    p.inc(ins, 'pe')
                p.t(mmnum)

                def mmS(tensor, h=h, n=n, psS=psS):
                    ins = tensor.matmul(
                        psS, q32b[:, 6432:6496], _lslot65(v2l, h, n),
                        start=(n == 0), stop=(n == NT - 1),
                        skip_group_check=True)
                    p.inc(ins, 'pe')
                p.t(mmS)

                def evnum(vector, n=n, pnum=pnum, psS=psS, sext=sext):
                    p.wait(vector, 'pe')
                    vector.tensor_scalar(u512[:, 0:1], pnum[:, 64:65], 1e-6,
                                         None, ALU.add)
                    vector.drain()
                    vector.reciprocal(u512[:, 1:2], u512[:, 0:1])
                    vector.drain()
                    vector.tensor_scalar(lstg[:, 0:64], pnum[:, 0:64],
                                         u512[:, 1:2], None, ALU.mult)
                    ins = vector.tensor_copy(sext, psS)
                    vector.drain()
                    p.inc(ins, 've')
                p.v(evnum)

                def trO(tensor, pot=pot):
                    p.wait(tensor, 've')
                    ins = tensor.transpose(pot[0:64, :], lstg[:, 0:64],
                                           ar['ident'])
                    p.inc(ins, 'pe')
                p.t(trO)

                def evO(vector, h=h, n=n, hb=hb, pot=pot):
                    p.wait(vector, 'pe')
                    ins = vector.tensor_copy(
                        xbb[hb:hb + 64, 16384 + 2048 * (h // 2) + 128 * n:
                            16384 + 2048 * (h // 2) + 128 * (n + 1)],
                        pot[0:64, :])
                    p.inc(ins, 've')
                p.v(evO)

    if S0_PHASES <= 2:
        emit_store_x(p, xa, x_out, wait='ve')
        return
    swa_o = xbb[:, 0:16384]
    lin_o = xbb[:, 16384:32768]
    emit_scale_load(p, ar, w['L0_attn_a_bp_a'], 1024, 96)
    emit_scale_load(p, ar, w['L0_attn_a_bo_s'], 1024, 104)

    def combb(vector):
        p.wait(vector, 've')
        ins = vector.tensor_tensor(ssb[:, 96:104], ssb[:, 96:104],
                                   ssb[:, 104:112], ALU.add)
        vector.drain()
        p.inc(ins, 've')
    p.v(combb)

    for og in range(2):
        def ldmx(sync, og=og):
            p.wait(sync, 'pe')
            ins = sync.dma_start(
                out=wsb[:, 0:4096].rearrange("p (k o) -> p k o", k=8),
                in_=ar['cache']['L0_attn_a_Wp_a'].rearrange("(k p) o -> p k o", p=128)[
                    :, :, 512 * og:512 * (og + 1)])
            p.inc(ins, 'sd', 16)
            ins = sync.dma_start(
                out=wsb[:, 4096:8192].rearrange("p (k o) -> p k o", k=8),
                in_=ar['cache']['L0_attn_a_Wo_s'].rearrange("(k p) o -> p k o", p=128)[
                    :, :, 512 * og:512 * (og + 1)])
            p.inc(ins, 'sd', 16)
        p.y(ldmx)

        for sub in range(16):
            oc, g = divmod(sub, 4)
            oabs = 4 * og + oc
            ps = ar['psf'][sub % 2]

            def mmmx(tensor, oc=oc, g=g, ps=ps, sub=sub, og=og, oabs=oabs):
                p.wait(tensor, 'gd')
                p.wait(tensor, 'sd')
                p.wait(tensor, 've')
                if not (og == 0 and sub < 2):
                    p.wait(tensor, 'se', p.cnt['se'] - 1)
                for k in range(8):
                    tensor.matmul(
                        ps[:, 0:512],
                        wsb[:, 0:4096].rearrange("p (k o) -> p k o", k=8)[
                            :, k, 128 * oc:128 * (oc + 1)],
                        swa_o[:, 2048 * k + 512 * g:2048 * k + 512 * (g + 1)],
                        start=(k == 0), stop=False)
                for k in range(8):
                    ins = tensor.matmul(
                        ps[:, 0:512],
                        wsb[:, 4096:8192].rearrange("p (k o) -> p k o", k=8)[
                            :, k, 128 * oc:128 * (oc + 1)],
                        lin_o[:, 2048 * k + 512 * g:2048 * k + 512 * (g + 1)],
                        start=False, stop=(k == 7))
                p.inc(ins, 'pe')
            p.t(mmmx)

            def evmx(scalar, oabs=oabs, g=g, ps=ps):
                p.wait(scalar, 'pe')
                ins = scalar.activation(
                    out=xcb[:, 2048 * oabs + 512 * g:2048 * oabs + 512 * (g + 1)],
                    in_=ps[:, 0:512], func=AF.Identity,
                    scale=1.0, bias=ssb[:, 96 + oabs:97 + oabs])
                p.inc(ins, 'se')
            p.s(evmx)

    bo = hilo_bias(p, ar, [(w['L0_attn_a_bo'], 1024)], 0)

    def ldwo2(sync):
        p.wait(sync, 'pe')
        ins = sync.dma_start(
            out=wsb[:, 0:8192].rearrange("p (k o) -> p k o", k=8),
            in_=ar['cache']['L0_attn_a_Wo'].rearrange("(k p) o -> p k o", p=128))
        p.inc(ins, 'sd', 16)
    p.y(ldwo2)

    for t in range(NT):
        ps0, ps1 = ar['psf'][(t % 2) * 2], ar['psf'][(t % 2) * 2 + 1]

        def mmwo(tensor, t=t, ps0=ps0, ps1=ps1):
            p.wait(tensor, 'gd')
            p.wait(tensor, 'sd')
            p.wait(tensor, 'se')
            p.wait(tensor, 've')
            if t >= 2:
                p.wait(tensor, 'vps', p.cnt['vps'] - 1)
            for half, ps in ((0, ps0), (1, ps1)):
                bias_mm(p, tensor, ps, bo, 512 * half, 512 * (half + 1),
                        start=True, stop=False)
                for k in range(8):
                    ins = tensor.matmul(
                        ps, xcb[:, 2048 * k + 128 * t:2048 * k + 128 * (t + 1)],
                        wsb[:, 1024 * k + 512 * half:1024 * k + 512 * (half + 1)],
                        start=False, stop=(k == 7))
            p.inc(ins, 'pe')
        p.t(mmwo)

        def evwo(vector, t=t, ps0=ps0, ps1=ps1):
            p.wait(vector, 'pe')
            for half, ps in ((0, ps0), (1, ps1)):
                vector.tensor_scalar(
                    xa[:, 1024 * t + 512 * half:1024 * t + 512 * (half + 1)],
                    xa[:, 1024 * t + 512 * half:1024 * t + 512 * (half + 1)],
                    stats[:, 64 + t:65 + t], stats[:, 80 + t:81 + t],
                    ALU.mult, ALU.add)
                ins = vector.tensor_tensor(
                    xa[:, 1024 * t + 512 * half:1024 * t + 512 * (half + 1)],
                    xa[:, 1024 * t + 512 * half:1024 * t + 512 * (half + 1)],
                    ps, ALU.add)
            p.inc(ins, 'vps')
        p.v(evwo)

    if S0_PHASES <= 3:
        def gate3(vector):
            p.wait(vector, 'vps')
            ins = vector.tensor_copy(u512[:, 0:1], xa[:, 0:1])
            p.inc(ins, 'vps')
        p.v(gate3)
        emit_store_x(p, xa, x_out, wait='vps')
        return
    def gate_mlp(scalar):
        p.wait(scalar, 'vps')
    p.s(gate_mlp)
    emit_ln_stats(p, xa, ar['dump'], stats, 1e-5)
    emit_normalize(p, xa, stats, xcb[:, 0:16384])
    emit_tm_to_fm(p, xcb[:, 0:16384], xcb, ar['scr_tm'], wait_sem='se')

    def affinem(vector):
        p.wait(vector, 'sd')
        for c in range(NCH):
            ins = vector.tensor_scalar(
                xcb[:, 2048 * c:2048 * (c + 1)], xcb[:, 2048 * c:2048 * (c + 1)],
                ssb[:, 32 + c:33 + c], ssb[:, 40 + c:41 + c], ALU.mult, ALU.add)
        p.inc(ins, 've')
    p.v(affinem)

    b2m = hilo_bias(p, ar, [(w['L0_attn_a_b2m'], 1024)], 1)

    for tb in range(2):
        t0 = 1024 * tb
        for og in range(8):
            o0 = 512 * og

            def ldw1(sync, o0=o0):
                p.wait(sync, 'pe')
                p.wait(sync, 've')
                p.wait(sync, 'se')
                p.wait(sync, 'vps')
                ins = sync.dma_start(
                    out=wsb[:, 0:4096].rearrange("p (k o) -> p k o", k=8),
                    in_=ar['cache']['L0_attn_a_W1'].rearrange(
                        "(k p) o -> p k o", p=128)[:, :, o0:o0 + 512])
                p.inc(ins, 'sd', 16)
            p.y(ldw1)

            for sub in range(8):
                oc, g = divmod(sub, 2)
                oabs = 4 * og + oc
                ps = ar['psf'][sub % 4]

                def mm1(tensor, oc=oc, g=g, ps=ps, t0=t0):
                    p.wait(tensor, 'gd')
                    p.wait(tensor, 'sd')
                    p.wait(tensor, 'se')
                    for k in range(8):
                        ins = tensor.matmul(
                            ps,
                            wsb[:, 512 * k + 128 * oc:512 * k + 128 * (oc + 1)],
                            xcb[:, 2048 * k + t0 + 512 * g:
                                2048 * k + t0 + 512 * (g + 1)],
                            start=(k == 0), stop=(k == 7))
                    p.inc(ins, 'pe')
                p.t(mm1)

                def ev1(scalar, ps=ps, oabs=oabs, g=g):
                    p.wait(scalar, 'pe')
                    ins = scalar.activation(
                        out=xbb[:, 1024 * oabs + 512 * g:
                                1024 * oabs + 512 * (g + 1)],
                        in_=ps, func=AF.Gelu,
                        scale=1.0, bias=ssb[:, 48 + oabs:49 + oabs])
                    p.inc(ins, 'se')
                p.s(ev1)

        for th in range(2):
            tt0 = t0 + 512 * th

            def ldx2(sync, tt0=tt0):
                p.wait(sync, 'vps')
                p.wait(sync, 've')
                p.wait(sync, 'st')
                p.wait(sync, 'gd')
                ins = sync.dma_start(
                    out=q32.rearrange("p (t f) -> p t f", t=4),
                    in_=x_in[tt0:tt0 + 512, :].rearrange(
                        "(t p) f -> p t f", p=128))
                p.inc(ins, 'sd', 16)
            p.y(ldx2)

            for kg in range(4):
                def ldw2(sync, kg=kg):
                    p.wait(sync, 'pe')
                    p.wait(sync, 'sd')
                    if kg == 0:
                        p.wait(sync, 'se')
                    ins = sync.dma_start(
                        out=wsb[:, 0:8192].rearrange("p (k o) -> p k o", k=8),
                        in_=ar['cache']['L0_attn_a_W2'].rearrange(
                            "(k p) o -> p k o", p=128)[:, 8 * kg:8 * kg + 8, :])
                    p.inc(ins, 'sd', 16)
                p.y(ldw2)

                def mm2(tensor, kg=kg, tt0=tt0, t0=t0):
                    p.wait(tensor, 'gd')
                    p.wait(tensor, 'sd')
                    if kg == 0:
                        p.wait(tensor, 'vps')
                        p.wait(tensor, 'se')
                    for tt in range(4):
                        for half in range(2):
                            ps = ar['psf'][tt] if half == 0 else ar['psb'][tt]
                            for k8 in range(8):
                                k = 8 * kg + k8
                                ins = tensor.matmul(
                                    ps,
                                    xbb[:, 1024 * k + (tt0 - t0) + 128 * tt:
                                        1024 * k + (tt0 - t0) + 128 * (tt + 1)],
                                    wsb[:, 1024 * k8 + 512 * half:
                                        1024 * k8 + 512 * (half + 1)],
                                    start=(kg == 0 and k8 == 0), stop=False)
                    p.inc(ins, 'pe')
                p.t(mm2)

            def mmb2(tensor, tt0=tt0):
                for tt in range(4):
                    for half in range(2):
                        ps = ar['psf'][tt] if half == 0 else ar['psb'][tt]
                        ins = tensor.matmul(
                            ps, p.ones2[:, 0:128],
                            b2m[:, 512 * half:512 * (half + 1)],
                            start=False, stop=True)
                p.inc(ins, 'pe')
            p.t(mmb2)

            def ev2(vector, tt0=tt0):
                p.wait(vector, 'pe')
                xbase = (tt0 // 128) * 1024
                for tt in range(4):
                    for half in range(2):
                        ps = ar['psf'][tt] if half == 0 else ar['psb'][tt]
                        col = 1024 * tt + 512 * half
                        vector.scalar_tensor_tensor(
                            out=q32[:, col:col + 512], in0=ps, scalar=1.0,
                            in1=q32[:, col:col + 512], op0=ALU.mult,
                            op1=ALU.add)
                        ins = vector.tensor_tensor(
                            q32[:, col:col + 512], q32[:, col:col + 512],
                            xa[:, xbase + col:xbase + col + 512], ALU.add)
                p.inc(ins, 'vps')
            p.v(ev2)

            def st2(sync, tt0=tt0):
                p.wait(sync, 'vps')
                ins = sync.dma_start(
                    out=x_out[tt0:tt0 + 512, :].rearrange(
                        "(t p) f -> p t f", p=128),
                    in_=q32.rearrange("p (t f) -> p t f", t=4))
                p.inc(ins, 'st', 16)
            p.y(st2)


def _lslots(v2l, t):
    return v2l.rearrange("p (a r) -> p a r", a=2)[:, :, 65 * t:65 * t + 64]


def _lslot65(v2l, h, n):
    return v2l[:, 1040 * (h % 2) + 65 * n:1040 * (h % 2) + 65 * n + 65]





CACHE_SPECS = [
    ('L0_attn_a_Wqkv_a', 1024, 3072),
    ('L0_attn_a_Wqkv_s', 1024, 3072),
    ('L0_attn_a_Wp_a', 1024, 1024),
    ('L0_attn_a_Wo_s', 1024, 1024),
    ('L0_attn_a_Wo', 1024, 1024),
    ('L0_attn_a_W1', 1024, 4096),
    ('L0_attn_a_W2', 4096, 1024),
    ('L0_attn_b_wq', 1024, 1024),
    ('L0_attn_b_wk', 1024, 1024),
    ('L0_attn_b_wv', 1024, 1024),
    ('L0_attn_b_wo', 1024, 1024),
    ('L0_attn_b_w1', 1024, 4096),
    ('L0_attn_b_w2', 4096, 1024),
    ('L1_attn_b_W1', 1024, 1024),
]


def emit_cache_build(p, ar, w, needed):
    wsb = ar['WSb']
    for name, K, O in CACHE_SPECS:
        if name not in needed:
            continue
        src = w[name]
        if src.ndim == 3:
            src = src.rearrange("a b c -> (a b) c") if src.shape[0] * \
                src.shape[1] == K else src.rearrange("a b c -> a (b c)")
        cache = ar['cache'][name]
        cols = min(O, 4096)
        rows_per = 128
        for r0 in range(0, K, rows_per):
            for c0 in range(0, O, cols):
                half = ((r0 // rows_per + c0 // cols) % 2) * 4096
                buf = wsb[:, half:half + cols]

                def cast(gpsimd, buf=buf, src=src, r0=r0, c0=c0, cols=cols):
                    p.wait(gpsimd, 'sd')
                    ins = gpsimd.dma_start(
                        out=buf, in_=src[r0:r0 + 128, c0:c0 + cols])
                    p.inc(ins, 'gd', 16)
                p.g(cast)

                def wr(sync, buf=buf, cache=cache, r0=r0, c0=c0, cols=cols):
                    p.wait(sync, 'gd')
                    ins = sync.dma_start(
                        out=cache[r0:r0 + 128, c0:c0 + cols], in_=buf)
                    p.inc(ins, 'sd', 16)
                p.y(wr)



def build_arenas(nc, es):
    ar = {}
    sb = lambda n, s, d=F32: es.enter_context(nc.sbuf_tensor(n, s, d))
    XA = sb("XA", [128, 16384])
    XB = sb("XB", [128, 16384])
    XC = sb("XC", [128, 8192])
    WS = sb("WS", [128, 4096])
    ar['XA'] = XA[:]
    ar['XAb'] = XA[:].bitcast(BF16)
    ar['XB'] = XB[:]
    ar['XBb'] = XB[:].bitcast(BF16)
    ar['XCb'] = XC[:].bitcast(BF16)
    ar['WSb'] = WS[:].bitcast(BF16)
    ar['stats'] = sb("stats", [128, 96])[:]
    ar['q16'] = sb("q16", [128, 96])[:]
    ar['q32'] = sb("q32", [128, 4096])[:]
    ar['dump'] = ar['q32'][:, 3072:4096]
    ar['brow'] = ar['q32'][0:1, 0:2048]
    ar['s32'] = ar['q32'][0:32, 2048:2176]
    bias2 = sb("bias2", [2, 2176], BF16)
    ar['bias2'] = [bias2[:, 0:1024], bias2[:, 1024:2048], bias2[:, 2048:2176]]
    ar['ident'] = sb("ident", [128, 128], BF16)[:]
    ar['ones2'] = sb("ones2", [2, 128], BF16)[:]
    ar['ones128'] = sb("ones128", [128, 2], BF16)[:]
    ar['w23'] = sb("w23", [128, 256], BF16)
    ar['u512'] = sb("u512", [128, 512])[:]
    ar['g512'] = sb("g512", [128, 1024])[:]
    ar['ssb'] = sb("ssb", [128, 136])[:]

    ar['ident32'] = sb("ident32", [128, 128])[:]
    ar['ctri'] = sb("ctri", [128, 128])[:]
    ar['cswa0'] = sb("cswa0", [128, 128])[:]
    ar['cswa1'] = sb("cswa1", [128, 256])[:]
    ar['ctril'] = sb("ctril", [128, 128])[:]
    ar['psf'] = [es.enter_context(nc.psum_tensor(f"psf{i}", [128, 512], F32))[:]
                 for i in range(4)]
    ar['psb'] = [es.enter_context(nc.psum_tensor(f"psb{i}", [128, 512], F32))[:]
                 for i in range(4)]
    ar['scr_tm'] = nc.dram_tensor("scr_tm", [L, D], BF16, kind="Internal").ap()
    if DEBUG_ATTNB:
        ar['dbg_out'] = nc.dram_tensor("dbg", [128, 1056], F32,
                                       kind="ExternalOutput").ap()
        ar['dbg2_out'] = nc.dram_tensor("dbg2", [128, 1152], F32,
                                        kind="ExternalOutput").ap()
        ar['dbg3_out'] = nc.dram_tensor("dbg3", [128, 2048], BF16,
                                        kind="ExternalOutput").ap()
    ar['scr_w'] = nc.dram_tensor("scr_w", [1024, 1024], BF16, kind="Internal").ap()
    ar['pst'] = ar['psb'][0][:, 0:128]
    ar['c_raw'] = nc.dram_tensor("c_raw", [4096, 4096], BF16, kind="Internal").ap()
    ar['cT_w1'] = nc.dram_tensor("cT_w1", [1024, 4096], BF16, kind="Internal").ap()
    ar['cT_w2'] = nc.dram_tensor("cT_w2", [1024, 4096], BF16, kind="Internal").ap()
    ar['cT_h1'] = nc.dram_tensor("cT_h1", [4096, 4096], BF16, kind="Internal").ap()
    ar['cT_h2'] = nc.dram_tensor("cT_h2", [4096, 4096], BF16, kind="Internal").ap()
    ar['cT_w3'] = nc.dram_tensor("cT_w3", [4096, 1024], BF16, kind="Internal").ap()
    return ar


SUBLAYERS = ['l0_attn_a', 'l0_ffn', 'l0_attn_b', 'l0_ffn',
             'l1_attn_a', 'l1_ffn', 'l1_attn_b', 'l1_ffn']


def build_program(lo, hi, weights_meta):
    _, wlog = _build(lo, hi, weights_meta, 'record')
    nc, _ = _build(lo, hi, weights_meta, 'replay', wlog)
    return nc


def _build(lo, hi, weights_meta, mode, wlog=None):
    nc = bass.Bass("TRN2", target_bir_lowering=False)
    x_in = nc.dram_tensor("x", [L, D], F32, kind="ExternalInput").ap()
    x_out = nc.dram_tensor("y", [L, D], F32, kind="ExternalOutput").ap()
    consts = {'c_ident': nc.dram_tensor("c_ident", [128, 128], BF16,
                                        kind="ExternalInput").ap(),
              'c_tri': nc.dram_tensor("c_tri", [128, 128], F32,
                                      kind="ExternalInput").ap(),
              'c_swa0': nc.dram_tensor("c_swa0", [128, 128], F32,
                                       kind="ExternalInput").ap(),
              'c_swa1': nc.dram_tensor("c_swa1", [128, 256], F32,
                                       kind="ExternalInput").ap(),
              'c_tril': nc.dram_tensor("c_tril", [128, 128], F32,
                                       kind="ExternalInput").ap()}
    w = {name: nc.dram_tensor(name, list(shape), F32, kind="ExternalInput").ap()
         for name, shape in weights_meta}
    xbufs = [nc.dram_tensor(f"xbuf{i}", [L, D], F32, kind="Internal").ap()
             for i in range(2)]

    es = ExitStack()
    with es:
        ar = build_arenas(nc, es)
        ar['cache'] = {}
        needed = set()
        for nm, K, O in CACHE_SPECS:
            pref = nm[:nm.index('_', 3)]
            ar['cache'][nm] = nc.dram_tensor(
                "cch_" + nm, [K, O], BF16, kind="Internal").ap()
            needed.add(nm)
        with nc.Block() as block:
            p = P(nc, block, es, mode=mode, wlog=wlog)
            p.ones2 = ar['ones2']

            def init(gpsimd):
                gpsimd.memset(ar['ones2'], 1.0)
                ins = gpsimd.memset(ar['ones128'], 1.0)
                p.inc(ins, 'gp')
            p.g(init)
            p.ones128 = ar['ones128'][:, 0:1]

            def initc(sync):
                ins = sync.dma_start(out=ar['ident'], in_=consts['c_ident'])
                p.inc(ins, 'sd', 16)
                ins = sync.dma_start(out=ar['ctri'], in_=consts['c_tri'])
                p.inc(ins, 'sd', 16)
                ins = sync.dma_start(out=ar['cswa0'], in_=consts['c_swa0'])
                p.inc(ins, 'sd', 16)
                ins = sync.dma_start(out=ar['cswa1'], in_=consts['c_swa1'])
                p.inc(ins, 'sd', 16)
                ins = sync.dma_start(out=ar['ctril'], in_=consts['c_tril'])
                p.inc(ins, 'sd', 16)
            p.y(initc)

            def initc2(vector):
                p.wait(vector, 'sd')
                ins = vector.tensor_copy(ar['ident32'], ar['ident'])
                p.inc(ins, 've')
            p.v(initc2)

            emit_cache_build(p, ar, w, needed)
            first_l0ffn = True
            first_l1ffn = True
            cur = x_in
            for i in range(lo, hi):
                name = SUBLAYERS[i]
                nxt = x_out if i == hi - 1 else xbufs[i % 2]
                if name == 'l1_ffn':
                    emit_l1_ffn(p, ar, cur, nxt, w, first=first_l1ffn)
                    first_l1ffn = False
                elif name == 'l1_attn_b':
                    emit_l1_attn_b(p, ar, cur, nxt, w)
                elif name == 'l0_ffn':
                    emit_l0_ffn(p, ar, cur, nxt, w, first=first_l0ffn)
                    first_l0ffn = False
                elif name == 'l0_attn_b':
                    emit_l0_attn_b(p, ar, cur, nxt, w)
                elif name == 'l1_attn_a':
                    emit_l1_attn_a(p, ar, cur, nxt, w)
                elif name == 'l0_attn_a':
                    emit_l0_attn_a(p, ar, cur, nxt, w)
                else:
                    raise NotImplementedError(name)
                cur = nxt
                if i != hi - 1:
                    p.barrier()

            def final(sync):
                p.wait(sync, 'st')
            p.y(final)
            if mode == 'replay':
                p.flush()
    return nc, p.wlog


def make_per_core(inputs):
    x = np.ascontiguousarray(inputs['L0_attn_a_x'])
    r_ = np.arange(128)[:, None]
    c_ = np.arange(128)[None, :]
    c2_ = np.arange(256)[None, :]
    consts = {
        'c_ident': np.eye(128, dtype=ml_dtypes.bfloat16),
        'c_tri': np.where(r_ > c_, np.float32(-10000.0), np.float32(0.0)),
        'c_swa0': np.where(c_ <= r_, np.float32(0.0), np.float32(-1e30)),
        'c_swa1': np.where((c2_ >= r_ + 1) & (c2_ <= r_ + 128),
                           np.float32(0.0), np.float32(-1e30)),
        'c_tril': np.where(r_ >= c_, np.float32(1.0), np.float32(0.0)),
    }
    per_core = []
    for b in range(8):
        m = {'x': x[b]}
        m.update(consts)
        for k, v in inputs.items():
            if k != 'L0_attn_a_x':
                m[k] = np.ascontiguousarray(v)
        per_core.append(m)
    return per_core


def kernel(**inputs):
    weights_meta = [(k, v.shape) for k, v in inputs.items() if k != 'L0_attn_a_x']
    per_core = make_per_core(inputs)
    nc = build_program(0, 8, weights_meta)
    res = run_bass_kernel_spmd(nc, per_core, core_ids=list(range(8)))
    return np.stack([r['y'] for r in res.results], axis=0)


# version 06
# speedup vs baseline: 1.1790x; 1.0150x over previous
import numpy as np
import ml_dtypes
from contextlib import ExitStack
import concourse.bass as bass
import concourse.mybir as mybir
from concourse.bass_utils import run_bass_kernel_spmd

F32 = mybir.dt.float32
BF16 = mybir.dt.bfloat16
AF = mybir.ActivationFunctionType
ALU = mybir.AluOpType
AX = mybir.AxisListType

L, D, H, E = 2048, 1024, 16, 64
DEBUG_ATTNB = False
S2_PHASES = 99
S2_V7 = 0
S0_PHASES = 99
NT = L // 128
NCH = D // 128
MAGIC = 12582912.0

ENGINES = ('tensor', 'vector', 'scalar', 'gpsimd', 'sync')



_PROXY_SKIP = {'wait_ge', 'drain', 'engine_nop', 'nop'}


class _EngProxy:

    def __init__(self, p, eng, semname):
        self._p, self._eng, self._sem = p, eng, semname

    def __getattr__(self, name):
        val = getattr(self._eng, name)
        if not callable(val) or name.startswith('_') or name in _PROXY_SKIP:
            return val

        def wrapper(*a, **k):
            p, eng, semname = self._p, self._eng, self._sem
            if p.cnt.get(semname, 0) > 0:
                eng.wait_ge(p._sem(semname), p.cnt[semname])
            ins = val(*a, **k)
            if ins is not None and hasattr(ins, 'then_inc'):
                p.inc(ins, semname)
            return ins
        return wrapper


class P:

    def __init__(self, nc, block, es, mode='record', wlog=None):
        self.nc, self.block, self.es = nc, block, es
        self.sems, self.cnt = {}, {}
        self.marks = {}
        self.ones2 = None
        self.mode = mode
        self.wlog = [] if wlog is None else wlog
        self.nsec = -1
        self.cur = -1
        self.queue = {e: [] for e in ENGINES}
        self.wptr = {}
        for n in ('sd', 'gd', 'st', 'pe', 'gp', 've', 'vps', 'se',
                  'vself', 'sself'):
            self.cnt[n] = 0

    def _sem(self, name):
        if name not in self.sems:
            self.sems[name] = self.es.enter_context(self.nc.semaphore(name))
            self.cnt.setdefault(name, 0)
        return self.sems[name]

    def _group(self, eng, pairs_fn):
        if self.mode == 'record':
            pairs = [(n, t) for (n, t) in pairs_fn() if t > 0]
            self.wlog[self.cur].append(pairs)
        else:
            i = self.wptr.get(self.cur, 0)
            self.wptr[self.cur] = i + 1
            pairs = self.wlog[self.cur][i]
        for n, t in pairs:
            eng.wait_ge(self._sem(n), t)

    def _section(self, engname, fn):
        self.nsec += 1
        if self.mode == 'record':
            self.cur = self.nsec
            self.wlog.append([])
            getattr(self.block, engname)(fn)
        else:
            self.queue[engname].append((self.nsec, fn))

    def flush(self):
        assert self.mode == 'replay'
        for engname in ENGINES:
            items = self.queue[engname]
            if not items:
                continue

            def runner(eng, items=items):
                for sec, fn in items:
                    self.cur = sec
                    fn(eng)
                    assert self.wptr.get(sec, 0) == len(self.wlog[sec]), \
                        f"section {sec}: wait groups not fully consumed"
            getattr(self.block, engname)(runner)

    ALIAS = {'ve': 'vself', 'vps': 'vself', 'se': 'sself'}

    def inc(self, instr, name, n=1):
        if name in self.ALIAS:
            base = self.ALIAS[name]
            self.cnt[name] = self.cnt.get(name, 0) + n
            self.marks.setdefault(name, {})[self.cnt[name]] = \
                self.cnt.get(base, 0)
            return self.cnt[name]
        instr.then_inc(self._sem(name), n)
        self.cnt[name] = self.cnt.get(name, 0) + n
        return self.cnt[name]

    def wait(self, eng, name, val=None):
        def pairs():
            if name in self.ALIAS:
                base = self.ALIAS[name]
                if val is None:
                    tgt = self.cnt.get(base, 0)
                else:
                    if val <= 0:
                        return []
                    tgt = self.marks[name][val]
                return [(base, tgt)]
            tgt = self.cnt.get(name, 0) if val is None else val
            return [(name, tgt)]
        self._group(eng, pairs)

    def t(self, fn): self._section('tensor', fn)

    def v(self, fn):
        self._section('vector',
                      lambda eng: fn(_EngProxy(self, eng, 'vself')))

    def s(self, fn):
        self._section('scalar',
                      lambda eng: fn(_EngProxy(self, eng, 'sself')))

    def _dma_section(self, engname, fn, sems):
        before = {n: self.cnt.get(n, 0) for n in sems}

        def wrapped(eng):
            fn(eng)
            self._group(eng, lambda: [(n, self.cnt[n]) for n in sems
                                      if self.cnt.get(n, 0) > before[n]])
        self._section(engname, wrapped)

    def g(self, fn): self._dma_section('gpsimd', fn, ('gd',))
    def y(self, fn): self._dma_section('sync', fn, ('sd', 'st'))

    def barrier(self):
        def mk(engname):
            def f(eng):
                self._group(eng, lambda: [(n, self.cnt[n])
                                          for n in list(self.sems)
                                          if self.cnt[n] > 0])
            return f
        for e in ENGINES:
            self._section(e, mk(e))



def hilo_bias(p, ar, vecs, slot):
    ntot = sum(n for _, n in vecs)
    f32row = ar['brow'][:, 0:ntot]
    f32row2 = ar['brow'][:, 1024:1024 + ntot]
    dst = ar['bias2'][slot][:, 0:ntot]

    def ld(sync):
        p.wait(sync, 'gd')
        p.wait(sync, 've')
        p.wait(sync, 'pe')
        off = 0
        for v, n in vecs:
            src = v if v.ndim == 2 else v.unsqueeze(0)
            ins = sync.dma_start(out=f32row[:, off:off + n], in_=src)
            p.inc(ins, 'sd', 16)
            off += n
    p.y(ld)

    def mk(vector):
        p.wait(vector, 'sd')
        vector.tensor_copy(dst[0:1, :], f32row)
        vector.drain()
        vector.tensor_copy(f32row2, dst[0:1, :])
        vector.drain()
        ins = vector.tensor_tensor(f32row2, f32row, f32row2, ALU.subtract)
        vector.drain()
        p.inc(ins, 've')
    p.v(mk)

    def place_lo(gpsimd):
        p.wait(gpsimd, 've')
        ins = gpsimd.dma_start(out=dst[1:2, :], in_=f32row2)
        p.inc(ins, 'gd', 16)
    p.g(place_lo)
    return dst


def bias_mm(p, tensor, psum, bias2, c0, c1, start=False, stop=False):
    tensor.matmul(psum, p.ones2[:, 0:psum.shape[0]], bias2[:, c0:c1],
                  start=start, stop=stop)


def emit_load_x(p, x_in, xa):
    def load(sync):
        p.wait(sync, 'st')
        ins = sync.dma_start(out=xa.rearrange("p (t f) -> p t f", t=NT),
                             in_=x_in.rearrange("(t p) f -> p t f", p=128))
        p.inc(ins, 'sd', 16)
    p.y(load)


def emit_ln_stats(p, xa, st_dump, stats, eps, double_eps=None, ntiles=NT):
    sums, sq = stats[:, 0:16], stats[:, 16:32]
    sc, nb = stats[:, 32:48], stats[:, 48:64]

    def pass1(scalar):
        p.wait(scalar, 'sd')
        for t in range(ntiles):
            ins = scalar.activation(out=st_dump, in_=xa[:, 1024 * t:1024 * (t + 1)],
                                    func=AF.Identity, accum_out=sums[:, t:t + 1])
        p.inc(ins, 'se')
    p.s(pass1)

    def negm(vector):
        p.wait(vector, 'se')
        ins = vector.tensor_scalar(sums[:, 0:ntiles], sums[:, 0:ntiles],
                                   -1.0 / 1024.0, None, ALU.mult)
        vector.drain()
        p.inc(ins, 've')
    p.v(negm)

    def pass2(scalar):
        p.wait(scalar, 've')
        for t in range(ntiles):
            ins = scalar.activation(out=st_dump, in_=xa[:, 1024 * t:1024 * (t + 1)],
                                    func=AF.Square, bias=sums[:, t:t + 1], scale=1.0,
                                    accum_out=sq[:, t:t + 1])
        p.inc(ins, 'se')
    p.s(pass2)

    def var_recip(vector):
        p.wait(vector, 'se')
        vector.tensor_scalar(sc[:, 0:ntiles], sq[:, 0:ntiles], 1.0 / 1024.0, eps,
                             ALU.mult, ALU.add)
        vector.drain()
        ins = vector.reciprocal(sc[:, 0:ntiles], sc[:, 0:ntiles])
        vector.drain()
        p.inc(ins, 've')
    p.v(var_recip)

    def sqrt1(scalar):
        p.wait(scalar, 've')
        ins = scalar.activation(out=sc[:, 0:ntiles], in_=sc[:, 0:ntiles],
                                func=AF.Sqrt)
        p.inc(ins, 'se')
    p.s(sqrt1)

    if double_eps is not None:
        def comb(vector):
            p.wait(vector, 'se')
            vector.tensor_scalar(nb[:, 0:ntiles], sq[:, 0:ntiles], 1.0 / 1024.0,
                                 None, ALU.mult)
            vector.drain()
            vector.tensor_tensor(nb[:, 0:ntiles], nb[:, 0:ntiles],
                                 sc[:, 0:ntiles], ALU.mult)
            vector.drain()
            vector.tensor_tensor(nb[:, 0:ntiles], nb[:, 0:ntiles],
                                 sc[:, 0:ntiles], ALU.mult)
            vector.drain()
            vector.tensor_scalar(nb[:, 0:ntiles], nb[:, 0:ntiles], double_eps,
                                 None, ALU.add)
            vector.drain()
            ins = vector.reciprocal(nb[:, 0:ntiles], nb[:, 0:ntiles])
            vector.drain()
            p.inc(ins, 've')
        p.v(comb)

        def sqrt2(scalar):
            p.wait(scalar, 've')
            ins = scalar.activation(out=nb[:, 0:ntiles], in_=nb[:, 0:ntiles],
                                    func=AF.Sqrt)
            p.inc(ins, 'se')
        p.s(sqrt2)

        def comb2(vector):
            p.wait(vector, 'se')
            vector.tensor_tensor(stats[:, 80:80 + ntiles], sums[:, 0:ntiles],
                                 sc[:, 0:ntiles], ALU.mult)
            vector.tensor_copy(stats[:, 64:64 + ntiles], sc[:, 0:ntiles])
            vector.tensor_tensor(sc[:, 0:ntiles], sc[:, 0:ntiles],
                                 nb[:, 0:ntiles], ALU.mult)
            vector.drain()
            ins = vector.tensor_tensor(nb[:, 0:ntiles], sums[:, 0:ntiles],
                                       sc[:, 0:ntiles], ALU.mult)
            vector.drain()
            p.inc(ins, 've')
        p.v(comb2)
    else:
        def finish(vector):
            p.wait(vector, 'se')
            ins = vector.tensor_tensor(nb[:, 0:ntiles], sums[:, 0:ntiles],
                                       sc[:, 0:ntiles], ALU.mult)
            vector.drain()
            p.inc(ins, 've')
        p.v(finish)


def emit_normalize(p, xa, stats, dst_bf, dst_f32=None, ntiles=NT):
    sc, nb = stats[:, 32:48], stats[:, 48:64]

    def norm(scalar):
        p.wait(scalar, 've')
        for t in range(ntiles):
            if dst_bf is not None:
                ins = scalar.activation(out=dst_bf[:, 1024 * t:1024 * (t + 1)],
                                        in_=xa[:, 1024 * t:1024 * (t + 1)],
                                        func=AF.Identity, bias=nb[:, t:t + 1],
                                        scale=sc[:, t:t + 1])
            if dst_f32 is not None:
                ins = scalar.activation(out=dst_f32[:, 1024 * t:1024 * (t + 1)],
                                        in_=xa[:, 1024 * t:1024 * (t + 1)],
                                        func=AF.Identity, bias=nb[:, t:t + 1],
                                        scale=sc[:, t:t + 1])
        p.inc(ins, 'se')
    p.s(norm)


def emit_tm_to_fm(p, src_bf_tm, fm_dst, scr, wait_sem, ntiles=NT, chunks=NCH):
    def to_dram(sync):
        p.wait(sync, wait_sem)
        ins = sync.dma_start(out=scr.rearrange("(t p) f -> p t f", p=128),
                             in_=src_bf_tm.rearrange("p (t f) -> p t f", t=ntiles))
        p.inc(ins, 'sd', 16)
    p.y(to_dram)

    def from_dram(sync):
        p.wait(sync, 'sd')
        for c in range(chunks):
            ins = sync.dma_start_transpose(
                out=fm_dst[:, ntiles * 128 * c:ntiles * 128 * (c + 1)],
                in_=scr[:, 128 * c:128 * (c + 1)])
            p.inc(ins, 'sd', 16)
    p.y(from_dram)


def emit_quant_tile(p, vector, src_f32, dst_bf, q16, q32):
    bm, scl, rs = q16[:, 0:32], q16[:, 32:64], q16[:, 64:96]
    aq = q32[:, 0:1024]
    t1 = q32[:, 1024:2048]
    t2 = q32[:, 2048:3072]
    t3 = q32[:, 3072:4096]
    seg = lambda ap: ap.rearrange("p (n c) -> p n c", c=32)
    vector.tensor_reduce(bm, seg(src_f32), AX.X, ALU.max, apply_absolute_value=True)
    vector.drain()
    vector.tensor_scalar(scl, bm, 1.0 / 7.5, 1e-30, ALU.mult, ALU.max)
    vector.drain()
    vector.reciprocal(rs, scl)
    vector.drain()
    vector.tensor_tensor(seg(aq), seg(src_f32),
                         rs.unsqueeze(2).broadcast_to([128, 32, 32]), ALU.mult)
    vector.tensor_tensor(t3, aq, aq, ALU.mult)
    vector.tensor_scalar(t1, t3, 4.0, None, ALU.is_ge)
    vector.tensor_scalar(t2, t3, 16.0, None, ALU.is_ge)
    vector.tensor_scalar(t3, t1, 0.125, 0.125, ALU.mult, ALU.add)
    vector.tensor_scalar(t2, t2, 1.0, None, ALU.add)
    vector.tensor_tensor(t2, t2, t3, ALU.mult)
    vector.tensor_tensor(t3, aq, aq, ALU.mult)
    vector.tensor_scalar(t3, t3, 16.0, None, ALU.is_ge)
    vector.tensor_scalar(t1, t1, -4.0, 8.0, ALU.mult, ALU.add)
    vector.tensor_scalar(t3, t3, -0.5, 1.0, ALU.mult, ALU.add)
    vector.tensor_tensor(t1, t1, t3, ALU.mult)
    vector.tensor_tensor(t1, aq, t1, ALU.mult)
    vector.tensor_scalar(t3, t1, MAGIC, MAGIC, ALU.add, ALU.subtract)
    vector.tensor_tensor(t1, t1, t3, ALU.subtract)
    vector.tensor_scalar(t1, t1, -0.5, None, ALU.is_equal)
    vector.tensor_tensor(t1, t3, t1, ALU.subtract)
    vector.tensor_tensor(t1, t1, t2, ALU.mult)
    ins = vector.tensor_tensor(seg(dst_bf), seg(t1),
                               scl.unsqueeze(2).broadcast_to([128, 32, 32]),
                               ALU.mult)
    return ins


def emit_store_x(p, xsrc, x_out, wait='vps'):
    def store(sync):
        p.wait(sync, wait)
        ins = sync.dma_start(out=x_out.rearrange("(t p) f -> p t f", p=128),
                             in_=xsrc.rearrange("p (t f) -> p t f", t=NT))
        p.inc(ins, 'st', 16)
    p.y(store)



def emit_l1_ffn(p, ar, x_in, x_out, w, first):
    xa, xb = ar['XA'], ar['XB']
    xc = ar['XCb']
    wdT = ar['WSb'][:, 0:8192]
    stats = ar['stats']
    q16, q32 = ar['q16'], ar['q32']

    emit_load_x(p, x_in, xa)
    b2 = hilo_bias(p, ar, [(w['L1_ffn_bias'], 1024)], 0)

    if first:
        def loadw(sync):
            ins = sync.dma_start(out=xb[:, 0:8192].rearrange("p (o f) -> p o f", o=8),
                                 in_=w['L1_ffn_weight'].rearrange(
                                     "(o p) f -> p o f", p=128))
            p.inc(ins, 'sd', 16)
        p.y(loadw)

        def quantw(vector):
            p.wait(vector, 'sd')
            p.wait(vector, 'gd')
            for o in range(8):
                ins = emit_quant_tile(p, vector, xb[:, 1024 * o:1024 * (o + 1)],
                                      xc[:, 1024 * o:1024 * (o + 1)], q16, q32)
            vector.drain()
            p.inc(ins, 've')
        p.v(quantw)

        def wtod(sync):
            p.wait(sync, 've')
            ins = sync.dma_start(
                out=ar['scr_w'].rearrange("(t p) f -> p t f", p=128),
                in_=xc[:, 0:8192].rearrange("p (t f) -> p t f", t=8))
            p.inc(ins, 'sd', 16)
        p.y(wtod)

    def wfrd(sync):
        p.wait(sync, 'sd')
        for c in range(8):
            ins = sync.dma_start_transpose(
                out=wdT[:, 1024 * c:1024 * (c + 1)],
                in_=ar['scr_w'][:, 128 * c:128 * (c + 1)])
            p.inc(ins, 'sd', 16)
    p.y(wfrd)

    emit_ln_stats(p, xa, ar['dump'], stats, 1e-5)
    emit_normalize(p, xa, stats, None, dst_f32=xb)

    def quantx(vector):
        p.wait(vector, 'se')
        p.wait(vector, 'gd')
        for t in range(NT):
            ins = emit_quant_tile(p, vector, xb[:, 1024 * t:1024 * (t + 1)],
                                  xc[:, 1024 * t:1024 * (t + 1)], q16, q32)
        vector.drain()
        p.inc(ins, 've')
    p.v(quantx)

    emit_tm_to_fm(p, xc[:, 0:16384], xc, ar['scr_tm'], wait_sem='ve')

    sd_ready = p.cnt['sd']
    ve_ready = p.cnt['ve']
    vps_base = p.cnt.get('vps', 0)

    gd_ready = p.cnt.get('gd', 0)

    def gate(tensor):
        p.wait(tensor, 'sd', sd_ready)
        p.wait(tensor, 've', ve_ready)
        p.wait(tensor, 'gd', gd_ready)
        p.wait(tensor, 'gp')
    p.t(gate)

    for t in range(NT):
        pb0 = ar['psf'][(t % 2) * 2 + 0]
        pb1 = ar['psf'][(t % 2) * 2 + 1]

        def mm_t(tensor, t=t, pb0=pb0, pb1=pb1):
            if t >= 2:
                p.wait(tensor, 'vps', vps_base + t - 1)
            for half, pb in ((0, pb0), (1, pb1)):
                bias_mm(p, tensor, pb, b2, 512 * half, 512 * (half + 1),
                        start=True, stop=False)
                for k in range(8):
                    ins = tensor.matmul(
                        pb,
                        xc[:, 2048 * k + 128 * t:2048 * k + 128 * (t + 1)],
                        wdT[:, 1024 * k + 512 * half:1024 * k + 512 * (half + 1)],
                        start=False, stop=(k == 7))
            p.inc(ins, 'pe')
        p.t(mm_t)

        def evac_t(vector, t=t, pb0=pb0, pb1=pb1):
            p.wait(vector, 'pe')
            vector.scalar_tensor_tensor(
                out=xb[:, 1024 * t:1024 * t + 512], in0=pb0, scalar=1.0,
                in1=xa[:, 1024 * t:1024 * t + 512], op0=ALU.mult, op1=ALU.add)
            ins = vector.scalar_tensor_tensor(
                out=xb[:, 1024 * t + 512:1024 * (t + 1)], in0=pb1, scalar=1.0,
                in1=xa[:, 1024 * t + 512:1024 * (t + 1)], op0=ALU.mult,
                op1=ALU.add)
            p.inc(ins, 'vps')
        p.v(evac_t)

    emit_store_x(p, xb, x_out)



def emit_l1_attn_b(p, ar, x_in, x_out, w):
    xa, xb = ar['XA'], ar['XB']
    xc = ar['XCb']
    stats = ar['stats']
    q32 = ar['q32']
    w1b = ar['WSb'][:, 0:8192]
    w23b = ar['w23']

    emit_load_x(p, x_in, xa)
    b1 = hilo_bias(p, ar, [(w['L1_attn_b_b1'], 1024)], 0)
    b23 = hilo_bias(p, ar, [(w['L1_attn_b_b2'], 16), (w['L1_attn_b_b3'], 16)], 1)

    def loadw1(sync):
        p.wait(sync, 'pe')
        ins = sync.dma_start(out=w1b.rearrange("p (k f) -> p k f", k=8),
                             in_=ar['cache']['L1_attn_b_W1'].rearrange(
                                 "(k p) f -> p k f", p=128))
        p.inc(ins, 'sd', 16)
    p.y(loadw1)

    def loadw23(gpsimd):
        ins = gpsimd.dma_start(
            out=w23b[:].rearrange("p (k a f) -> p k a f", k=8, a=2)[:, :, 0, :],
            in_=w['L1_attn_b_W2'].rearrange("(k p) f -> p k f", p=128))
        p.inc(ins, 'gd', 16)
        ins = gpsimd.dma_start(
            out=w23b[:].rearrange("p (k a f) -> p k a f", k=8, a=2)[:, :, 1, :],
            in_=w['L1_attn_b_W3'].rearrange("(k p) f -> p k f", p=128))
        p.inc(ins, 'gd', 16)
    p.g(loadw23)

    emit_ln_stats(p, xa, ar['dump'], stats, 1e-5)
    emit_normalize(p, xa, stats, xc[:, 0:16384], dst_f32=xb)
    emit_tm_to_fm(p, xc[:, 0:16384], xc, ar['scr_tm'], wait_sem='se')

    sd_ready = p.cnt['sd']
    gd_ready = p.cnt['gd']
    ve_ready = p.cnt['ve']
    vps_base = p.cnt.get('vps', 0)

    def gate(tensor):
        p.wait(tensor, 'sd', sd_ready)
        p.wait(tensor, 'gd', gd_ready)
        p.wait(tensor, 've', ve_ready)
        p.wait(tensor, 'gp')
    p.t(gate)

    for t in range(NT):
        pb0 = ar['psf'][(t % 2) * 2 + 0]
        pb1 = ar['psf'][(t % 2) * 2 + 1]
        pbc = ar['psb'][t % 2][:, 0:32]

        def mm_t(tensor, t=t, pb0=pb0, pb1=pb1, pbc=pbc):
            if t >= 2:
                p.wait(tensor, 'vps', vps_base + t - 1)
            for half, pb in ((0, pb0), (1, pb1)):
                bias_mm(p, tensor, pb, b1, 512 * half, 512 * (half + 1),
                        start=True, stop=False)
                for k in range(8):
                    tensor.matmul(
                        pb,
                        xc[:, 2048 * k + 128 * t:2048 * k + 128 * (t + 1)],
                        w1b[:, 1024 * k + 512 * half:1024 * k + 512 * (half + 1)],
                        start=False, stop=(k == 7))
            bias_mm(p, tensor, pbc, b23, 0, 32, start=True, stop=False)
            for k in range(8):
                ins = tensor.matmul(
                    pbc,
                    xc[:, 2048 * k + 128 * t:2048 * k + 128 * (t + 1)],
                    w23b[:, 32 * k:32 * (k + 1)],
                    start=False, stop=(k == 7))
            p.inc(ins, 'pe')
        p.t(mm_t)

        d0 = q32[:, 0:1024]
        d1 = q32[:, 1024:2048]
        d2 = q32[:, 2048:3072]
        rd = ar['q16'][:, 0:16]

        def evac_v(vector, t=t, pb0=pb0, pb1=pb1, pbc=pbc, d0=d0, d1=d1, rd=rd):
            p.wait(vector, 'pe')
            vector.tensor_copy(d0[:, 0:512], pb0)
            vector.tensor_copy(d0[:, 512:1024], pb1)
            vector.tensor_copy(d1[:, 0:32], pbc)
            vector.drain()
            ins = vector.scalar_tensor_tensor(
                out=d1[:, 32:48], in0=d1[:, 0:16], scalar=1.0,
                in1=d1[:, 16:32], op0=ALU.mult, op1=ALU.mult,
                accum_out=rd[:, 0:1])
            vector.drain()
            p.inc(ins, 've')
        p.v(evac_v)

        def absz(vector, d0=d0, d1=d1):
            vector.tensor_scalar(d1, d0, -1.0, None, ALU.mult)
            ins = vector.tensor_tensor(d1, d0, d1, ALU.max)
            p.inc(ins, 've')
        p.v(absz)

        def sp_a(scalar, d1=d1, d2=d2):
            p.wait(scalar, 've')
            scalar.activation(out=d2, in_=d1, func=AF.Exp, scale=-1.0)
            ins = scalar.activation(out=d2, in_=d2, func=AF.Ln, bias=1.0)
            p.inc(ins, 'se')
        p.s(sp_a)

        def fin_v(vector, t=t, d0=d0, d1=d1, d2=d2, rd=rd):
            p.wait(vector, 'se')
            if DEBUG_ATTNB and t == 0:
                vector.tensor_copy(ar['dump'], d0)
                vector.tensor_copy(stats[:, 0:1], rd[:, 0:1])
            vector.tensor_scalar(d0, d0, 0.0, None, ALU.max)
            vector.tensor_tensor(d0, d0, d2, ALU.add)
            vector.scalar_tensor_tensor(
                out=d0, in0=d0, scalar=rd[:, 0:1],
                in1=xb[:, 1024 * t:1024 * (t + 1)], op0=ALU.mult, op1=ALU.mult)
            ins = vector.tensor_tensor(
                xb[:, 1024 * t:1024 * (t + 1)], d0,
                xa[:, 1024 * t:1024 * (t + 1)], ALU.add)
            p.inc(ins, 'vps')
        p.v(fin_v)

    emit_store_x(p, xb, x_out)
    if DEBUG_ATTNB:
        def dbgst(sync):
            p.wait(sync, 'vps')
            ins = sync.dma_start(out=ar['dbg_out'][:, 0:1024], in_=ar['dump'])
            p.inc(ins, 'st', 16)
            ins = sync.dma_start(out=ar['dbg_out'][:, 1024:1056],
                                 in_=stats[:, 0:32])
            p.inc(ins, 'st', 16)
            ins = sync.dma_start(out=ar['dbg2_out'][:, 0:1024],
                                 in_=xb[:, 0:1024])
            p.inc(ins, 'st', 16)
            ins = sync.dma_start(out=ar['dbg3_out'], in_=xc[:, 0:2048])
            p.inc(ins, 'st', 16)
        p.y(dbgst)





FFN_SPECS = [
    ('L0_ffn_w1_w', 'L0_ffn_w1_s', 4096, 1024, 'cT_w1'),
    ('L0_ffn_w2_w', 'L0_ffn_w2_s', 4096, 1024, 'cT_w2'),
    ('L0_ffn_h1_w', 'L0_ffn_h1_s', 4096, 4096, 'cT_h1'),
    ('L0_ffn_h2_w', 'L0_ffn_h2_s', 4096, 4096, 'cT_h2'),
    ('L0_ffn_w3_w', 'L0_ffn_w3_s', 1024, 4096, 'cT_w3'),
]


def emit_scale_load(p, ar, dram_vec, n, dst_cols):
    nch = n // 128
    tmp = ar['s32']
    dst = ar['ssb'][:, dst_cols:dst_cols + nch]

    def ld(gpsimd):
        p.wait(gpsimd, 'pe')
        ins = gpsimd.dma_start(out=tmp[0:nch, :],
                               in_=dram_vec.rearrange("(c q) -> c q", q=128))
        p.inc(ins, 'gd', 16)
    p.g(ld)



    def tr(tensor):
        p.wait(tensor, 'gd')
        p.wait(tensor, 've')
        ins = tensor.transpose(ar['pst'][:, 0:nch], tmp[0:nch, :],
                               ar['ident32'][0:nch, 0:nch])
        p.inc(ins, 'pe')
    p.t(tr)

    def ev(vector):
        p.wait(vector, 'pe')
        ins = vector.tensor_copy(dst, ar['pst'][:, 0:nch])
        p.inc(ins, 've')
    p.v(ev)


def emit_ffn_cache_build(p, ar, w):
    wsb = ar['WSb']
    for wi, (wname, sname, O, K, cname) in enumerate(FFN_SPECS):
        cT = ar[cname]
        craw = ar['c_raw']
        is_w3 = (wname == 'L0_ffn_w3_w')
        for oc in range(O // 128):
            buf = wsb[:, (oc % 2) * 4096:(oc % 2) * 4096 + K]

            def cast(gpsimd, oc=oc, buf=buf, wname=wname, K=K):
                p.wait(gpsimd, 'sd')
                ins = gpsimd.dma_start(
                    out=buf, in_=w[wname][128 * oc:128 * (oc + 1), 0:K])
                p.inc(ins, 'gd', 16)
            p.g(cast)

            if is_w3:
                def scale(vector, oc=oc, buf=buf):
                    p.wait(vector, 'gd')
                    ins = vector.tensor_scalar(
                        buf, buf, ar['ssb'][:, 128 + oc:129 + oc], None, ALU.mult)
                    p.inc(ins, 've')
                p.v(scale)

            def wr(sync, oc=oc, buf=buf, craw=craw, K=K, is_w3=is_w3):
                p.wait(sync, 'gd')
                if is_w3:
                    p.wait(sync, 've')
                ins = sync.dma_start(out=craw[128 * oc:128 * (oc + 1), 0:K],
                                     in_=buf)
                p.inc(ins, 'sd', 16)
            p.y(wr)
        for kc in range(K // 128):
            buf = wsb[:, (kc % 2) * 4096:(kc % 2) * 4096 + O]

            def tr(sync, kc=kc, buf=buf, craw=craw, O=O):
                p.wait(sync, 'sd')
                ins = sync.dma_start_transpose(
                    out=buf, in_=craw[0:O, 128 * kc:128 * (kc + 1)])
                p.inc(ins, 'sd', 16)
                return ins
            p.y(tr)

            def wr2(sync, kc=kc, buf=buf, cT=cT, O=O):
                p.wait(sync, 'sd')
                ins = sync.dma_start(out=cT[128 * kc:128 * (kc + 1), 0:O],
                                     in_=buf)
                p.inc(ins, 'sd', 16)
            p.y(wr2)


def emit_l0_ffn(p, ar, x_in, x_out, w, first):
    xa, xb = ar['XA'], ar['XB']
    xab, xbb = ar['XAb'], ar['XBb']
    xc = ar['XCb']
    wsb = ar['WSb']
    stats = ar['stats']
    q32 = ar['q32']
    ssb = ar['ssb']

    emit_scale_load(p, ar, w['L0_ffn_w1_s'], 4096, 0)
    emit_scale_load(p, ar, w['L0_ffn_w2_s'], 4096, 32)
    emit_scale_load(p, ar, w['L0_ffn_h1_s'], 4096, 64)
    emit_scale_load(p, ar, w['L0_ffn_h2_s'], 4096, 96)
    emit_scale_load(p, ar, w['L0_ffn_w3_s'], 1024, 128)

    if first:
        emit_ffn_cache_build(p, ar, w)

    emit_load_x(p, x_in, xa)
    emit_ln_stats(p, xa, ar['dump'], stats, 1e-5)
    emit_normalize(p, xa, stats, xc[:, 0:16384])
    emit_tm_to_fm(p, xc[:, 0:16384], xc, ar['scr_tm'], wait_sem='se')

    ve_ready = p.cnt['ve']
    vps_base = p.cnt.get('vps', 0)

    for tb in range(2):
        t0 = 1024 * tb
        for og in range(8):
            o0 = 512 * og

            def lda(sync, o0=o0):
                p.wait(sync, 'pe')
                p.wait(sync, 'sd')
                p.wait(sync, 'vps')
                p.wait(sync, 'st')
                ins = sync.dma_start(
                    out=wsb[:, 0:4096].rearrange("p (k o) -> p k o", k=8),
                    in_=ar['cT_w1'].rearrange("(k p) o -> p k o", p=128)[
                        :, :, o0:o0 + 512])
                p.inc(ins, 'sd', 16)
                ins = sync.dma_start(
                    out=wsb[:, 4096:8192].rearrange("p (k o) -> p k o", k=8),
                    in_=ar['cT_w2'].rearrange("(k p) o -> p k o", p=128)[
                        :, :, o0:o0 + 512])
                p.inc(ins, 'sd', 16)
            p.y(lda)

            for sub in range(8):
                oc, g = divmod(sub, 2)
                oabs = 4 * og + oc
                par = sub % 2
                ps1, ps2 = ar['psf'][par * 2], ar['psf'][par * 2 + 1]

                def mma(tensor, oc=oc, g=g, ps1=ps1, ps2=ps2, sub=sub, t0=t0):
                    p.wait(tensor, 'sd')
                    v0 = p.cnt.get('vps', 0)
                    p.wait(tensor, 'vps',
                           max(vps_base, v0 - 1) if v0 > vps_base else 0)
                    for wsl, ps in ((0, ps1), (4096, ps2)):
                        for k in range(8):
                            ins = tensor.matmul(
                                ps,
                                wsb[:, wsl + 512 * k + 128 * oc:
                                    wsl + 512 * k + 128 * (oc + 1)],
                                xc[:, 2048 * k + t0 + 512 * g:
                                    2048 * k + t0 + 512 * (g + 1)],
                                start=(k == 0), stop=(k == 7))
                    p.inc(ins, 'pe')
                p.t(mma)

                def eva_s(scalar, ps1=ps1, oabs=oabs):
                    p.wait(scalar, 'pe')
                    p.wait(scalar, 'vps')
                    ins = scalar.activation(out=ar['u512'], in_=ps1,
                                            func=AF.Silu,
                                            scale=ssb[:, oabs:oabs + 1])
                    p.inc(ins, 'se')
                p.s(eva_s)

                def eva_v(vector, ps2=ps2, oabs=oabs, g=g):
                    p.wait(vector, 'se')
                    ins = vector.scalar_tensor_tensor(
                        out=xbb[:, 1024 * oabs + 512 * g:
                                1024 * oabs + 512 * (g + 1)],
                        in0=ps2, scalar=ssb[:, 32 + oabs:33 + oabs],
                        in1=ar['u512'], op0=ALU.mult, op1=ALU.mult)
                    p.inc(ins, 'vps')
                p.v(eva_v)

        for stage, (hname, scol, src, dst) in enumerate(
                [('cT_h1', 64, xbb, xab), ('cT_h2', 96, xab, xbb)]):
            for og in range(16):
                o0 = 256 * og

                def ldb(sync, o0=o0, hname=hname):
                    p.wait(sync, 'pe')
                    p.wait(sync, 'vps')
                    p.wait(sync, 'sd')
                    ins = sync.dma_start(
                        out=wsb[:, 0:8192].rearrange("p (k o) -> p k o", k=32),
                        in_=ar[hname].rearrange("(k p) o -> p k o", p=128)[
                            :, :, o0:o0 + 256])
                    p.inc(ins, 'sd', 16)
                p.y(ldb)

                for sub in range(4):
                    oc, g = divmod(sub, 2)
                    oabs = 2 * og + oc
                    ps = ar['psf'][sub]

                    def mmb(tensor, oc=oc, g=g, ps=ps, src=src):
                        p.wait(tensor, 'sd')
                        p.wait(tensor, 'se')
                        for k in range(32):
                            ins = tensor.matmul(
                                ps,
                                wsb[:, 256 * k + 128 * oc:256 * k + 128 * (oc + 1)],
                                src[:, 1024 * k + 512 * g:
                                    1024 * k + 512 * (g + 1)],
                                start=(k == 0), stop=(k == 31))
                        p.inc(ins, 'pe')
                    p.t(mmb)

                    def evb(scalar, ps=ps, oabs=oabs, g=g, dst=dst, scol=scol):
                        p.wait(scalar, 'pe')
                        ins = scalar.activation(
                            out=dst[:, 1024 * oabs + 512 * g:
                                    1024 * oabs + 512 * (g + 1)],
                            in_=ps, func=AF.Gelu,
                            scale=ssb[:, scol + oabs:scol + oabs + 1])
                        p.inc(ins, 'se')
                    p.s(evb)

        for th in range(2):
            tt0 = t0 + 512 * th

            def ldx(sync, tt0=tt0):
                p.wait(sync, 'vps')
                p.wait(sync, 've')
                p.wait(sync, 'st')
                p.wait(sync, 'gd')
                ins = sync.dma_start(
                    out=q32.rearrange("p (t f) -> p t f", t=4),
                    in_=x_in[tt0:tt0 + 512, :].rearrange(
                        "(t p) f -> p t f", p=128))
                p.inc(ins, 'sd', 16)
            p.y(ldx)

            for kg in range(4):
                def ldw3(sync, kg=kg):
                    p.wait(sync, 'pe')
                    p.wait(sync, 'sd')
                    if kg == 0:
                        p.wait(sync, 'se')
                    ins = sync.dma_start(
                        out=wsb[:, 0:8192].rearrange("p (k o) -> p k o", k=8),
                        in_=ar['cT_w3'].rearrange("(k p) o -> p k o", p=128)[
                            :, 8 * kg:8 * kg + 8, :])
                    p.inc(ins, 'sd', 16)
                p.y(ldw3)

                def mmd(tensor, kg=kg, tt0=tt0, t0=t0):
                    p.wait(tensor, 'sd')
                    if kg == 0:
                        p.wait(tensor, 'vps')
                    for tt in range(4):
                        for half in range(2):
                            ps = ar['psf'][tt] if half == 0 else ar['psb'][tt]
                            for k8 in range(8):
                                k = 8 * kg + k8
                                ins = tensor.matmul(
                                    ps,
                                    xbb[:, 1024 * k + (tt0 - t0) + 128 * tt:
                                        1024 * k + (tt0 - t0) + 128 * (tt + 1)],
                                    wsb[:, 1024 * k8 + 512 * half:
                                        1024 * k8 + 512 * (half + 1)],
                                    start=(kg == 0 and k8 == 0),
                                    stop=(kg == 3 and k8 == 7))
                    p.inc(ins, 'pe')
                p.t(mmd)

            def evd(vector, tt0=tt0):
                p.wait(vector, 'pe')
                for tt in range(4):
                    for half in range(2):
                        ps = ar['psf'][tt] if half == 0 else ar['psb'][tt]
                        ins = vector.scalar_tensor_tensor(
                            out=q32[:, 1024 * tt + 512 * half:
                                    1024 * tt + 512 * (half + 1)],
                            in0=ps, scalar=1.0,
                            in1=q32[:, 1024 * tt + 512 * half:
                                    1024 * tt + 512 * (half + 1)],
                            op0=ALU.mult, op1=ALU.add)
                p.inc(ins, 'vps')
            p.v(evd)

            def std(sync, tt0=tt0):
                p.wait(sync, 'vps')
                ins = sync.dma_start(
                    out=x_out[tt0:tt0 + 512, :].rearrange(
                        "(t p) f -> p t f", p=128),
                    in_=q32.rearrange("p (t f) -> p t f", t=4))
                p.inc(ins, 'st', 16)
            p.y(std)





def emit_l0_attn_b(p, ar, x_in, x_out, w):
    xa, xb = ar['XA'], ar['XB']
    xab, xbb = ar['XAb'], ar['XBb']
    xc = ar['XCb']
    wsb = ar['WSb']
    stats = ar['stats']
    q32 = ar['q32']
    q32b = ar['q32'].bitcast(BF16)
    ssb = ar['ssb']
    u512 = ar['u512']

    emit_scale_load(p, ar, w['L0_attn_b_gamma1'], 1024, 0)
    emit_scale_load(p, ar, w['L0_attn_b_beta1'], 1024, 8)
    emit_scale_load(p, ar, w['L0_attn_b_b1'], 4096, 16)
    b2 = hilo_bias(p, ar, [(w['L0_attn_b_b2'], 1024)], 0)

    emit_load_x(p, x_in, xa)
    emit_ln_stats(p, xa, ar['dump'], stats, 1e-5, double_eps=1e-6)
    emit_normalize(p, xa, stats, xc[:, 0:16384])
    emit_tm_to_fm(p, xc[:, 0:16384], xc, ar['scr_tm'], wait_sem='se')

    def affine(vector):
        p.wait(vector, 'sd')
        for c in range(NCH):
            ins = vector.tensor_scalar(
                xc[:, 2048 * c:2048 * (c + 1)], xc[:, 2048 * c:2048 * (c + 1)],
                ssb[:, c:c + 1], ssb[:, 8 + c:9 + c], ALU.mult, ALU.add)
        p.inc(ins, 've')
    p.v(affine)


    for wi, wname in enumerate(['L0_attn_b_wk', 'L0_attn_b_wv']):
        def ldw(sync, wname=wname):
            p.wait(sync, 'pe')
            ins = sync.dma_start(
                out=wsb[:, 0:8192].rearrange("p (k o) -> p k o", k=8),
                in_=ar['cache'][wname].rearrange("(k p) o -> p k o", p=128))
            p.inc(ins, 'sd', 16)
        p.y(ldw)

        for t in range(NT):
            ps0, ps1 = ar['psf'][(t % 2) * 2], ar['psf'][(t % 2) * 2 + 1]

            def mmp(tensor, t=t, ps0=ps0, ps1=ps1, wi=wi):
                p.wait(tensor, 'gd')
                p.wait(tensor, 'sd')
                p.wait(tensor, 've')
                if not (wi == 0 and t < 2):
                    p.wait(tensor, 'vps', p.cnt['vps'] - 1)
                for half, ps in ((0, ps0), (1, ps1)):
                    for k in range(8):
                        ins = tensor.matmul(
                            ps, xc[:, 2048 * k + 128 * t:2048 * k + 128 * (t + 1)],
                            wsb[:, 1024 * k + 512 * half:1024 * k + 512 * (half + 1)],
                            start=(k == 0), stop=(k == 7))
                p.inc(ins, 'pe')
            p.t(mmp)

            if wi == 0:
                def evk(vector, t=t, ps0=ps0, ps1=ps1):
                    p.wait(vector, 'pe')
                    vector.tensor_scalar(xbb[:, 1024 * t:1024 * t + 512],
                                         ps0, 0.0, 0.001, ALU.max, ALU.add)
                    ins = vector.tensor_scalar(
                        xbb[:, 1024 * t + 512:1024 * (t + 1)],
                        ps1, 0.0, 0.001, ALU.max, ALU.add)
                    p.inc(ins, 'vps')
                p.v(evk)
            else:
                def evv(vector, t=t, ps0=ps0, ps1=ps1):
                    p.wait(vector, 'pe')
                    vector.tensor_copy(xbb[:, 16384 + 1024 * t:
                                           16384 + 1024 * t + 512], ps0)
                    ins = vector.tensor_copy(
                        xbb[:, 16384 + 1024 * t + 512:16384 + 1024 * (t + 1)],
                        ps1)
                    p.inc(ins, 'vps')
                p.v(evv)

    if S2_PHASES == 2:
        emit_store_x(p, xb, x_out, wait='vps')
        return
    kvs = q32b[0:64, 0:1040]
    klist = []
    for h in range(H):
        psk = ar['psb'][1 + (h % 2)][0:64, 0:65]

        def mmk(tensor, h=h, psk=psk):
            if h < 2:
                p.wait(tensor, 'vps')
                p.wait(tensor, 'gp')
            if h >= 2:
                p.wait(tensor, 've', klist[h - 2])
            for t in range(NT):
                tensor.matmul(
                    psk[:, 0:64],
                    xbb[:, 1024 * t + 64 * h:1024 * t + 64 * (h + 1)],
                    xbb[:, 16384 + 1024 * t + 64 * h:16384 + 1024 * t + 64 * (h + 1)],
                    start=(t == 0), stop=False, skip_group_check=True)
                ins = tensor.matmul(
                    psk[:, 64:65],
                    xbb[:, 1024 * t + 64 * h:1024 * t + 64 * (h + 1)],
                    p.ones128,
                    start=False, stop=(t == NT - 1), skip_group_check=True)
            p.inc(ins, 'pe')
        p.t(mmk)

        def evkv(vector, h=h, psk=psk):
            p.wait(vector, 'pe')
            ins = vector.tensor_copy(kvs[:, 65 * h:65 * (h + 1)], psk)
            p.inc(ins, 've')
        p.v(evkv)
        klist.append(p.cnt['ve'])

    if S2_PHASES == 25:
        def gate25(vector):
            p.wait(vector, 've')
            ins = vector.tensor_copy(xb[0:64, 0:520], kvs[0:64, 0:520])
            p.inc(ins, 'vps')
        p.v(gate25)
        emit_store_x(p, xb, x_out, wait='vps')
        return
    def ldwq(sync):
        p.wait(sync, 'pe')
        p.wait(sync, 'vps')
        ins = sync.dma_start(
            out=wsb[:, 0:8192].rearrange("p (k o) -> p k o", k=8),
            in_=ar['cache']['L0_attn_b_wq'].rearrange("(k p) o -> p k o", p=128))
        p.inc(ins, 'sd', 16)
    p.y(ldwq)

    qpf = q32b[0:64, 1664:3712]
    for t in range(NT):
        ps0, ps1 = ar['psf'][0], ar['psf'][1]
        pstA = ar['psf'][2][:, 0:512].bitcast(BF16)
        pstB = ar['psb'][0][:, 0:512].bitcast(BF16)

        def mmq(tensor, t=t, ps0=ps0, ps1=ps1):
            p.wait(tensor, 'gd')
            p.wait(tensor, 'sd')
            p.wait(tensor, 've')
            for half, ps in ((0, ps0), (1, ps1)):
                for k in range(8):
                    ins = tensor.matmul(
                        ps, xc[:, 2048 * k + 128 * t:2048 * k + 128 * (t + 1)],
                        wsb[:, 1024 * k + 512 * half:1024 * k + 512 * (half + 1)],
                        start=(k == 0), stop=(k == 7))
            p.inc(ins, 'pe')
        p.t(mmq)

        def evq(vector, t=t, ps0=ps0, ps1=ps1):
            p.wait(vector, 'pe')
            vector.tensor_scalar(xbb[:, 1024 * t:1024 * t + 512],
                                 ps0, 0.0, 0.001, ALU.max, ALU.add)
            ins = vector.tensor_scalar(
                xbb[:, 1024 * t + 512:1024 * (t + 1)],
                ps1, 0.0, 0.001, ALU.max, ALU.add)
            p.inc(ins, 've')
        p.v(evq)

        if S2_PHASES == 3.1:
            continue

        def trq(tensor, t=t, pstA=pstA, pstB=pstB):
            p.wait(tensor, 've')
            for h in range(H):
                pst = pstA if h < 8 else pstB
                ins = tensor.transpose(
                    pst[0:64, 128 * (h % 8):128 * (h % 8 + 1)],
                    xbb[:, 1024 * t + 64 * h:1024 * t + 64 * (h + 1)],
                    ar['ident'])
            p.inc(ins, 'pe')
        p.t(trq)

        def evt(vector, t=t, pstA=pstA, pstB=pstB):
            p.wait(vector, 'pe')
            vector.tensor_copy(qpf[:, 0:1024], pstA[0:64, :])
            ins = vector.tensor_copy(qpf[:, 1024:2048], pstB[0:64, :])
            p.inc(ins, 've')
        p.v(evt)

        if S2_PHASES == 3.2:
            continue
        for hg in range(4):
            psn = ar['psb'][3][:, 0:264] if hg % 2 == 0 else ar['psf'][3][:, 0:264]

            def mmn(tensor, t=t, hg=hg, psn=psn):
                p.wait(tensor, 've')
                for hh in range(4):
                    h = 4 * hg + hh
                    ins = tensor.matmul(
                        psn[:, 66 * hh:66 * hh + 65],
                        qpf[:, 128 * h:128 * (h + 1)],
                        kvs[:, 65 * h:65 * (h + 1)],
                        start=(hh == 0), stop=(hh == 3), skip_group_check=True)
                p.inc(ins, 'pe')
            p.t(mmn)

            def evn(vector, t=t, hg=hg, psn=psn):
                p.wait(vector, 'pe')
                if S2_PHASES == 3.3:
                    ins = vector.tensor_copy(u512[:, 0:64], u512[:, 64:128])
                    p.inc(ins, 've')
                    return
                if S2_PHASES == 3.4:
                    for hh in range(4):
                        h = 4 * hg + hh
                        ins = vector.tensor_scalar(
                            xbb[:, 1024 * t + 64 * h:1024 * t + 64 * (h + 1)],
                            psn[:, 66 * hh:66 * hh + 64],
                            1.0, None, ALU.mult)
                    p.inc(ins, 've')
                    return
                for hh in range(4):
                    h = 4 * hg + hh
                    vector.reciprocal(u512[:, h:h + 1],
                                      psn[:, 66 * hh + 64:66 * hh + 65])
                vector.drain()
                for hh in range(4):
                    h = 4 * hg + hh
                    ins = vector.tensor_scalar(
                        xbb[:, 1024 * t + 64 * h:1024 * t + 64 * (h + 1)],
                        psn[:, 66 * hh:66 * hh + 64],
                        u512[:, h:h + 1], None, ALU.mult)
                p.inc(ins, 've')
            p.v(evn)

    if S2_PHASES in (3, 3.1, 3.2, 3.3, 3.4):
        def gate3(vector):
            p.wait(vector, 've')
            ins = vector.tensor_copy(xb[:, 0:1024], xab[:, 0:1024])
            p.inc(ins, 'vps')
        p.v(gate3)
        emit_store_x(p, xb, x_out, wait='vps')
        return

    def ldwo(sync):
        p.wait(sync, 'pe')
        p.wait(sync, 've')
        ins = sync.dma_start(
            out=wsb[:, 0:8192].rearrange("p (k o) -> p k o", k=8),
            in_=ar['cache']['L0_attn_b_wo'].rearrange("(k p) o -> p k o", p=128))
        p.inc(ins, 'sd', 16)
    p.y(ldwo)

    attf = q32b[:, 1664:2688]
    for t in range(NT):
        pst = ar['psf'][2][:, 0:512].bitcast(BF16)

        def tra(tensor, t=t, pst=pst):
            p.wait(tensor, 've')
            p.wait(tensor, 'gd')
            for pr in range(8):
                ins = tensor.transpose(
                    pst[:, 128 * pr:128 * (pr + 1)],
                    xbb[:, 1024 * t + 128 * pr:1024 * t + 128 * (pr + 1)],
                    ar['ident'])
            p.inc(ins, 'pe')
        p.t(tra)

        def eva(vector, t=t, pst=pst):
            p.wait(vector, 'pe')
            ins = vector.tensor_copy(attf, pst)
            p.inc(ins, 've')
        p.v(eva)

        ps0, ps1 = ar['psf'][0], ar['psf'][1]

        def mmo(tensor, t=t, ps0=ps0, ps1=ps1):
            p.wait(tensor, 've')
            p.wait(tensor, 'sd')
            if t >= 1:
                p.wait(tensor, 'vps', p.cnt['vps'] - 0)
            for half, ps in ((0, ps0), (1, ps1)):
                for k in range(8):
                    ins = tensor.matmul(
                        ps, attf[:, 128 * k:128 * (k + 1)],
                        wsb[:, 1024 * k + 512 * half:1024 * k + 512 * (half + 1)],
                        start=(k == 0), stop=(k == 7))
            p.inc(ins, 'pe')
        p.t(mmo)

        def evo(vector, t=t, ps0=ps0, ps1=ps1):
            p.wait(vector, 'pe')
            vector.tensor_scalar(xa[:, 1024 * t:1024 * t + 512],
                                 xa[:, 1024 * t:1024 * t + 512],
                                 stats[:, 64 + t:65 + t], stats[:, 80 + t:81 + t],
                                 ALU.mult, ALU.add)
            vector.tensor_scalar(xa[:, 1024 * t + 512:1024 * (t + 1)],
                                 xa[:, 1024 * t + 512:1024 * (t + 1)],
                                 stats[:, 64 + t:65 + t], stats[:, 80 + t:81 + t],
                                 ALU.mult, ALU.add)
            vector.tensor_tensor(xa[:, 1024 * t:1024 * t + 512],
                                 xa[:, 1024 * t:1024 * t + 512], ps0, ALU.add)
            ins = vector.tensor_tensor(
                xa[:, 1024 * t + 512:1024 * (t + 1)],
                xa[:, 1024 * t + 512:1024 * (t + 1)], ps1, ALU.add)
            p.inc(ins, 'vps')
        p.v(evo)

    if S2_PHASES <= 4.5:
        emit_store_x(p, xb, x_out, wait='vps')
        return
    def gate5(scalar):
        p.wait(scalar, 'vps')
    p.s(gate5)
    emit_ln_stats(p, xa, ar['dump'], stats, 1e-6)
    emit_normalize(p, xa, stats, xc[:, 0:16384])
    emit_tm_to_fm(p, xc[:, 0:16384], xc, ar['scr_tm'], wait_sem='se')

    def affine2(vector):
        p.wait(vector, 'sd')
        for c in range(NCH):
            ins = vector.tensor_scalar(
                xc[:, 2048 * c:2048 * (c + 1)], xc[:, 2048 * c:2048 * (c + 1)],
                ssb[:, c:c + 1], ssb[:, 8 + c:9 + c], ALU.mult, ALU.add)
        p.inc(ins, 've')
    p.v(affine2)

    if S2_PHASES <= 5.5:
        emit_store_x(p, xb, x_out, wait='vps')
        return
    for tb in range(2):
        t0 = 1024 * tb
        for og in range(8):
            o0 = 512 * og

            def ldw1(sync, o0=o0):
                p.wait(sync, 'pe')
                p.wait(sync, 've')
                p.wait(sync, 'se')
                p.wait(sync, 'vps')
                ins = sync.dma_start(
                    out=wsb[:, 0:4096].rearrange("p (k o) -> p k o", k=8),
                    in_=ar['cache']['L0_attn_b_w1'].rearrange(
                        "(k p) o -> p k o", p=128)[:, :, o0:o0 + 512])
                p.inc(ins, 'sd', 16)
            p.y(ldw1)

            for sub in range(8):
                oc, g = divmod(sub, 2)
                oabs = 4 * og + oc
                ps = ar['psf'][sub % 4]

                def mm1(tensor, oc=oc, g=g, ps=ps, t0=t0):
                    p.wait(tensor, 'gd')
                    p.wait(tensor, 'sd')
                    p.wait(tensor, 'se')
                    for k in range(8):
                        ins = tensor.matmul(
                            ps,
                            wsb[:, 512 * k + 128 * oc:512 * k + 128 * (oc + 1)],
                            xc[:, 2048 * k + t0 + 512 * g:
                                2048 * k + t0 + 512 * (g + 1)],
                            start=(k == 0), stop=(k == 7))
                    p.inc(ins, 'pe')
                p.t(mm1)

                def ev1(scalar, ps=ps, oabs=oabs, g=g):
                    p.wait(scalar, 'pe')
                    ins = scalar.activation(
                        out=xbb[:, 1024 * oabs + 512 * g:
                                1024 * oabs + 512 * (g + 1)],
                        in_=ps, func=AF.Gelu,
                        scale=1.0, bias=ssb[:, 16 + oabs:17 + oabs])
                    p.inc(ins, 'se')
                p.s(ev1)

        if S2_PHASES <= 6:
            continue
        for th in range(2):
            tt0 = t0 + 512 * th

            def ldx2(sync, tt0=tt0):
                if S2_V7 == 15:
                    p.wait(sync, 'sd')
                if S2_V7 not in (10, 15):
                    p.wait(sync, 'vps')
                if S2_V7 not in (10, 11, 15):
                    p.wait(sync, 've')
                if S2_V7 not in (10, 11, 12, 15):
                    p.wait(sync, 'st')
                if S2_V7 == 9:
                    return
                dst = ar['XB'][:, 0:4096] if S2_V7 == 7 else q32
                ins = sync.dma_start(
                    out=dst.rearrange("p (t f) -> p t f", t=4),
                    in_=x_in[tt0:tt0 + 512, :].rearrange(
                        "(t p) f -> p t f", p=128))
                p.inc(ins, 'sd', 16)
            p.y(ldx2)

            if S2_V7 >= 6:
                def st2b(sync, tt0=tt0):
                    p.wait(sync, 'sd')
                    src = q32
                    if S2_V7 in (8, 10, 11, 12, 15):
                        pass
                    else:
                        ins = sync.dma_start(
                            out=x_out[tt0:tt0 + 512, :].rearrange(
                                "(t p) f -> p t f", p=128),
                            in_=src.rearrange("p (t f) -> p t f", t=4))
                        p.inc(ins, 'st', 16)
                p.y(st2b)
                continue
            for kg in range(4):
                def ldw2(sync, kg=kg):
                    p.wait(sync, 'pe')
                    p.wait(sync, 'sd')
                    if kg == 0:
                        p.wait(sync, 'se')
                    ins = sync.dma_start(
                        out=wsb[:, 0:8192].rearrange("p (k o) -> p k o", k=8),
                        in_=ar['cache']['L0_attn_b_w2'].rearrange(
                            "(k p) o -> p k o", p=128)[:, 8 * kg:8 * kg + 8, :])
                    p.inc(ins, 'sd', 16)
                p.y(ldw2)

                def mm2(tensor, kg=kg, tt0=tt0, t0=t0):
                    p.wait(tensor, 'gd')
                    p.wait(tensor, 'sd')
                    if kg == 0:
                        p.wait(tensor, 'vps')
                        p.wait(tensor, 'se')
                    if S2_V7 == 5:
                        ps = ar['psf'][0]
                        ins = tensor.matmul(
                            ps, xab[:, 0:128], wsb[:, 0:512],
                            start=True, stop=(kg == 3))
                        p.inc(ins, 'pe')
                        return
                    for tt in range(4):
                        for half in range(2):
                            ps = ar['psf'][tt] if half == 0 else ar['psb'][tt]
                            for k8 in range(8):
                                k = 8 * kg + k8
                                ins = tensor.matmul(
                                    ps,
                                    xbb[:, 1024 * k + (tt0 - t0) + 128 * tt:
                                        1024 * k + (tt0 - t0) + 128 * (tt + 1)],
                                    wsb[:, 1024 * k8 + 512 * half:
                                        1024 * k8 + 512 * (half + 1)],
                                    start=(kg == 0 and k8 == 0),
                                    stop=(S2_V7 == 1 and kg == 3 and k8 == 7))
                    p.inc(ins, 'pe')
                p.t(mm2)

            if S2_V7 != 1:
                def mmb2(tensor, tt0=tt0):
                    for tt in range(4):
                        for half in range(2):
                            ps = ar['psf'][tt] if half == 0 else ar['psb'][tt]
                            ins = tensor.matmul(
                                ps, p.ones2[:, 0:128],
                                b2[:, 512 * half:512 * (half + 1)],
                                start=False, stop=True)
                    p.inc(ins, 'pe')
                p.t(mmb2)

            for rnd in range(4):
                def evg(scalar, rnd=rnd):
                    p.wait(scalar, 'pe')
                    if rnd >= 1:
                        p.wait(scalar, 'vps')
                    for j in range(2):
                        q = 2 * rnd + j
                        tt, half = divmod(q, 2)
                        ps = ar['psf'][tt] if half == 0 else ar['psb'][tt]
                        if S2_V7 == 2:
                            ins = scalar.copy(
                                ar['g512'][:, 512 * j:512 * (j + 1)], ps)
                        else:
                            ins = scalar.activation(
                                out=ar['g512'][:, 512 * j:512 * (j + 1)],
                                in_=ps, func=AF.Gelu)
                    p.inc(ins, 'se')
                p.s(evg)

                def evs(vector, rnd=rnd, tt0=tt0):
                    p.wait(vector, 'se')
                    for j in range(2):
                        q = 2 * rnd + j
                        col = 512 * q
                        if S2_V7 == 3:
                            ins2 = vector.tensor_copy(
                                q32[:, col:col + 512],
                                ar['g512'][:, 512 * j:512 * (j + 1)])
                            continue
                        vector.tensor_tensor(
                            q32[:, col:col + 512], q32[:, col:col + 512],
                            ar['g512'][:, 512 * j:512 * (j + 1)], ALU.add)
                        xbase = (tt0 // 128) * 1024
                        ins2 = vector.tensor_tensor(
                            q32[:, col:col + 512], q32[:, col:col + 512],
                            xa[:, xbase + col:xbase + col + 512], ALU.add)
                    p.inc(ins2, 'vps')
                p.v(evs)

            def st2(sync, tt0=tt0):
                p.wait(sync, 'vps')
                ins = sync.dma_start(
                    out=x_out[tt0:tt0 + 512, :].rearrange(
                        "(t p) f -> p t f", p=128),
                    in_=q32.rearrange("p (t f) -> p t f", t=4))
                p.inc(ins, 'st', 16)
            p.y(st2)






def emit_l1_attn_a(p, ar, x_in, x_out, w):
    xa = ar['XA']
    xab, xbb = ar['XAb'], ar['XBb']
    xcb = ar['XCb']
    wsb = ar['WSb']
    stats = ar['stats']
    q32 = ar['q32']
    q32b = ar['q32'].bitcast(BF16)
    ssb = ar['ssb']
    u512 = ar['u512']
    NEG = -10000.0

    emit_scale_load(p, ar, w['L1_attn_a_Wqkv_b'][0:1024], 1024, 0)
    emit_scale_load(p, ar, w['L1_attn_a_Wqkv_b'][1024:2048], 1024, 8)
    bv = hilo_bias(p, ar, [(w['L1_attn_a_Wqkv_b'][2048:3072], 1024)], 0)
    bo = hilo_bias(p, ar, [(w['L1_attn_a_out_b'], 1024)], 1)

    def scale_kbias(vector):
        p.wait(vector, 've')
        ins = vector.tensor_scalar(ssb[:, 8:16], ssb[:, 8:16], 0.125, None,
                                   ALU.mult)
        p.inc(ins, 've')
    p.v(scale_kbias)

    wsrc = w['L1_attn_a_Wqkv_w']
    for oc in range(24):
        buf = wsb[:, (oc % 2) * 4096:(oc % 2) * 4096 + 1024]

        def cast(gpsimd, oc=oc, buf=buf):
            p.wait(gpsimd, 'sd')
            ins = gpsimd.dma_start(out=buf,
                                   in_=wsrc[128 * oc:128 * (oc + 1), :])
            p.inc(ins, 'gd', 16)
        p.g(cast)

        def wr(sync, oc=oc, buf=buf):
            p.wait(sync, 'gd')
            ins = sync.dma_start(
                out=ar['c_raw'][128 * oc:128 * (oc + 1), 0:1024], in_=buf)
            p.inc(ins, 'sd', 16)
        p.y(wr)
    for kc in range(8):
        buf = wsb[:, (kc % 2) * 4096:(kc % 2) * 4096 + 3072]

        def tr(sync, kc=kc, buf=buf):
            p.wait(sync, 'sd')
            ins = sync.dma_start_transpose(
                out=buf, in_=ar['c_raw'][0:3072, 128 * kc:128 * (kc + 1)])
            p.inc(ins, 'sd', 16)
        p.y(tr)

        def wr2(sync, kc=kc, buf=buf):
            p.wait(sync, 'sd')
            ins = sync.dma_start(out=ar['cT_w1'][128 * kc:128 * (kc + 1), 0:3072],
                                 in_=buf)
            p.inc(ins, 'sd', 16)
        p.y(wr2)
    wosrc = w['L1_attn_a_out_w']
    for oc in range(8):
        buf = wsb[:, (oc % 2) * 4096:(oc % 2) * 4096 + 1024]

        def cast2(gpsimd, oc=oc, buf=buf):
            p.wait(gpsimd, 'sd')
            ins = gpsimd.dma_start(out=buf,
                                   in_=wosrc[128 * oc:128 * (oc + 1), :])
            p.inc(ins, 'gd', 16)
        p.g(cast2)

        def wr3(sync, oc=oc, buf=buf):
            p.wait(sync, 'gd')
            ins = sync.dma_start(
                out=ar['scr_w'][128 * oc:128 * (oc + 1), 0:1024], in_=buf)
            p.inc(ins, 'sd', 16)
        p.y(wr3)
    for kc in range(8):
        buf = wsb[:, (kc % 2) * 4096:(kc % 2) * 4096 + 1024]

        def tr2(sync, kc=kc, buf=buf):
            p.wait(sync, 'sd')
            ins = sync.dma_start_transpose(
                out=buf, in_=ar['scr_w'][0:1024, 128 * kc:128 * (kc + 1)])
            p.inc(ins, 'sd', 16)
        p.y(tr2)

        def wr4(sync, kc=kc, buf=buf):
            p.wait(sync, 'sd')
            ins = sync.dma_start(out=ar['cT_w2'][128 * kc:128 * (kc + 1), 0:1024],
                                 in_=buf)
            p.inc(ins, 'sd', 16)
        p.y(wr4)

    emit_load_x(p, x_in, xa)
    emit_ln_stats(p, xa, ar['dump'], stats, 1e-5)
    emit_normalize(p, xa, stats, xbb[:, 16384:32768])
    emit_tm_to_fm(p, xbb[:, 16384:32768], xbb[:, 16384:32768], ar['scr_tm'],
                  wait_sem='se')
    xn = xbb[:, 16384:32768]

    def ldwv(sync):
        p.wait(sync, 'pe')
        ins = sync.dma_start(
            out=wsb[:, 0:8192].rearrange("p (k o) -> p k o", k=8),
            in_=ar['cT_w1'].rearrange("(k p) o -> p k o", p=128)[:, :, 2048:3072])
        p.inc(ins, 'sd', 16)
    p.y(ldwv)

    def ones_v(gpsimd):
        p.wait(gpsimd, 'se')
        p.wait(gpsimd, 've')
        ins = gpsimd.memset(xab[:, 0:16640], 1.0)
        p.inc(ins, 'gp')
    p.g(ones_v)

    for t in range(NT):
        ps0, ps1 = ar['psf'][(t % 2) * 2], ar['psf'][(t % 2) * 2 + 1]

        def mmv(tensor, t=t, ps0=ps0, ps1=ps1):
            p.wait(tensor, 'sd')
            p.wait(tensor, 'gp')
            if t >= 2:
                p.wait(tensor, 'vps', p.cnt['vps'] - 1)
            for half, ps in ((0, ps0), (1, ps1)):
                bias_mm(p, tensor, ps, bv, 512 * half, 512 * (half + 1),
                        start=True, stop=False)
                for k in range(8):
                    ins = tensor.matmul(
                        ps, xn[:, 2048 * k + 128 * t:2048 * k + 128 * (t + 1)],
                        wsb[:, 1024 * k + 512 * half:1024 * k + 512 * (half + 1)],
                        start=False, stop=(k == 7))
            p.inc(ins, 'pe')
        p.t(mmv)

        def evv(vector, t=t, ps0=ps0, ps1=ps1):
            p.wait(vector, 'pe')
            for half, ps in ((0, ps0), (1, ps1)):
                ins = vector.tensor_copy(
                    _vslots(xab, t, half),
                    ps.rearrange("p (h e) -> p h e", h=8))
            p.inc(ins, 'vps')
        p.v(evv)

    for which, (cols, dst, biascol, scl) in enumerate(
            [(slice(0, 1024), xbb[:, 0:16384], 0, 1.0),
             (slice(1024, 2048), xcb[:, 0:16384], 8, 0.125)]):
        def ldwq(sync, cols=cols):
            p.wait(sync, 'pe')
            ins = sync.dma_start(
                out=wsb[:, 0:8192].rearrange("p (k o) -> p k o", k=8),
                in_=ar['cT_w1'].rearrange("(k p) o -> p k o", p=128)[:, :, cols])
            p.inc(ins, 'sd', 16)
        p.y(ldwq)

        for sub in range(32):
            oc, g = divmod(sub, 4)
            ps = ar['psf'][sub % 2]

            def mmq(tensor, oc=oc, g=g, ps=ps, sub=sub, which=which):
                p.wait(tensor, 'sd')
                if not (which == 0 and sub < 2):
                    p.wait(tensor, 'se', p.cnt['se'] - 1)
                if which == 0 and sub < 2:
                    p.wait(tensor, 'vps')
                for k in range(8):
                    ins = tensor.matmul(
                        ps, wsb[:, 1024 * k + 128 * oc:1024 * k + 128 * (oc + 1)],
                        xn[:, 2048 * k + 512 * g:2048 * k + 512 * (g + 1)],
                        start=(k == 0), stop=(k == 7))
                p.inc(ins, 'pe')
            p.t(mmq)

            def evq(scalar, oc=oc, g=g, ps=ps, dst=dst, biascol=biascol, scl=scl):
                p.wait(scalar, 'pe')
                ins = scalar.activation(
                    out=dst[:, 2048 * oc + 512 * g:2048 * oc + 512 * (g + 1)],
                    in_=ps, func=AF.Identity, scale=scl,
                    bias=ssb[:, biascol + oc:biascol + oc + 1])
                p.inc(ins, 'se')
            p.s(evq)

    sq = q32b[:, 0:2048]
    mq2 = q32[0:1, 3872:3888]
    mk2 = q32[0:1, 3888:3904]
    mv2 = q32[0:1, 3904:3920]
    m1r = q32[0:1, 3920:3936]
    m2r = q32[0:1, 3936:3952]

    def normqk(which, src, dstrow):
        for c in range(NCH):
            def sqr(scalar, c=c, src=src):
                p.wait(scalar, 've')
                if c == 0:
                    p.wait(scalar, 'vps')
                ins = scalar.activation(out=sq, in_=src[:, 2048 * c:2048 * (c + 1)],
                                        func=AF.Square)
                p.inc(ins, 'se')
            p.s(sqr)
            for hh in range(2):
                h = 2 * c + hh
                for g in range(4):
                    def mmn(tensor, hh=hh, g=g):
                        p.wait(tensor, 'se')
                        p.wait(tensor, 've')
                        ins = tensor.matmul(
                            ar['psb'][3][0:1, 0:512],
                            ar['ones128'][64 * hh:64 * hh + 64, 0:1],
                            sq[64 * hh:64 * hh + 64, 512 * g:512 * (g + 1)],
                            start=True, stop=True)
                        p.inc(ins, 'pe')
                    p.t(mmn)

                    def redm(vector, h=h, g=g, dstrow=dstrow):
                        p.wait(vector, 'pe')
                        vector.tensor_reduce(u512[0:1, 0:1],
                                             ar['psb'][3][0:1, 0:512],
                                             AX.X, ALU.max)
                        vector.drain()
                        if g == 0:
                            ins = vector.tensor_copy(dstrow[:, h:h + 1],
                                                     u512[0:1, 0:1])
                        else:
                            ins = vector.tensor_tensor(
                                dstrow[:, h:h + 1], dstrow[:, h:h + 1],
                                u512[0:1, 0:1], ALU.max)
                        vector.drain()
                        p.inc(ins, 've')
                    p.v(redm)

    normqk(0, xbb[:, 0:16384], mq2)
    normqk(1, xcb[:, 0:16384], mk2)

    vn = q32[:, 3584:3840]

    def vnorm(vector):
        p.wait(vector, 'vps')
        for t in range(NT):
            for h in range(H):
                sl = _vslot(xab, t, h)
                vector.scalar_tensor_tensor(
                    out=u512[:, 0:64], in0=sl, scalar=1.0, in1=sl,
                    op0=ALU.mult, op1=ALU.mult,
                    accum_out=vn[:, 16 * h + t:16 * h + t + 1])
        ins = vector.tensor_reduce(
            q32[:, 3840:3856],
            vn.rearrange("p (h t) -> p h t", h=16), AX.X, ALU.max)
        vector.drain()
        p.inc(ins, 've')
    p.v(vnorm)

    def vtr(tensor):
        p.wait(tensor, 've')
        ins = tensor.transpose(ar['psb'][3][0:16, 0:128], q32[:, 3840:3856],
                               ar['ident32'])
        p.inc(ins, 'pe')
    p.t(vtr)

    def vred(vector):
        p.wait(vector, 'pe')
        ins = vector.tensor_reduce(q32[0:16, 3856:3857],
                                   ar['psb'][3][0:16, 0:128], AX.X, ALU.max)
        vector.drain()
        p.inc(ins, 've')
    p.v(vred)

    def vtr2(tensor):
        p.wait(tensor, 've')
        ins = tensor.transpose(ar['psb'][3][0:1, 0:16], q32[0:16, 3856:3857],
                               ar['ident32'][0:16, 0:16])
        p.inc(ins, 'pe')
    p.t(vtr2)

    def vred2(vector):
        p.wait(vector, 'pe')
        ins = vector.tensor_copy(mv2[:, 0:16], ar['psb'][3][0:1, 0:16])
        vector.drain()
        p.inc(ins, 've')
    p.v(vred2)

    nm1b = q32b[0:1, 7936:7952]
    nm2b = q32b[0:1, 7952:7968]

    def mkm(scalar):
        p.wait(scalar, 've')
        scalar.activation(out=mq2, in_=mq2, func=AF.Sqrt)
        scalar.activation(out=mk2, in_=mk2, func=AF.Sqrt)
        ins = scalar.activation(out=mv2, in_=mv2, func=AF.Sqrt)
        p.inc(ins, 'se')
    p.s(mkm)

    def mkm2(vector):
        p.wait(vector, 'se')
        vector.tensor_tensor(m1r, mq2, mk2, ALU.mult)
        vector.tensor_tensor(m2r, mv2, mk2, ALU.mult)
        vector.drain()
        vector.tensor_scalar(m1r, m1r, -1.0, None, ALU.mult)
        vector.tensor_scalar(m2r, m2r, -1.0, None, ALU.mult)
        vector.drain()
        vector.tensor_copy(nm1b, m1r)
        ins = vector.tensor_copy(nm2b, m2r)
        vector.drain()
        p.inc(ins, 've')
    p.v(mkm2)

    reps = [(xbb[:, 0:16384], xbb[:, 16384:32768], nm1b),
            (xbb[:, 16384:32768], xbb[:, 0:16384], nm2b),
            (xbb[:, 0:16384], xbb[:, 16384:32768], nm2b)]
    mbf = q32[:, 3968:3969]
    dscb = q32b[0:1, 2048:2560]
    rbsb = q32[0:64, 1536:2048]
    pstage = [q32b[:, 512 * i:512 * (i + 1)] for i in range(2)]
    last_rd = [0, 0]
    vemk_ctx = [0]

    for rep, (qsrc, cdst, nmb) in enumerate(reps):
        for h in range(H):
            hb = 64 * (h % 2)
            qh = qsrc[hb:hb + 64 if hb == 0 else 128,
                      2048 * (h // 2):2048 * (h // 2 + 1)] if False else \
                qsrc[hb:hb + 64, 2048 * (h // 2):2048 * (h // 2 + 1)]
            kh = xcb[hb:hb + 64, 2048 * (h // 2):2048 * (h // 2 + 1)]

            def bcm(tensor, h=h, nmb=nmb):
                p.wait(tensor, 've')
                ins = tensor.matmul(ar['psb'][2][:, 0:1],
                                    p.ones2[0:1, 0:128], nmb[:, h:h + 1],
                                    start=True, stop=True)
                p.inc(ins, 'pe')
            p.t(bcm)

            def bcme(vector):
                p.wait(vector, 'pe')
                ins = vector.tensor_copy(mbf, ar['psb'][2][:, 0:1])
                vector.drain()
                p.inc(ins, 've')
            p.v(bcme)
            vemk_h = p.cnt['ve']

            for qb in range(4):
                nst = 4 * (qb + 1)
                pscs = (ar['psf'][2], ar['psf'][3])
                pctx = ar['psb'][3][0:65, 0:512]

                def emit_av(st0, semv, h=h, qb=qb, nst=nst, pctx=pctx):
                    diag_off = max(0, 128 * (st0 - 4 * qb))
                    vctx = vemk_ctx[0] if st0 == 0 else 0

                    def mmav(tensor, st0=st0, semv=semv, slot=st0 % 2,
                             diag_off=diag_off, pctx=pctx, h=h, nst=nst,
                             vctx=vctx):
                        p.wait(tensor, 'se', semv)
                        if st0 == 0:
                            p.wait(tensor, 've', vctx)
                        ins = tensor.matmul(
                            pctx[:, diag_off:512],
                            _vslot65(xab, st0, h),
                            pstage[slot][:, diag_off:512],
                            start=(st0 == 0), stop=(st0 == nst - 1),
                            skip_group_check=True)
                        p.inc(ins, 'pe')
                    p.t(mmav)

                pend_av = []
                for st in range(nst):
                    bank = st % 2

                    def mms(tensor, st=st, qb=qb, qh=qh, kh=kh,
                            psc=pscs[bank], rd=last_rd[bank]):
                        if st == 0 and qb == 0:
                            p.wait(tensor, 've')
                            p.wait(tensor, 'se')
                        p.wait(tensor, 'se', rd)
                        ins = tensor.matmul(
                            psc, kh[:, 128 * st:128 * (st + 1)],
                            qh[:, 512 * qb:512 * (qb + 1)],
                            start=True, stop=True)
                        p.inc(ins, 'pe')
                    p.t(mms)
                    pemk = p.cnt['pe']

                    if st >= 4 * qb:
                        def trimask(vector, st=st, qb=qb, psc=pscs[bank],
                                    pv=pemk):
                            p.wait(vector, 'pe', pv)
                            co = 128 * (st - 4 * qb)
                            ins = vector.tensor_tensor(
                                psc[:, co:co + 128], psc[:, co:co + 128],
                                ar['ctri'], ALU.add)
                            p.inc(ins, 've')
                        p.v(trimask)
                        vemk = p.cnt['ve']
                    else:
                        vemk = vemk_h

                    def expp(scalar, psc=pscs[bank], slot=st % 2, pv=pemk,
                             vv=vemk, do=max(0, 128 * (st - 4 * qb))):
                        p.wait(scalar, 'pe', pv)
                        p.wait(scalar, 've', vv)
                        ins = scalar.activation(
                            out=pstage[slot][:, do:512], in_=psc[:, do:512],
                            func=AF.Exp, bias=mbf[:, 0:1])
                        p.inc(ins, 'se')
                    p.s(expp)
                    last_rd[bank] = p.cnt['se']
                    pend_av.append((st, p.cnt['se']))
                    if st >= 1:
                        emit_av(*pend_av.pop(0))
                while pend_av:
                    emit_av(*pend_av.pop(0))

                def evctx(vector, h=h, qb=qb, pctx=pctx, cdst=cdst):
                    p.wait(vector, 'pe')
                    vector.reciprocal(u512[0:1, 0:512], pctx[64:65, :])
                    vector.drain()
                    ins = vector.tensor_copy(dscb, u512[0:1, 0:512])
                    vector.drain()
                    p.inc(ins, 've')
                p.v(evctx)

                def bcd(tensor):
                    p.wait(tensor, 've')
                    ins = tensor.matmul(ar['psb'][2][0:64, 0:512],
                                        p.ones2[0:1, 0:64], dscb,
                                        start=True, stop=True)
                    p.inc(ins, 'pe')
                p.t(bcd)

                def evctx2(vector, h=h, qb=qb, pctx=pctx, cdst=cdst):
                    p.wait(vector, 'pe')
                    vector.tensor_copy(rbsb, ar['psb'][2][0:64, 0:512])
                    vector.drain()
                    hb = 64 * (h % 2)
                    ins = vector.scalar_tensor_tensor(
                        out=cdst[hb:hb + 64,
                                 2048 * (h // 2) + 512 * qb:
                                 2048 * (h // 2) + 512 * (qb + 1)],
                        in0=pctx[0:64, :], scalar=1.0, in1=rbsb,
                        op0=ALU.mult, op1=ALU.mult)
                    p.inc(ins, 've')
                p.v(evctx2)
                vemk_ctx[0] = p.cnt['ve']

    ctx = xbb[:, 16384:32768]

    def ldwo(sync):
        p.wait(sync, 'pe')
        ins = sync.dma_start(
            out=wsb[:, 0:8192].rearrange("p (k o) -> p k o", k=8),
            in_=ar['cT_w2'].rearrange("(k p) o -> p k o", p=128)[0:128, 0:8, 0:1024])
        p.inc(ins, 'sd', 16)
    p.y(ldwo)

    for t in range(NT):
        ps0, ps1 = ar['psf'][(t % 2) * 2], ar['psf'][(t % 2) * 2 + 1]
        xsl = q32[:, 1024 * (t % 4):1024 * (t % 4 + 1)]

        def ldxt(sync, t=t, xsl=xsl):
            p.wait(sync, 'st')
            p.wait(sync, 've')
            p.wait(sync, 'gd')
            ins = sync.dma_start(
                out=xsl, in_=x_in[128 * t:128 * (t + 1), :])
            p.inc(ins, 'sd', 16)
        p.y(ldxt)

        def mmo(tensor, t=t, ps0=ps0, ps1=ps1):
            p.wait(tensor, 'sd')
            p.wait(tensor, 've')
            if t >= 2:
                p.wait(tensor, 'vps', p.cnt['vps'] - 1)
            for half, ps in ((0, ps0), (1, ps1)):
                bias_mm(p, tensor, ps, bo, 512 * half, 512 * (half + 1),
                        start=True, stop=False)
                for k in range(8):
                    ins = tensor.matmul(
                        ps, ctx[:, 2048 * k + 128 * t:2048 * k + 128 * (t + 1)],
                        wsb[:, 1024 * k + 512 * half:1024 * k + 512 * (half + 1)],
                        start=False, stop=(k == 7))
            p.inc(ins, 'pe')
        p.t(mmo)

        def evo(vector, t=t, ps0=ps0, ps1=ps1, xsl=xsl):
            p.wait(vector, 'pe')
            vector.scalar_tensor_tensor(
                out=xsl[:, 0:512], in0=ps0, scalar=1.0, in1=xsl[:, 0:512],
                op0=ALU.mult, op1=ALU.add)
            ins = vector.scalar_tensor_tensor(
                out=xsl[:, 512:1024], in0=ps1, scalar=1.0, in1=xsl[:, 512:1024],
                op0=ALU.mult, op1=ALU.add)
            p.inc(ins, 'vps')
        p.v(evo)

        def sto(sync, t=t, xsl=xsl):
            p.wait(sync, 'vps')
            ins = sync.dma_start(out=x_out[128 * t:128 * (t + 1), :], in_=xsl)
            p.inc(ins, 'st', 16)
        p.y(sto)


def _vslot(xab, t, h):
    return xab[:, 1040 * h + 65 * t:1040 * h + 65 * t + 64]


def _vslot65(xab, t, h):
    return xab[:, 1040 * h + 65 * t:1040 * h + 65 * t + 65]


def _vslots(xab, t, half):
    base = 1040 * (8 * half) + 65 * t
    return xab[:, base:base + 8 * 1040].rearrange(
        "p (h r) -> p h r", h=8)[:, :, 0:64]





def emit_l0_attn_a(p, ar, x_in, x_out, w):
    xa = ar['XA']
    xbb = ar['XBb']
    xcb = ar['XCb']
    wsb = ar['WSb']
    stats = ar['stats']
    q32 = ar['q32']
    q32b = ar['q32'].bitcast(BF16)
    ssb = ar['ssb']
    u512 = ar['u512']
    g512 = ar['g512']

    emit_scale_load(p, ar, w['L0_attn_a_g1'], 1024, 0)
    emit_scale_load(p, ar, w['L0_attn_a_b1'], 1024, 8)
    emit_scale_load(p, ar, w['L0_attn_a_bqkv_a'][0:1024], 1024, 16)
    emit_scale_load(p, ar, w['L0_attn_a_bqkv_a'][1024:2048], 1024, 24)
    emit_scale_load(p, ar, w['L0_attn_a_gm'], 1024, 32)
    emit_scale_load(p, ar, w['L0_attn_a_bm'], 1024, 40)
    emit_scale_load(p, ar, w['L0_attn_a_b1m'], 4096, 48)
    emit_scale_load(p, ar, w['L0_attn_a_bqkv_s'][0:1024], 1024, 80)
    emit_scale_load(p, ar, w['L0_attn_a_bqkv_s'][1024:2048], 1024, 88)

    def scale_qbias(vector):
        p.wait(vector, 've')
        vector.tensor_scalar(ssb[:, 16:24], ssb[:, 16:24], 0.125, None,
                             ALU.mult)
        ins = vector.tensor_scalar(ssb[:, 80:88], ssb[:, 80:88], 0.125, None,
                                   ALU.mult)
        vector.drain()
        p.inc(ins, 've')
    p.v(scale_qbias)

    bva = hilo_bias(p, ar, [(w['L0_attn_a_bqkv_a'][2048:3072], 1024)], 0)
    bvs = hilo_bias(p, ar, [(w['L0_attn_a_bqkv_s'][2048:3072], 1024)], 1)

    emit_load_x(p, x_in, xa)
    emit_ln_stats(p, xa, ar['dump'], stats, 1e-5, double_eps=1e-5)
    emit_normalize(p, xa, stats, xcb[:, 0:16384])
    emit_tm_to_fm(p, xcb[:, 0:16384], xcb, ar['scr_tm'], wait_sem='se')

    def affine(vector):
        p.wait(vector, 'sd')
        for c in range(NCH):
            ins = vector.tensor_scalar(
                xcb[:, 2048 * c:2048 * (c + 1)], xcb[:, 2048 * c:2048 * (c + 1)],
                ssb[:, c:c + 1], ssb[:, 8 + c:9 + c], ALU.mult, ALU.add)
        p.inc(ins, 've')
    p.v(affine)
    i1 = xcb

    q2 = q32b[:, 0:2048]
    k2 = q32b[:, 2048:4096]
    v2 = q32b[:, 4096:6144]
    pstg = q32b[:, 6144:6400]
    ptst = q32b[:, 6400:6656]

    for hp in range(8):
        def ldwa(sync, hp=hp):
            p.wait(sync, 'pe')
            ins = sync.dma_start(
                out=wsb[:, 0:1024].rearrange("p (k o) -> p k o", k=8),
                in_=ar['cache']['L0_attn_a_Wqkv_a'].rearrange("(k p) o -> p k o", p=128)[
                    :, :, 128 * hp:128 * (hp + 1)])
            p.inc(ins, 'sd', 16)
            ins = sync.dma_start(
                out=wsb[:, 1024:2048].rearrange("p (k o) -> p k o", k=8),
                in_=ar['cache']['L0_attn_a_Wqkv_a'].rearrange("(k p) o -> p k o", p=128)[
                    :, :, 1024 + 128 * hp:1024 + 128 * (hp + 1)])
            p.inc(ins, 'sd', 16)
            ins = sync.dma_start(
                out=wsb[:, 2048:3072].rearrange("p (k o) -> p k o", k=8),
                in_=ar['cache']['L0_attn_a_Wqkv_a'].rearrange("(k p) o -> p k o", p=128)[
                    :, :, 2048 + 128 * hp:2048 + 128 * (hp + 1)])
            p.inc(ins, 'sd', 16)
        p.y(ldwa)

        for which, dst, bcol, scl in ((0, q2, 16, 0.125), (1, k2, 24, 1.0)):
            for g in range(4):
                ps = ar['psf'][g % 2]

                def mmqk(tensor, which=which, g=g, ps=ps, hp=hp):
                    p.wait(tensor, 'gd')
                    p.wait(tensor, 'sd')
                    p.wait(tensor, 'se')
                    p.wait(tensor, 've')
                    for k in range(8):
                        ins = tensor.matmul(
                            ps[:, 0:512],
                            wsb[:, 1024 * which + 128 * k:
                                1024 * which + 128 * (k + 1)],
                            i1[:, 2048 * k + 512 * g:2048 * k + 512 * (g + 1)],
                            start=(k == 0), stop=(k == 7))
                    p.inc(ins, 'pe')
                p.t(mmqk)

                def evqk(scalar, which=which, g=g, ps=ps, dst=dst, bcol=bcol,
                         scl=scl, hp=hp):
                    p.wait(scalar, 'pe')
                    ins = scalar.activation(
                        out=dst[:, 512 * g:512 * (g + 1)], in_=ps[:, 0:512],
                        func=AF.Identity, scale=scl,
                        bias=ssb[:, bcol + hp:bcol + hp + 1])
                    p.inc(ins, 'se')
                p.s(evqk)

        for t in range(NT):
            ps = ar['psf'][2 + (t % 2)]

            def mmva(tensor, t=t, ps=ps, hp=hp):
                p.wait(tensor, 'gd')
                p.wait(tensor, 'sd')
                p.wait(tensor, 've')
                bias_mm(p, tensor, ps[:, 0:128], bva, 128 * hp, 128 * (hp + 1),
                        start=True, stop=False)
                for k in range(8):
                    ins = tensor.matmul(
                        ps[:, 0:128],
                        i1[:, 2048 * k + 128 * t:2048 * k + 128 * (t + 1)],
                        wsb[:, 2048 + 128 * k:2048 + 128 * (k + 1)],
                        start=False, stop=(k == 7))
                p.inc(ins, 'pe')
            p.t(mmva)

            def evva(vector, t=t, ps=ps):
                p.wait(vector, 'pe')
                ins = vector.tensor_copy(v2[:, 128 * t:128 * (t + 1)],
                                         ps[:, 0:128])
                p.inc(ins, 've')
            p.v(evva)

        for h in (2 * hp, 2 * hp + 1):
            hb = 64 * (h % 2)
            for n in range(NT):
                psc = ar['psf'][0][:, 0:256]
                pav = ar['psb'][0][0:64, 0:512]
                ptp = ar['psf'][1][:, 0:512].bitcast(BF16)[:, 0:256]
                wid = 128 if n == 0 else 256
                klo = 128 * (n - 1) if n >= 1 else 0

                def mmsc(tensor, h=h, n=n, hb=hb, psc=psc, wid=wid, klo=klo):
                    p.wait(tensor, 'se')
                    p.wait(tensor, 've')
                    ins = tensor.matmul(
                        psc[:, 0:wid], q2[hb:hb + 64, 128 * n:128 * (n + 1)],
                        k2[hb:hb + 64, klo:klo + wid],
                        start=True, stop=True)
                    p.inc(ins, 'pe')
                p.t(mmsc)

                def smax(vector, n=n, psc=psc, wid=wid):
                    p.wait(vector, 'pe')
                    mask = ar['cswa0'] if n == 0 else ar['cswa1']
                    vector.tensor_tensor(psc[:, 0:wid], psc[:, 0:wid],
                                         mask[:, 0:wid], ALU.add)
                    vector.tensor_reduce(u512[:, 0:1], psc[:, 0:wid], AX.X,
                                         ALU.max)
                    vector.drain()
                    ins = vector.tensor_scalar(u512[:, 1:2], u512[:, 0:1],
                                               -1.0, None, ALU.mult)
                    vector.drain()
                    p.inc(ins, 've')
                p.v(smax)

                def sexp(scalar, psc=psc, wid=wid):
                    p.wait(scalar, 've')
                    ins = scalar.activation(out=pstg[:, 0:wid], in_=psc[:, 0:wid],
                                            func=AF.Exp, bias=u512[:, 1:2],
                                            accum_out=u512[:, 2:3])
                    p.inc(ins, 'se')
                p.s(sexp)

                def pnorm(vector, wid=wid):
                    p.wait(vector, 'se')
                    vector.reciprocal(u512[:, 3:4], u512[:, 2:3])
                    vector.drain()
                    ins = vector.tensor_scalar(pstg[:, 0:wid], pstg[:, 0:wid],
                                               u512[:, 3:4], None, ALU.mult)
                    vector.drain()
                    p.inc(ins, 've')
                p.v(pnorm)

                def ptr(tensor, wid=wid, ptp=ptp):
                    p.wait(tensor, 've')
                    for j in range(wid // 128):
                        ins = tensor.transpose(
                            ptp[:, 128 * j:128 * (j + 1)],
                            pstg[:, 128 * j:128 * (j + 1)], ar['ident'])
                    p.inc(ins, 'pe')
                p.t(ptr)

                def pte(vector, wid=wid, ptp=ptp):
                    p.wait(vector, 'pe')
                    ins = vector.tensor_copy(ptst[:, 0:wid], ptp[:, 0:wid])
                    p.inc(ins, 've')
                p.v(pte)

                def mmav(tensor, h=h, n=n, hb=hb, wid=wid, klo=klo, pav=pav):
                    p.wait(tensor, 've')
                    for j in range(wid // 128):
                        ins = tensor.matmul(
                            pav[:, 128 * (n % 4):128 * (n % 4) + 128],
                            v2[:, 128 * (klo // 128 + j) + hb:
                                128 * (klo // 128 + j) + hb + 64],
                            ptst[:, 128 * j:128 * (j + 1)],
                            start=(n % 4 == 0 and j == 0),
                            stop=(n % 4 == 3 and j == wid // 128 - 1),
                            skip_group_check=True)
                    p.inc(ins, 'pe')
                p.t(mmav)

                if n % 4 == 3:
                    def evav(vector, h=h, n=n, hb=hb, pav=pav):
                        p.wait(vector, 'pe')
                        ins = vector.tensor_copy(
                            xbb[hb:hb + 64, 2048 * (h // 2) + 128 * (n - 3):
                                2048 * (h // 2) + 128 * (n + 1)],
                            pav)
                        p.inc(ins, 've')
                    p.v(evav)

    if S0_PHASES <= 1:
        emit_store_x(p, xa, x_out, wait='ve')
        return
    q2l, k2l = q2, k2
    lstg = q32b[:, 6176:6304]
    lAT = q32b[:, 6304:6432]
    v2l = q32b[:, 4096:6176]
    ssl = [ar['w23'][0:64, 0:65], ar['w23'][64:128, 0:65]]

    for hp in range(8):
        def ldws(sync, hp=hp):
            p.wait(sync, 'pe')
            ins = sync.dma_start(
                out=wsb[:, 0:1024].rearrange("p (k o) -> p k o", k=8),
                in_=ar['cache']['L0_attn_a_Wqkv_s'].rearrange("(k p) o -> p k o", p=128)[
                    :, :, 128 * hp:128 * (hp + 1)])
            p.inc(ins, 'sd', 16)
            ins = sync.dma_start(
                out=wsb[:, 1024:2048].rearrange("p (k o) -> p k o", k=8),
                in_=ar['cache']['L0_attn_a_Wqkv_s'].rearrange("(k p) o -> p k o", p=128)[
                    :, :, 1024 + 128 * hp:1024 + 128 * (hp + 1)])
            p.inc(ins, 'sd', 16)
            ins = sync.dma_start(
                out=wsb[:, 2048:3072].rearrange("p (k o) -> p k o", k=8),
                in_=ar['cache']['L0_attn_a_Wqkv_s'].rearrange("(k p) o -> p k o", p=128)[
                    :, :, 2048 + 128 * hp:2048 + 128 * (hp + 1)])
            p.inc(ins, 'sd', 16)
        p.y(ldws)

        def onesl(gpsimd, hp=hp):
            p.wait(gpsimd, 've')
            p.wait(gpsimd, 'pe')
            ins = gpsimd.memset(v2l, 1.0)
            p.inc(ins, 'gp')
        p.g(onesl)

        for which, dst, bcol, scl in ((0, q2l, 80, 0.125), (1, k2l, 88, 1.0)):
            for g in range(4):
                ps = ar['psf'][g % 2]

                def mmqs(tensor, which=which, g=g, ps=ps):
                    p.wait(tensor, 'gd')
                    p.wait(tensor, 'sd')
                    p.wait(tensor, 've')
                    p.wait(tensor, 'se')
                    for k in range(8):
                        ins = tensor.matmul(
                            ps[:, 0:512],
                            wsb[:, 1024 * which + 128 * k:
                                1024 * which + 128 * (k + 1)],
                            i1[:, 2048 * k + 512 * g:2048 * k + 512 * (g + 1)],
                            start=(k == 0), stop=(k == 7))
                    p.inc(ins, 'pe')
                p.t(mmqs)

                def evz(scalar, which=which, g=g, ps=ps, bcol=bcol, scl=scl,
                        hp=hp):
                    p.wait(scalar, 'pe')
                    scalar.activation(out=u512, in_=ps[:, 0:512],
                                      func=AF.Identity, scale=scl,
                                      bias=ssb[:, bcol + hp:bcol + hp + 1])
                    ins = scalar.activation(out=g512[:, 0:512], in_=u512,
                                            func=AF.Exp)
                    p.inc(ins, 'se')
                p.s(evz)

                def evf(vector, which=which, g=g, dst=dst):
                    p.wait(vector, 'se')
                    vector.tensor_scalar(g512[:, 0:512], g512[:, 0:512],
                                         1.0, 0.0001, ALU.min, ALU.add)
                    vector.tensor_scalar(u512, u512, 0.0, None, ALU.max)
                    ins = vector.tensor_tensor(dst[:, 512 * g:512 * (g + 1)],
                                               u512, g512[:, 0:512], ALU.add)
                    p.inc(ins, 've')
                p.v(evf)

        for t in range(NT):
            ps = ar['psf'][2 + (t % 2)]

            def mmvs(tensor, t=t, ps=ps, hp=hp):
                p.wait(tensor, 'gd')
                p.wait(tensor, 'sd')
                p.wait(tensor, 've')
                p.wait(tensor, 'gp')
                bias_mm(p, tensor, ps[:, 0:128], bvs, 128 * hp, 128 * (hp + 1),
                        start=True, stop=False)
                for k in range(8):
                    ins = tensor.matmul(
                        ps[:, 0:128],
                        i1[:, 2048 * k + 128 * t:2048 * k + 128 * (t + 1)],
                        wsb[:, 2048 + 128 * k:2048 + 128 * (k + 1)],
                        start=False, stop=(k == 7))
                p.inc(ins, 'pe')
            p.t(mmvs)

            def evvs(vector, t=t, ps=ps):
                p.wait(vector, 'pe')
                ins = vector.tensor_copy(
                    _lslots(v2l, t),
                    ps[:, 0:128].rearrange("p (a e) -> p a e", a=2))
                p.inc(ins, 'vps')
            p.v(evvs)

        for h in (2 * hp, 2 * hp + 1):
            hb = 64 * (h % 2)
            sext = ssl[h % 2]
            psS = ar['psb'][2][0:64, 0:65] if h % 2 == 0 else \
                ar['psb'][3][0:64, 0:65]

            def sinit(vector, sext=sext):
                p.wait(vector, 'vps')
                ins = vector.memset(sext, 0.0)
                vector.drain()
                p.inc(ins, 've')
            p.v(sinit)

            for n in range(NT):
                psA = ar['psf'][0][:, 0:128]
                pAT = ar['psf'][1][:, 0:512].bitcast(BF16)[:, 0:128]
                pKT = ar['psf'][2][:, 0:512].bitcast(BF16)[:, 0:64]
                pnum = ar['psf'][3][:, 0:65]
                pot = ar['psb'][1][:, 0:512].bitcast(BF16)[:, 0:128]

                def mmA(tensor, h=h, n=n, hb=hb, psA=psA):
                    p.wait(tensor, 've')
                    p.wait(tensor, 'se')
                    ins = tensor.matmul(
                        psA, q2l[hb:hb + 64, 128 * n:128 * (n + 1)],
                        k2l[hb:hb + 64, 128 * n:128 * (n + 1)],
                        start=True, stop=True)
                    p.inc(ins, 'pe')
                p.t(mmA)

                def evA(vector, psA=psA):
                    p.wait(vector, 'pe')
                    ins = vector.tensor_tensor(lstg[:, 0:128], psA,
                                               ar['ctril'], ALU.mult)
                    p.inc(ins, 've')
                p.v(evA)

                def trAK(tensor, n=n, hb=hb, pAT=pAT, pKT=pKT):
                    p.wait(tensor, 've')
                    tensor.transpose(pAT, lstg[:, 0:128], ar['ident'])
                    ins = tensor.transpose(
                        pKT, k2l[hb:hb + 64, 128 * n:128 * (n + 1)],
                        ar['ident'][hb:hb + 64, hb:hb + 64])
                    p.inc(ins, 'pe')
                p.t(trAK)

                def evAT(vector, pAT=pAT, pKT=pKT):
                    p.wait(vector, 'pe')
                    vector.tensor_copy(lAT[:, 0:128], pAT)
                    ins = vector.tensor_copy(q32b[:, 6432:6496], pKT)
                    p.inc(ins, 've')
                p.v(evAT)

                def mmnum(tensor, h=h, n=n, hb=hb, pnum=pnum, sext=sext):
                    p.wait(tensor, 've')
                    tensor.matmul(pnum, lAT[:, 0:128], _lslot65(v2l, h, n),
                                  start=True, stop=False,
                                  skip_group_check=True)
                    ins = tensor.matmul(
                        pnum, q2l[hb:hb + 64, 128 * n:128 * (n + 1)], sext,
                        start=False, stop=True, skip_group_check=True)
                    p.inc(ins, 'pe')
                p.t(mmnum)

                def mmS(tensor, h=h, n=n, psS=psS):
                    ins = tensor.matmul(
                        psS, q32b[:, 6432:6496], _lslot65(v2l, h, n),
                        start=(n == 0), stop=(n == NT - 1),
                        skip_group_check=True)
                    p.inc(ins, 'pe')
                p.t(mmS)

                def evnum(vector, n=n, pnum=pnum, psS=psS, sext=sext):
                    p.wait(vector, 'pe')
                    vector.tensor_scalar(u512[:, 0:1], pnum[:, 64:65], 1e-6,
                                         None, ALU.add)
                    vector.drain()
                    vector.reciprocal(u512[:, 1:2], u512[:, 0:1])
                    vector.drain()
                    vector.tensor_scalar(lstg[:, 0:64], pnum[:, 0:64],
                                         u512[:, 1:2], None, ALU.mult)
                    ins = vector.tensor_copy(sext, psS)
                    vector.drain()
                    p.inc(ins, 've')
                p.v(evnum)

                def trO(tensor, pot=pot):
                    p.wait(tensor, 've')
                    ins = tensor.transpose(pot[0:64, :], lstg[:, 0:64],
                                           ar['ident'])
                    p.inc(ins, 'pe')
                p.t(trO)

                def evO(vector, h=h, n=n, hb=hb, pot=pot):
                    p.wait(vector, 'pe')
                    ins = vector.tensor_copy(
                        xbb[hb:hb + 64, 16384 + 2048 * (h // 2) + 128 * n:
                            16384 + 2048 * (h // 2) + 128 * (n + 1)],
                        pot[0:64, :])
                    p.inc(ins, 've')
                p.v(evO)

    if S0_PHASES <= 2:
        emit_store_x(p, xa, x_out, wait='ve')
        return
    swa_o = xbb[:, 0:16384]
    lin_o = xbb[:, 16384:32768]
    emit_scale_load(p, ar, w['L0_attn_a_bp_a'], 1024, 96)
    emit_scale_load(p, ar, w['L0_attn_a_bo_s'], 1024, 104)

    def combb(vector):
        p.wait(vector, 've')
        ins = vector.tensor_tensor(ssb[:, 96:104], ssb[:, 96:104],
                                   ssb[:, 104:112], ALU.add)
        vector.drain()
        p.inc(ins, 've')
    p.v(combb)

    for og in range(2):
        def ldmx(sync, og=og):
            p.wait(sync, 'pe')
            ins = sync.dma_start(
                out=wsb[:, 0:4096].rearrange("p (k o) -> p k o", k=8),
                in_=ar['cache']['L0_attn_a_Wp_a'].rearrange("(k p) o -> p k o", p=128)[
                    :, :, 512 * og:512 * (og + 1)])
            p.inc(ins, 'sd', 16)
            ins = sync.dma_start(
                out=wsb[:, 4096:8192].rearrange("p (k o) -> p k o", k=8),
                in_=ar['cache']['L0_attn_a_Wo_s'].rearrange("(k p) o -> p k o", p=128)[
                    :, :, 512 * og:512 * (og + 1)])
            p.inc(ins, 'sd', 16)
        p.y(ldmx)

        for sub in range(16):
            oc, g = divmod(sub, 4)
            oabs = 4 * og + oc
            ps = ar['psf'][sub % 2]

            def mmmx(tensor, oc=oc, g=g, ps=ps, sub=sub, og=og, oabs=oabs):
                p.wait(tensor, 'gd')
                p.wait(tensor, 'sd')
                p.wait(tensor, 've')
                if not (og == 0 and sub < 2):
                    p.wait(tensor, 'se', p.cnt['se'] - 1)
                for k in range(8):
                    tensor.matmul(
                        ps[:, 0:512],
                        wsb[:, 0:4096].rearrange("p (k o) -> p k o", k=8)[
                            :, k, 128 * oc:128 * (oc + 1)],
                        swa_o[:, 2048 * k + 512 * g:2048 * k + 512 * (g + 1)],
                        start=(k == 0), stop=False)
                for k in range(8):
                    ins = tensor.matmul(
                        ps[:, 0:512],
                        wsb[:, 4096:8192].rearrange("p (k o) -> p k o", k=8)[
                            :, k, 128 * oc:128 * (oc + 1)],
                        lin_o[:, 2048 * k + 512 * g:2048 * k + 512 * (g + 1)],
                        start=False, stop=(k == 7))
                p.inc(ins, 'pe')
            p.t(mmmx)

            def evmx(scalar, oabs=oabs, g=g, ps=ps):
                p.wait(scalar, 'pe')
                ins = scalar.activation(
                    out=xcb[:, 2048 * oabs + 512 * g:2048 * oabs + 512 * (g + 1)],
                    in_=ps[:, 0:512], func=AF.Identity,
                    scale=1.0, bias=ssb[:, 96 + oabs:97 + oabs])
                p.inc(ins, 'se')
            p.s(evmx)

    bo = hilo_bias(p, ar, [(w['L0_attn_a_bo'], 1024)], 0)

    def ldwo2(sync):
        p.wait(sync, 'pe')
        ins = sync.dma_start(
            out=wsb[:, 0:8192].rearrange("p (k o) -> p k o", k=8),
            in_=ar['cache']['L0_attn_a_Wo'].rearrange("(k p) o -> p k o", p=128))
        p.inc(ins, 'sd', 16)
    p.y(ldwo2)

    for t in range(NT):
        ps0, ps1 = ar['psf'][(t % 2) * 2], ar['psf'][(t % 2) * 2 + 1]

        def mmwo(tensor, t=t, ps0=ps0, ps1=ps1):
            p.wait(tensor, 'gd')
            p.wait(tensor, 'sd')
            p.wait(tensor, 'se')
            p.wait(tensor, 've')
            if t >= 2:
                p.wait(tensor, 'vps', p.cnt['vps'] - 1)
            for half, ps in ((0, ps0), (1, ps1)):
                bias_mm(p, tensor, ps, bo, 512 * half, 512 * (half + 1),
                        start=True, stop=False)
                for k in range(8):
                    ins = tensor.matmul(
                        ps, xcb[:, 2048 * k + 128 * t:2048 * k + 128 * (t + 1)],
                        wsb[:, 1024 * k + 512 * half:1024 * k + 512 * (half + 1)],
                        start=False, stop=(k == 7))
            p.inc(ins, 'pe')
        p.t(mmwo)

        def evwo(vector, t=t, ps0=ps0, ps1=ps1):
            p.wait(vector, 'pe')
            for half, ps in ((0, ps0), (1, ps1)):
                vector.tensor_scalar(
                    xa[:, 1024 * t + 512 * half:1024 * t + 512 * (half + 1)],
                    xa[:, 1024 * t + 512 * half:1024 * t + 512 * (half + 1)],
                    stats[:, 64 + t:65 + t], stats[:, 80 + t:81 + t],
                    ALU.mult, ALU.add)
                ins = vector.tensor_tensor(
                    xa[:, 1024 * t + 512 * half:1024 * t + 512 * (half + 1)],
                    xa[:, 1024 * t + 512 * half:1024 * t + 512 * (half + 1)],
                    ps, ALU.add)
            p.inc(ins, 'vps')
        p.v(evwo)

    if S0_PHASES <= 3:
        def gate3(vector):
            p.wait(vector, 'vps')
            ins = vector.tensor_copy(u512[:, 0:1], xa[:, 0:1])
            p.inc(ins, 'vps')
        p.v(gate3)
        emit_store_x(p, xa, x_out, wait='vps')
        return
    def gate_mlp(scalar):
        p.wait(scalar, 'vps')
    p.s(gate_mlp)
    emit_ln_stats(p, xa, ar['dump'], stats, 1e-5)
    emit_normalize(p, xa, stats, xcb[:, 0:16384])
    emit_tm_to_fm(p, xcb[:, 0:16384], xcb, ar['scr_tm'], wait_sem='se')

    def affinem(vector):
        p.wait(vector, 'sd')
        for c in range(NCH):
            ins = vector.tensor_scalar(
                xcb[:, 2048 * c:2048 * (c + 1)], xcb[:, 2048 * c:2048 * (c + 1)],
                ssb[:, 32 + c:33 + c], ssb[:, 40 + c:41 + c], ALU.mult, ALU.add)
        p.inc(ins, 've')
    p.v(affinem)

    b2m = hilo_bias(p, ar, [(w['L0_attn_a_b2m'], 1024)], 1)

    for tb in range(2):
        t0 = 1024 * tb
        for og in range(8):
            o0 = 512 * og

            def ldw1(sync, o0=o0):
                p.wait(sync, 'pe')
                p.wait(sync, 've')
                p.wait(sync, 'se')
                p.wait(sync, 'vps')
                ins = sync.dma_start(
                    out=wsb[:, 0:4096].rearrange("p (k o) -> p k o", k=8),
                    in_=ar['cache']['L0_attn_a_W1'].rearrange(
                        "(k p) o -> p k o", p=128)[:, :, o0:o0 + 512])
                p.inc(ins, 'sd', 16)
            p.y(ldw1)

            for sub in range(8):
                oc, g = divmod(sub, 2)
                oabs = 4 * og + oc
                ps = ar['psf'][sub % 4]

                def mm1(tensor, oc=oc, g=g, ps=ps, t0=t0):
                    p.wait(tensor, 'gd')
                    p.wait(tensor, 'sd')
                    p.wait(tensor, 'se')
                    for k in range(8):
                        ins = tensor.matmul(
                            ps,
                            wsb[:, 512 * k + 128 * oc:512 * k + 128 * (oc + 1)],
                            xcb[:, 2048 * k + t0 + 512 * g:
                                2048 * k + t0 + 512 * (g + 1)],
                            start=(k == 0), stop=(k == 7))
                    p.inc(ins, 'pe')
                p.t(mm1)

                def ev1(scalar, ps=ps, oabs=oabs, g=g):
                    p.wait(scalar, 'pe')
                    ins = scalar.activation(
                        out=xbb[:, 1024 * oabs + 512 * g:
                                1024 * oabs + 512 * (g + 1)],
                        in_=ps, func=AF.Gelu,
                        scale=1.0, bias=ssb[:, 48 + oabs:49 + oabs])
                    p.inc(ins, 'se')
                p.s(ev1)

        for th in range(2):
            tt0 = t0 + 512 * th

            def ldx2(sync, tt0=tt0):
                p.wait(sync, 'vps')
                p.wait(sync, 've')
                p.wait(sync, 'st')
                p.wait(sync, 'gd')
                ins = sync.dma_start(
                    out=q32.rearrange("p (t f) -> p t f", t=4),
                    in_=x_in[tt0:tt0 + 512, :].rearrange(
                        "(t p) f -> p t f", p=128))
                p.inc(ins, 'sd', 16)
            p.y(ldx2)

            for kg in range(4):
                def ldw2(sync, kg=kg):
                    p.wait(sync, 'pe')
                    p.wait(sync, 'sd')
                    if kg == 0:
                        p.wait(sync, 'se')
                    ins = sync.dma_start(
                        out=wsb[:, 0:8192].rearrange("p (k o) -> p k o", k=8),
                        in_=ar['cache']['L0_attn_a_W2'].rearrange(
                            "(k p) o -> p k o", p=128)[:, 8 * kg:8 * kg + 8, :])
                    p.inc(ins, 'sd', 16)
                p.y(ldw2)

                def mm2(tensor, kg=kg, tt0=tt0, t0=t0):
                    p.wait(tensor, 'gd')
                    p.wait(tensor, 'sd')
                    if kg == 0:
                        p.wait(tensor, 'vps')
                        p.wait(tensor, 'se')
                    for tt in range(4):
                        for half in range(2):
                            ps = ar['psf'][tt] if half == 0 else ar['psb'][tt]
                            for k8 in range(8):
                                k = 8 * kg + k8
                                ins = tensor.matmul(
                                    ps,
                                    xbb[:, 1024 * k + (tt0 - t0) + 128 * tt:
                                        1024 * k + (tt0 - t0) + 128 * (tt + 1)],
                                    wsb[:, 1024 * k8 + 512 * half:
                                        1024 * k8 + 512 * (half + 1)],
                                    start=(kg == 0 and k8 == 0), stop=False)
                    p.inc(ins, 'pe')
                p.t(mm2)

            def mmb2(tensor, tt0=tt0):
                for tt in range(4):
                    for half in range(2):
                        ps = ar['psf'][tt] if half == 0 else ar['psb'][tt]
                        ins = tensor.matmul(
                            ps, p.ones2[:, 0:128],
                            b2m[:, 512 * half:512 * (half + 1)],
                            start=False, stop=True)
                p.inc(ins, 'pe')
            p.t(mmb2)

            def ev2(vector, tt0=tt0):
                p.wait(vector, 'pe')
                xbase = (tt0 // 128) * 1024
                for tt in range(4):
                    for half in range(2):
                        ps = ar['psf'][tt] if half == 0 else ar['psb'][tt]
                        col = 1024 * tt + 512 * half
                        vector.scalar_tensor_tensor(
                            out=q32[:, col:col + 512], in0=ps, scalar=1.0,
                            in1=q32[:, col:col + 512], op0=ALU.mult,
                            op1=ALU.add)
                        ins = vector.tensor_tensor(
                            q32[:, col:col + 512], q32[:, col:col + 512],
                            xa[:, xbase + col:xbase + col + 512], ALU.add)
                p.inc(ins, 'vps')
            p.v(ev2)

            def st2(sync, tt0=tt0):
                p.wait(sync, 'vps')
                ins = sync.dma_start(
                    out=x_out[tt0:tt0 + 512, :].rearrange(
                        "(t p) f -> p t f", p=128),
                    in_=q32.rearrange("p (t f) -> p t f", t=4))
                p.inc(ins, 'st', 16)
            p.y(st2)


def _lslots(v2l, t):
    return v2l.rearrange("p (a r) -> p a r", a=2)[:, :, 65 * t:65 * t + 64]


def _lslot65(v2l, h, n):
    return v2l[:, 1040 * (h % 2) + 65 * n:1040 * (h % 2) + 65 * n + 65]





CACHE_SPECS = [
    ('L0_attn_a_Wqkv_a', 1024, 3072),
    ('L0_attn_a_Wqkv_s', 1024, 3072),
    ('L0_attn_a_Wp_a', 1024, 1024),
    ('L0_attn_a_Wo_s', 1024, 1024),
    ('L0_attn_a_Wo', 1024, 1024),
    ('L0_attn_a_W1', 1024, 4096),
    ('L0_attn_a_W2', 4096, 1024),
    ('L0_attn_b_wq', 1024, 1024),
    ('L0_attn_b_wk', 1024, 1024),
    ('L0_attn_b_wv', 1024, 1024),
    ('L0_attn_b_wo', 1024, 1024),
    ('L0_attn_b_w1', 1024, 4096),
    ('L0_attn_b_w2', 4096, 1024),
    ('L1_attn_b_W1', 1024, 1024),
]


def emit_cache_build(p, ar, w, needed):
    wsb = ar['WSb']
    for name, K, O in CACHE_SPECS:
        if name not in needed:
            continue
        src = w[name]
        if src.ndim == 3:
            src = src.rearrange("a b c -> (a b) c") if src.shape[0] * \
                src.shape[1] == K else src.rearrange("a b c -> a (b c)")
        cache = ar['cache'][name]
        cols = min(O, 4096)
        rows_per = 128
        for r0 in range(0, K, rows_per):
            for c0 in range(0, O, cols):
                half = ((r0 // rows_per + c0 // cols) % 2) * 4096
                buf = wsb[:, half:half + cols]

                def cast(gpsimd, buf=buf, src=src, r0=r0, c0=c0, cols=cols):
                    p.wait(gpsimd, 'sd')
                    ins = gpsimd.dma_start(
                        out=buf, in_=src[r0:r0 + 128, c0:c0 + cols])
                    p.inc(ins, 'gd', 16)
                p.g(cast)

                def wr(sync, buf=buf, cache=cache, r0=r0, c0=c0, cols=cols):
                    p.wait(sync, 'gd')
                    ins = sync.dma_start(
                        out=cache[r0:r0 + 128, c0:c0 + cols], in_=buf)
                    p.inc(ins, 'sd', 16)
                p.y(wr)



def build_arenas(nc, es):
    ar = {}
    sb = lambda n, s, d=F32: es.enter_context(nc.sbuf_tensor(n, s, d))
    XA = sb("XA", [128, 16384])
    XB = sb("XB", [128, 16384])
    XC = sb("XC", [128, 8192])
    WS = sb("WS", [128, 4096])
    ar['XA'] = XA[:]
    ar['XAb'] = XA[:].bitcast(BF16)
    ar['XB'] = XB[:]
    ar['XBb'] = XB[:].bitcast(BF16)
    ar['XCb'] = XC[:].bitcast(BF16)
    ar['WSb'] = WS[:].bitcast(BF16)
    ar['stats'] = sb("stats", [128, 96])[:]
    ar['q16'] = sb("q16", [128, 96])[:]
    ar['q32'] = sb("q32", [128, 4096])[:]
    ar['dump'] = ar['q32'][:, 3072:4096]
    ar['brow'] = ar['q32'][0:1, 0:2048]
    ar['s32'] = ar['q32'][0:32, 2048:2176]
    bias2 = sb("bias2", [2, 2176], BF16)
    ar['bias2'] = [bias2[:, 0:1024], bias2[:, 1024:2048], bias2[:, 2048:2176]]
    ar['ident'] = sb("ident", [128, 128], BF16)[:]
    ar['ones2'] = sb("ones2", [2, 128], BF16)[:]
    ar['ones128'] = sb("ones128", [128, 2], BF16)[:]
    ar['w23'] = sb("w23", [128, 256], BF16)
    ar['u512'] = sb("u512", [128, 512])[:]
    ar['g512'] = sb("g512", [128, 1024])[:]
    ar['ssb'] = sb("ssb", [128, 136])[:]

    ar['ident32'] = sb("ident32", [128, 128])[:]
    ar['ctri'] = sb("ctri", [128, 128])[:]
    ar['cswa0'] = sb("cswa0", [128, 128])[:]
    ar['cswa1'] = sb("cswa1", [128, 256])[:]
    ar['ctril'] = sb("ctril", [128, 128])[:]
    ar['psf'] = [es.enter_context(nc.psum_tensor(f"psf{i}", [128, 512], F32))[:]
                 for i in range(4)]
    ar['psb'] = [es.enter_context(nc.psum_tensor(f"psb{i}", [128, 512], F32))[:]
                 for i in range(4)]
    ar['scr_tm'] = nc.dram_tensor("scr_tm", [L, D], BF16, kind="Internal").ap()
    if DEBUG_ATTNB:
        ar['dbg_out'] = nc.dram_tensor("dbg", [128, 1056], F32,
                                       kind="ExternalOutput").ap()
        ar['dbg2_out'] = nc.dram_tensor("dbg2", [128, 1152], F32,
                                        kind="ExternalOutput").ap()
        ar['dbg3_out'] = nc.dram_tensor("dbg3", [128, 2048], BF16,
                                        kind="ExternalOutput").ap()
    ar['scr_w'] = nc.dram_tensor("scr_w", [1024, 1024], BF16, kind="Internal").ap()
    ar['pst'] = ar['psb'][0][:, 0:128]
    ar['c_raw'] = nc.dram_tensor("c_raw", [4096, 4096], BF16, kind="Internal").ap()
    ar['cT_w1'] = nc.dram_tensor("cT_w1", [1024, 4096], BF16, kind="Internal").ap()
    ar['cT_w2'] = nc.dram_tensor("cT_w2", [1024, 4096], BF16, kind="Internal").ap()
    ar['cT_h1'] = nc.dram_tensor("cT_h1", [4096, 4096], BF16, kind="Internal").ap()
    ar['cT_h2'] = nc.dram_tensor("cT_h2", [4096, 4096], BF16, kind="Internal").ap()
    ar['cT_w3'] = nc.dram_tensor("cT_w3", [4096, 1024], BF16, kind="Internal").ap()
    return ar


SUBLAYERS = ['l0_attn_a', 'l0_ffn', 'l0_attn_b', 'l0_ffn',
             'l1_attn_a', 'l1_ffn', 'l1_attn_b', 'l1_ffn']


def build_program(lo, hi, weights_meta):
    _, wlog = _build(lo, hi, weights_meta, 'record')
    nc, _ = _build(lo, hi, weights_meta, 'replay', wlog)
    return nc


def _build(lo, hi, weights_meta, mode, wlog=None):
    nc = bass.Bass("TRN2", target_bir_lowering=False)
    x_in = nc.dram_tensor("x", [L, D], F32, kind="ExternalInput").ap()
    x_out = nc.dram_tensor("y", [L, D], F32, kind="ExternalOutput").ap()
    consts = {'c_ident': nc.dram_tensor("c_ident", [128, 128], BF16,
                                        kind="ExternalInput").ap(),
              'c_tri': nc.dram_tensor("c_tri", [128, 128], F32,
                                      kind="ExternalInput").ap(),
              'c_swa0': nc.dram_tensor("c_swa0", [128, 128], F32,
                                       kind="ExternalInput").ap(),
              'c_swa1': nc.dram_tensor("c_swa1", [128, 256], F32,
                                       kind="ExternalInput").ap(),
              'c_tril': nc.dram_tensor("c_tril", [128, 128], F32,
                                       kind="ExternalInput").ap()}
    w = {name: nc.dram_tensor(name, list(shape), F32, kind="ExternalInput").ap()
         for name, shape in weights_meta}
    xbufs = [nc.dram_tensor(f"xbuf{i}", [L, D], F32, kind="Internal").ap()
             for i in range(2)]

    es = ExitStack()
    with es:
        ar = build_arenas(nc, es)
        ar['cache'] = {}
        needed = set()
        for nm, K, O in CACHE_SPECS:
            pref = nm[:nm.index('_', 3)]
            ar['cache'][nm] = nc.dram_tensor(
                "cch_" + nm, [K, O], BF16, kind="Internal").ap()
            needed.add(nm)
        with nc.Block() as block:
            p = P(nc, block, es, mode=mode, wlog=wlog)
            p.ones2 = ar['ones2']

            def init(gpsimd):
                gpsimd.memset(ar['ones2'], 1.0)
                ins = gpsimd.memset(ar['ones128'], 1.0)
                p.inc(ins, 'gp')
            p.g(init)
            p.ones128 = ar['ones128'][:, 0:1]

            def initc(sync):
                ins = sync.dma_start(out=ar['ident'], in_=consts['c_ident'])
                p.inc(ins, 'sd', 16)
                ins = sync.dma_start(out=ar['ctri'], in_=consts['c_tri'])
                p.inc(ins, 'sd', 16)
                ins = sync.dma_start(out=ar['cswa0'], in_=consts['c_swa0'])
                p.inc(ins, 'sd', 16)
                ins = sync.dma_start(out=ar['cswa1'], in_=consts['c_swa1'])
                p.inc(ins, 'sd', 16)
                ins = sync.dma_start(out=ar['ctril'], in_=consts['c_tril'])
                p.inc(ins, 'sd', 16)
            p.y(initc)

            def initc2(vector):
                p.wait(vector, 'sd')
                ins = vector.tensor_copy(ar['ident32'], ar['ident'])
                p.inc(ins, 've')
            p.v(initc2)

            emit_cache_build(p, ar, w, needed)
            first_l0ffn = True
            first_l1ffn = True
            cur = x_in
            for i in range(lo, hi):
                name = SUBLAYERS[i]
                nxt = x_out if i == hi - 1 else xbufs[i % 2]
                if name == 'l1_ffn':
                    emit_l1_ffn(p, ar, cur, nxt, w, first=first_l1ffn)
                    first_l1ffn = False
                elif name == 'l1_attn_b':
                    emit_l1_attn_b(p, ar, cur, nxt, w)
                elif name == 'l0_ffn':
                    emit_l0_ffn(p, ar, cur, nxt, w, first=first_l0ffn)
                    first_l0ffn = False
                elif name == 'l0_attn_b':
                    emit_l0_attn_b(p, ar, cur, nxt, w)
                elif name == 'l1_attn_a':
                    emit_l1_attn_a(p, ar, cur, nxt, w)
                elif name == 'l0_attn_a':
                    emit_l0_attn_a(p, ar, cur, nxt, w)
                else:
                    raise NotImplementedError(name)
                cur = nxt
                if i != hi - 1:
                    p.barrier()

            def final(sync):
                p.wait(sync, 'st')
            p.y(final)
            if mode == 'replay':
                p.flush()
    return nc, p.wlog


def make_per_core(inputs):
    x = np.ascontiguousarray(inputs['L0_attn_a_x'])
    r_ = np.arange(128)[:, None]
    c_ = np.arange(128)[None, :]
    c2_ = np.arange(256)[None, :]
    consts = {
        'c_ident': np.eye(128, dtype=ml_dtypes.bfloat16),
        'c_tri': np.where(r_ > c_, np.float32(-10000.0), np.float32(0.0)),
        'c_swa0': np.where(c_ <= r_, np.float32(0.0), np.float32(-1e30)),
        'c_swa1': np.where((c2_ >= r_ + 1) & (c2_ <= r_ + 128),
                           np.float32(0.0), np.float32(-1e30)),
        'c_tril': np.where(r_ >= c_, np.float32(1.0), np.float32(0.0)),
    }
    per_core = []
    for b in range(8):
        m = {'x': x[b]}
        m.update(consts)
        for k, v in inputs.items():
            if k != 'L0_attn_a_x':
                m[k] = np.ascontiguousarray(v)
        per_core.append(m)
    return per_core


def kernel(**inputs):
    weights_meta = [(k, v.shape) for k, v in inputs.items() if k != 'L0_attn_a_x']
    per_core = make_per_core(inputs)
    nc = build_program(0, 8, weights_meta)
    res = run_bass_kernel_spmd(nc, per_core, core_ids=list(range(8)))
    return np.stack([r['y'] for r in res.results], axis=0)
